# version 11
# baseline (speedup 1.0000x reference)
import contextlib
import numpy as np
import ml_dtypes
import concourse.bass as bass
import concourse.mybir as mybir
from concourse.bass_utils import run_bass_kernel_spmd
from concourse.aot_env import allow_ici_all_to_all

F32 = mybir.dt.float32
BF16 = mybir.dt.bfloat16
AF = mybir.ActivationFunctionType
ALU = mybir.AluOpType
AX = mybir.AxisListType

B, T, D, DH = 2, 4096, 4096, 128
NTS = 8
TS = 512
NDC = D // 128
HPC = 4
NSLAB = B * NTS

_DEBUG = ()


class _Sem:
    __slots__ = ("h", "v", "name")

    def __init__(self, nc, name):
        self.h = nc.alloc_semaphore(name)
        self.v = 0
        self.name = name


class _Waits:

    def __init__(self):
        self.seen = {}

    def w(self, ename, eng, sem, tok):
        if tok <= 0:
            return
        k = (ename, sem.name)
        if self.seen.get(k, 0) >= tok:
            return
        eng.wait_ge(sem.h, tok)
        self.seen[k] = tok


def build_program(debug=()):
    nc = bass.Bass("TRN2", target_bir_lowering=False, debug=False)

    xt = nc.dram_tensor("xt", [NSLAB * 4 * 128, 8 * TS], F32, kind="ExternalInput").ap()
    wq = nc.dram_tensor("wq", [HPC * 128, NDC * 128], F32, kind="ExternalInput").ap()
    wk = nc.dram_tensor("wk", [128, NDC * 128], F32, kind="ExternalInput").ap()
    wv = nc.dram_tensor("wv", [128, NDC * 128], F32, kind="ExternalInput").ap()
    wo = nc.dram_tensor("wo", [NDC * 128, NDC * 128], F32, kind="ExternalInput").ap()
    wf = nc.dram_tensor("wf", [NDC * 128, NDC * 128], F32, kind="ExternalInput").ap()
    xres = nc.dram_tensor("xres", [2 * 128, NDC * TS], F32, kind="ExternalInput").ap()
    tabs = nc.dram_tensor("tabs", [4 * 128, T], BF16, kind="ExternalInput").ap()
    masks = nc.dram_tensor("masks", [128, 4 * TS], BF16, kind="ExternalInput").ap()
    rmat = nc.dram_tensor("rmat", [128, 128], BF16, kind="ExternalInput").ap()
    out = nc.dram_tensor("out", [2 * NDC * 128, TS], F32, kind="ExternalOutput").ap()
    a2a_src = nc.dram_tensor("a2a_src", [8 * 512, 1024], BF16)
    a2a_dst = nc.dram_tensor("a2a_dst", [8 * 512, 1024], BF16)

    dbg = {}
    if "kt" in debug:
        dbg["kt"] = nc.dram_tensor("dbg_kt", [128, B * T], BF16, kind="ExternalOutput").ap()
    if "v" in debug:
        dbg["v"] = nc.dram_tensor("dbg_v", [128, B * T], BF16, kind="ExternalOutput").ap()
    if "q" in debug:
        dbg["q"] = nc.dram_tensor("dbg_q", [128, HPC * TS], BF16, kind="ExternalOutput").ap()
    if "q0" in debug:
        dbg["q0"] = nc.dram_tensor("dbg_q0", [128, HPC * TS], BF16, kind="ExternalOutput").ap()
        dbg["negc0"] = nc.dram_tensor("dbg_negc0", [128, 4], F32, kind="ExternalOutput").ap()
        dbg["stats0"] = nc.dram_tensor("dbg_stats0", [128, 16], F32, kind="ExternalOutput").ap()
    if "osrc" in debug:
        dbg["osrc"] = nc.dram_tensor("dbg_osrc", [8 * 512, 1024], BF16, kind="ExternalOutput").ap()
    if "ogath" in debug:
        dbg["ogath"] = nc.dram_tensor("dbg_ogath", [8 * 512, 1024], BF16, kind="ExternalOutput").ap()
    if "z" in debug:
        dbg["z"] = nc.dram_tensor("dbg_z", [128, NDC * TS], F32, kind="ExternalOutput").ap()

    psSC = nc.alloc_psum_tensor("psSC", [128, 3072], F32)
    psF = nc.alloc_psum_tensor("psF", [128, 512], F32)
    psG = nc.alloc_psum_tensor("psG", [128, 512], F32)
    psA = psSC[:, 0:512]
    psB = psSC[:, 512:1024]

    S = lambda n: _Sem(nc, n)
    sxA, sxB = S("sxA"), S("sxB")
    stabA, stabB = S("stabA"), S("stabB")
    sw0 = S("sw0")
    sog = S("sog")
    sxr = S("sxr")
    swoA, swoB = S("swoA"), S("swoB")
    sao4 = [S(f"sao{i}") for i in range(4)]
    so = S("so")
    scc = S("scc")
    sg, sv, sa, sm = S("sg"), S("sv"), S("sa"), S("sm")

    W = _Waits()

    def edge(ename, eng, sem, inst):
        inst.then_inc(sem.h, 1)
        sem.v += 1
        eng.wait_ge(sem.h, sem.v)
        W.seen[(ename, sem.name)] = sem.v

    ones_bf = nc.alloc_sbuf_tensor("ones_bf", [128, 128], BF16)
    ones_f32 = nc.alloc_sbuf_tensor("ones_f32", [128, 128], F32)
    eps_col = nc.alloc_sbuf_tensor("eps_col", [128, 1], F32)

    stk1 = contextlib.ExitStack()
    sb = lambda n, s, d: stk1.enter_context(nc.sbuf_tensor(n, s, d))
    wq_bf = sb("wq_bf", [128, HPC * NDC * 128], BF16)
    wk_bf = sb("wk_bf", [128, NDC * 128], BF16)
    wv_bf = sb("wv_bf", [128, NDC * 128], BF16)
    kt_sb = sb("kt_sb", [128, B * T], BF16)
    v_sb = sb("v_sb", [128, B * T], BF16)
    xsA = sb("xsA", [128, 8 * TS], F32)
    xsB = sb("xsB", [128, 8 * TS], F32)
    xbf = sb("xbf", [128, NDC * TS], BF16)
    q_sb = sb("q_sb", [128, HPC * TS], BF16)
    ta_sb = sb("ta_sb", [128, TS], BF16)
    tb_sb = sb("tb_sb", [128, TS], BF16)
    sq_sb = sb("sq_sb", [128, TS], BF16)
    tab_sb = [sb(f"tab{p}", [128, 4 * TS], BF16) for p in range(2)]
    at_sb = [sb(f"at{p}", [128, 1024], BF16) for p in range(2)]
    ot_sb = [sb(f"ot{p}", [128, TS], BF16) for p in range(4)]
    linv_sb = sb("linv_sb", [128, TS], F32)
    masks_sb = sb("masks_sb", [128, 4 * TS], BF16)
    rmat_sb = sb("rmat_sb", [128, 128], BF16)
    qmax4 = sb("qmax4", [128, 4], F32)
    rmk = sb("rmk", [128, 1], F32)
    tmp1 = sb("tmp1", [128, 1], F32)
    prod4 = sb("prod4", [128, 4], F32)
    negc4 = sb("negc4", [128, 4], F32)
    stats_dbg = sb("stats_dbg", [128, 16], F32)

    with nc.Block() as block:
        pe = block.tensor
        act = block.scalar
        dve = block.vector
        gps = block.gpsimd
        syn = block.sync

        def _(g):
            g.memset(ones_bf[:], 1.0)
            g.memset(ones_f32[:], 1.0)
            g.memset(eps_col[:], 1e-5)
            g.memset(rmk[:], 0.0)
        gps(_)

        def _(s):
            s.dma_start(out=xsA[:], in_=wq[0:128, :]).then_inc(sw0.h, 16)
            s.dma_start(out=xsB[:], in_=wq[128:256, :]).then_inc(sw0.h, 16)
            s.dma_start(out=masks_sb[:], in_=masks).then_inc(sw0.h, 16)
            s.dma_start(out=rmat_sb[:], in_=rmat).then_inc(sw0.h, 16)
        syn(_)
        sw0.v += 64

        def _(g):
            W.w("gp", g, sw0, 64)
            g.tensor_copy(wq_bf[:, 0:4096], xsA[:])
            g.tensor_copy(wq_bf[:, 4096:8192], xsB[:]).then_inc(sg.h, 1)
        gps(_)
        sg.v += 1
        tok_wqcast = sg.v

        def _(s):
            W.w("sy", s, sg, tok_wqcast)
            s.dma_start(out=xsA[:], in_=wq[256:384, :]).then_inc(sw0.h, 16)
            s.dma_start(out=xsB[:], in_=wq[384:512, :]).then_inc(sw0.h, 16)
        syn(_)
        sw0.v += 32

        def _(g):
            W.w("gp", g, sw0, sw0.v)
            g.tensor_copy(wq_bf[:, 2 * 4096:3 * 4096], xsA[:])
            g.tensor_copy(wq_bf[:, 3 * 4096:4 * 4096], xsB[:]).then_inc(sg.h, 1)
        gps(_)
        sg.v += 1
        tok_wqcast2 = sg.v

        def _(s):
            W.w("sy", s, sg, tok_wqcast2)
            s.dma_start(out=xsA[:], in_=wk).then_inc(sw0.h, 16)
            s.dma_start(out=xsB[:], in_=wv).then_inc(sw0.h, 16)
        syn(_)
        sw0.v += 32

        def _(g):
            W.w("gp", g, sw0, sw0.v)
            g.tensor_copy(wk_bf[:], xsA[:])
            g.tensor_copy(wv_bf[:], xsB[:]).then_inc(sg.h, 1)
        gps(_)
        sg.v += 1
        tok_wkv = sg.v

        def _(s):
            W.w("sy", s, sg, tok_wkv)
            s.dma_start(out=xsA[:], in_=xt[0:128, :]).then_inc(sxA.h, 16)
            s.dma_start(out=xsB[:], in_=xt[128:256, :]).then_inc(sxB.h, 16)
            for i in range(4):
                s.dma_start(out=tab_sb[0][:, i * TS:(i + 1) * TS],
                            in_=tabs[i * 128:(i + 1) * 128, 0:TS]).then_inc(stabA.h, 16)
        syn(_)
        sxA.v += 16
        sxB.v += 16
        stabA.v += 64

        def _(v):
            W.w("dv", v, sxA, 16)
            v.tensor_copy(xbf[:, 0:4096], xsA[:]).then_inc(sv.h, 1)
            W.w("dv", v, sxB, 16)
            v.tensor_copy(xbf[:, 4096:8192], xsB[:]).then_inc(sv.h, 1)
        dve(_)
        sv.v += 2
        c01 = [sv.v - 1, sv.v]

        def _(s):
            W.w("sy", s, sv, c01[0])
            s.dma_start(out=xsA[:], in_=xt[256:384, :]).then_inc(sxA.h, 16)
            W.w("sy", s, sv, c01[1])
            s.dma_start(out=xsB[:], in_=xt[384:512, :]).then_inc(sxB.h, 16)
        syn(_)
        sxA.v += 16
        sxB.v += 16

        def _(v):
            W.w("dv", v, sxA, 32)
            v.tensor_copy(xbf[:, 8192:12288], xsA[:]).then_inc(sv.h, 1)
            W.w("dv", v, sxB, 32)
            v.tensor_copy(xbf[:, 12288:16384], xsB[:]).then_inc(sv.h, 1)
        dve(_)
        sv.v += 2
        tok_cast_next = sv.v
        xcast23 = [sv.v - 1, sv.v]

        xcast_tok = [0, 0]
        tok_xbf_consumed = 0
        tok_q_consumed = {}
        sc_hist = {0: 0, 1: 0, 2: 0}
        avl_hist = {0: 0, 1: 0}
        tok_ot_dve = 0
        tok_rope_add = 0
        tok_statsrd = 0
        tok_vevac = 0
        tok_ropemulA = 0
        tok_ropemulB = 0
        tok_tabread = [0, 0]
        tok_aosto = {}
        tok_lastexp = 0
        tok_negc_h = {0: 0, 1: 0, 2: 0, 3: 0}

        for slab in range(NSLAB):
            b, ts = divmod(slab, NTS)
            t0 = ts * TS
            tpar = slab % 2
            tabsem = stabA if tpar == 0 else stabB
            tab = tab_sb[tpar]
            tab_tok = 64 * (slab // 2 + 1)

            tok_cast = tok_cast_next

            if ts == 0 and b > 0:
                def _(g):
                    W.w("gp", g, sv, tok_statsrd)
                    g.memset(rmk[:], 0.0).then_inc(sg.h, 1)
                gps(_)
                sg.v += 1
                tok_rmk0 = sg.v
            else:
                tok_rmk0 = 0

            def _(t):
                W.w("pe", t, sv, tok_cast)
                W.w("pe", t, sa, tok_lastexp)
                W.w("pe", t, sv, tok_ropemulA)
                for dch in range(NDC):
                    mi = t.matmul(psA, wk_bf[:, dch * 128:(dch + 1) * 128],
                                  xbf[:, dch * TS:(dch + 1) * TS],
                                  start=(dch == 0), stop=(dch == NDC - 1))
                mi.then_inc(sm.h, 1)
            pe(_)
            sm.v += 1
            tokKproj = sm.v

            def _(v, tab=tab, tab_tok=tab_tok, tabsem=tabsem, tokKproj=tokKproj):
                W.w("dv", v, tabsem, tab_tok)
                W.w("dv", v, sm, tokKproj)
                v.tensor_mul(ta_sb[:], psA, tab[:, 2 * TS:3 * TS])
                v.tensor_mul(tb_sb[:], psA, tab[:, 3 * TS:4 * TS]).then_inc(sv.h, 1)
            dve(_)
            sv.v += 1
            tok_ropemulA = sv.v

            def _(t):
                W.w("pe", t, sv, tok_ropemulA)
                W.w("pe", t, sv, max(tok_ot_dve, tok_rope_add))
                t.matmul(psF[:], rmat_sb[:], tb_sb[:], start=True, stop=True).then_inc(sm.h, 1)
            pe(_)
            sm.v += 1
            tokKrope = sm.v

            def _(v, b=b, t0=t0, tokKrope=tokKrope):
                W.w("dv", v, sm, tokKrope)
                base = b * T + t0
                edge("dv", v, sv, v.tensor_add(kt_sb[:, base:base + TS], psF[:], ta_sb[:]))
                v.tensor_mul(sq_sb[:], kt_sb[:, base:base + TS], kt_sb[:, base:base + TS]).then_inc(sv.h, 1)
            dve(_)
            sv.v += 1
            tok_rope_add = sv.v
            tokKsq = sv.v

            def _(t, tokKsq=tokKsq):
                W.w("pe", t, sv, tokKsq)
                W.w("pe", t, sv, tok_statsrd)
                t.matmul(psG[:], ones_bf[:], sq_sb[:], start=True, stop=True).then_inc(sm.h, 1)
            pe(_)
            sm.v += 1
            tokKnorm = sm.v

            def _(v, tok_rmk0=tok_rmk0, tokKnorm=tokKnorm):
                W.w("dv", v, sm, tokKnorm)
                if tok_rmk0:
                    W.w("dv", v, sg, tok_rmk0)
                v.reduce_max(tmp1[:], psG[:], axis=AX.X).then_inc(sv.h, 1)
            dve(_)
            sv.v += 1
            tok_kred = sv.v

            def _(v, tok_kred=tok_kred):
                W.w("dv", v, sv, tok_kred)
                v.tensor_max(rmk[:], rmk[:], tmp1[:]).then_inc(sv.h, 1)
            dve(_)
            sv.v += 1
            tok_statsrd = sv.v

            def _(t):
                W.w("pe", t, sv, max(tok_vevac, tok_ropemulB))
                for tc in range(4):
                    for dch in range(NDC):
                        mi = t.matmul(psSC[:, 512 + tc * 128:512 + (tc + 1) * 128],
                                      xbf[:, dch * TS + tc * 128:dch * TS + (tc + 1) * 128],
                                      wv_bf[:, dch * 128:(dch + 1) * 128],
                                      start=(dch == 0), stop=(dch == NDC - 1))
                mi.then_inc(sm.h, 1)
            pe(_)
            sm.v += 1
            tokVproj = sm.v

            def _(v, b=b, ts=ts, tokVproj=tokVproj):
                W.w("dv", v, sm, tokVproj)
                base = (b * NDC + 4 * ts) * 128
                v.tensor_copy(v_sb[:, base:base + 512], psB).then_inc(sv.h, 1)
            dve(_)
            sv.v += 1
            tok_vevac = sv.v

            tok_qrope_mm = tokKrope
            for hh in range(HPC):
                pp = psA if hh % 2 == 0 else psB

                def _(t, hh=hh, pp=pp):
                    if hh % 2 == 0:
                        W.w("pe", t, sv, tok_ropemulA)
                    else:
                        W.w("pe", t, sv, max(tok_vevac, tok_ropemulB))
                    for dch in range(NDC):
                        mi = t.matmul(pp, wq_bf[:, (hh * NDC + dch) * 128:(hh * NDC + dch + 1) * 128],
                                      xbf[:, dch * TS:(dch + 1) * TS],
                                      start=(dch == 0), stop=(dch == NDC - 1))
                    mi.then_inc(sm.h, 1)
                pe(_)
                sm.v += 1
                tokQproj = sm.v
                if hh == HPC - 1:
                    tok_xbf_consumed = sm.v

                def _(v, pp=pp, tokQproj=tokQproj, tok_qrope_mm=tok_qrope_mm, tab=tab):
                    W.w("dv", v, sm, tokQproj)
                    W.w("dv", v, sm, tok_qrope_mm)
                    v.tensor_mul(ta_sb[:], pp, tab[:, 0:TS])
                    v.tensor_mul(tb_sb[:], pp, tab[:, TS:2 * TS]).then_inc(sv.h, 1)
                dve(_)
                sv.v += 1
                if hh % 2 == 0:
                    tok_ropemulA = sv.v
                else:
                    tok_ropemulB = sv.v
                tok_qmul = sv.v

                def _(t, tok_qmul=tok_qmul):
                    W.w("pe", t, sv, tok_qmul)
                    W.w("pe", t, sv, tok_rope_add)
                    t.matmul(psF[:], rmat_sb[:], tb_sb[:], start=True, stop=True).then_inc(sm.h, 1)
                pe(_)
                sm.v += 1
                tok_qrope_mm = sm.v

                def _(v, hh=hh, tok_qrope_mm=tok_qrope_mm):
                    W.w("dv", v, sm, tok_qrope_mm)
                    W.w("dv", v, sm, tok_q_consumed.get(hh, 0))
                    edge("dv", v, sv, v.tensor_add(q_sb[:, hh * TS:(hh + 1) * TS], psF[:], ta_sb[:]))
                    v.tensor_mul(sq_sb[:], q_sb[:, hh * TS:(hh + 1) * TS], q_sb[:, hh * TS:(hh + 1) * TS]).then_inc(sv.h, 1)
                dve(_)
                sv.v += 1
                tok_rope_add = sv.v
                tokQsq = sv.v

                def _(t, tokQsq=tokQsq):
                    W.w("pe", t, sv, tokQsq)
                    W.w("pe", t, sv, tok_statsrd)
                    t.matmul(psG[:], ones_bf[:], sq_sb[:], start=True, stop=True).then_inc(sm.h, 1)
                pe(_)
                sm.v += 1
                tokQnorm = sm.v

                def _(v, hh=hh, tokQnorm=tokQnorm):
                    W.w("dv", v, sm, tokQnorm)
                    v.reduce_max(qmax4[:, hh:hh + 1], psG[:], axis=AX.X).then_inc(sv.h, 1)
                dve(_)
                sv.v += 1
                tok_statsrd = sv.v

                def _(v, hh=hh):
                    W.w("dv", v, sv, tok_statsrd)
                    W.w("dv", v, sa, tok_negc_h[hh])
                    v.tensor_scalar(prod4[:, hh:hh + 1], qmax4[:, hh:hh + 1],
                                    rmk[:, 0:1], None, ALU.mult).then_inc(sv.h, 1)
                dve(_)
                sv.v += 1
                tok_prod_h = sv.v

                def _(sc, hh=hh, tok_prod_h=tok_prod_h):
                    W.w("ac", sc, sv, tok_prod_h)
                    pc = prod4[:, hh:hh + 1]
                    edge("ac", sc, sa, sc.activation(pc, pc, AF.Ln))
                    edge("ac", sc, sa, sc.activation(pc, pc, AF.Exp, scale=0.5))
                    sc.mul(negc4[:, hh:hh + 1], pc, -1.01).then_inc(sa.h, 1)
                act(_)
                sa.v += 1
                tok_negc_h[hh] = sa.v
            tok_tabread[tpar] = tok_rope_add


            if "q0" in dbg and slab == 0:
                def _(v):
                    W.w("dv", v, sa, sa.v)
                    v.tensor_copy(stats_dbg[:, 0:4], qmax4[:])
                    v.tensor_copy(stats_dbg[:, 4:5], rmk[:])
                    v.tensor_copy(stats_dbg[:, 5:9], prod4[:])
                    v.tensor_copy(stats_dbg[:, 9:10], tmp1[:]).then_inc(sv.h, 1)
                dve(_)
                sv.v += 1

                def _(s):
                    W.w("sy", s, sv, sv.v)
                    W.w("sy", s, sa, sa.v)
                    s.dma_start(out=dbg["q0"], in_=q_sb[:]).then_inc(so.h, 16)
                    s.dma_start(out=dbg["negc0"], in_=negc4[:]).then_inc(so.h, 16)
                    s.dma_start(out=dbg["stats0"], in_=stats_dbg[:]).then_inc(so.h, 16)
                syn(_)
                so.v += 48

            if slab + 1 < NSLAB:
                npar = (slab + 1) % 2
                ntabsem = stabA if npar == 0 else stabB

                def _(s, slab=slab, npar=npar, ntabsem=ntabsem):
                    W.w("sy", s, sv, xcast23[0])
                    s.dma_start(out=xsA[:], in_=xt[(4 * slab + 4) * 128:(4 * slab + 5) * 128, :]).then_inc(sxA.h, 16)
                    W.w("sy", s, sv, xcast23[1])
                    s.dma_start(out=xsB[:], in_=xt[(4 * slab + 5) * 128:(4 * slab + 6) * 128, :]).then_inc(sxB.h, 16)
                    W.w("sy", s, sv, tok_tabread[npar])
                    nt0 = ((slab + 1) % NTS) * TS
                    for i in range(4):
                        s.dma_start(out=tab_sb[npar][:, i * TS:(i + 1) * TS],
                                    in_=tabs[i * 128:(i + 1) * 128, nt0:nt0 + TS]).then_inc(ntabsem.h, 16)
                syn(_)
                sxA.v += 16
                sxB.v += 16
                ntabsem.v += 64

                def _(v, slab=slab):
                    W.w("dv", v, sxA, 16 * (2 * slab + 3))
                    W.w("dv", v, sm, tok_xbf_consumed)
                    v.tensor_copy(xbf[:, 0:4096], xsA[:]).then_inc(sv.h, 1)
                    W.w("dv", v, sxB, 16 * (2 * slab + 3))
                    v.tensor_copy(xbf[:, 4096:8192], xsB[:]).then_inc(sv.h, 1)
                dve(_)
                sv.v += 2
                nc01 = [sv.v - 1, sv.v]

                def _(s, slab=slab, nc01=nc01):
                    W.w("sy", s, sv, nc01[0])
                    s.dma_start(out=xsA[:], in_=xt[(4 * slab + 6) * 128:(4 * slab + 7) * 128, :]).then_inc(sxA.h, 16)
                    W.w("sy", s, sv, nc01[1])
                    s.dma_start(out=xsB[:], in_=xt[(4 * slab + 7) * 128:(4 * slab + 8) * 128, :]).then_inc(sxB.h, 16)
                syn(_)
                sxA.v += 16
                sxB.v += 16

                def _(v, slab=slab):
                    W.w("dv", v, sxA, 16 * (2 * slab + 4))
                    v.tensor_copy(xbf[:, 8192:12288], xsA[:]).then_inc(sv.h, 1)
                    W.w("dv", v, sxB, 16 * (2 * slab + 4))
                    v.tensor_copy(xbf[:, 12288:16384], xsB[:]).then_inc(sv.h, 1)
                dve(_)
                sv.v += 2
                tok_cast_next = sv.v
                xcast23 = [sv.v - 1, sv.v]

            nsc = 4 * (ts + 1)
            npairs = nsc // 2

            sa_p, sm_p, sg_p, sv_p = sa.v, sm.v, sg.v, sv.v
            p_scores, p_exp, p_mask, p_avl, p_ot = {}, {}, {}, {}, {}
            for hh in range(HPC):
                sm_p += 1
                p_scores[(hh, 0)] = sm_p
                if npairs > 1:
                    sm_p += 1
                    p_scores[(hh, 1)] = sm_p
                for p in range(npairs):
                    sa_p += 1
                    p_exp[(hh, p)] = sa_p
                    if 2 * p >= nsc - 4:
                        sg_p += 1
                        p_mask[(hh, p)] = sg_p
                    if p + 2 < npairs:
                        sm_p += 1
                        p_scores[(hh, p + 2)] = sm_p
                    sm_p += 1
                    p_avl[(hh, p)] = sm_p
                sv_p += 2
                p_ot[hh] = sv_p

            def gbuf(hh, p):
                return (hh * npairs + p) % 3

            def scwar(hh, p):
                g = hh * npairs + p - 3
                if g >= 0:
                    return p_exp[(g // npairs, g % npairs)]
                return sc_hist[gbuf(hh, p)]

            def avlwar(hh, p):
                g = hh * npairs + p - 2
                if g >= 0:
                    return p_avl[(g // npairs, g % npairs)]
                return avl_hist[p % 2]

            def scslice(hh, p):
                k = gbuf(hh, p)
                return psSC[:, k * 1024:(k + 1) * 1024]

            def _(sc, npairs=npairs):
                for hh in range(HPC):
                    W.w("ac", sc, sa, tok_negc_h[hh])
                    for p in range(npairs):
                        atb = at_sb[p % 2]
                        W.w("ac", sc, sm, p_scores[(hh, p)])
                        W.w("ac", sc, sm, avlwar(hh, p))
                        sc.activation(atb[:], scslice(hh, p), AF.Exp,
                                      bias=negc4[:, hh:hh + 1], scale=1.0).then_inc(sa.h, 1)
                        sa.v += 1
            act(_)

            def _(g, npairs=npairs, nsc=nsc):
                for hh in range(HPC):
                    for p in range(npairs):
                        if (hh, p) not in p_mask:
                            continue
                        atb = at_sb[p % 2]
                        W.w("gp", g, sa, p_exp[(hh, p)])
                        for half in range(2):
                            mii = 2 * p + half - (nsc - 4)
                            gi = g.tensor_mul(atb[:, half * 512:(half + 1) * 512],
                                              atb[:, half * 512:(half + 1) * 512],
                                              masks_sb[:, mii * TS:(mii + 1) * TS])
                        gi.then_inc(sg.h, 1)
                        sg.v += 1
            gps(_)

            def _(t, b=b, npairs=npairs, nsc=nsc):
                first_use = set()

                def emit_scores(hh, p):
                    k = gbuf(hh, p)
                    W.w("pe", t, sa, scwar(hh, p))
                    if k in (0, 1) and k not in first_use:
                        W.w("pe", t, sv, max(tok_ropemulA, tok_ropemulB))
                    first_use.add(k)
                    sl = scslice(hh, p)
                    for half in range(2):
                        s0 = b * T + (2 * p + half) * 128
                        mi = t.matmul(sl[:, half * 512:(half + 1) * 512],
                                      kt_sb[:, s0:s0 + 128],
                                      q_sb[:, hh * TS:(hh + 1) * TS], start=True, stop=True)
                    mi.then_inc(sm.h, 1)
                    sm.v += 1
                    assert sm.v == p_scores[(hh, p)]

                for hh in range(HPC):
                    emit_scores(hh, 0)
                    if npairs > 1:
                        emit_scores(hh, 1)
                    for p in range(npairs):
                        atb = at_sb[p % 2]
                        diag = (2 * p >= nsc - 4)
                        if p + 2 < npairs:
                            emit_scores(hh, p + 2)
                        if diag:
                            W.w("pe", t, sg, p_mask[(hh, p)])
                        else:
                            W.w("pe", t, sa, p_exp[(hh, p)])
                        if p == 0:
                            W.w("pe", t, sv, p_ot[hh - 1] if hh > 0
                                else max(tok_ot_dve, tok_rope_add, tok_statsrd))
                        for half in range(2):
                            sc_i = 2 * p + half
                            vbase = (b * NDC + sc_i) * 128
                            last = (sc_i == nsc - 1)
                            t.matmul(psF[:], v_sb[:, vbase:vbase + 128],
                                     atb[:, half * 512:(half + 1) * 512],
                                     start=(sc_i == 0), stop=last)
                            mi = t.matmul(psG[:], ones_bf[:],
                                          atb[:, half * 512:(half + 1) * 512],
                                          start=(sc_i == 0), stop=last)
                        mi.then_inc(sm.h, 1)
                        sm.v += 1
                        assert sm.v == p_avl[(hh, p)]
            pe(_)

            def _(v, npairs=npairs):
                for hh in range(HPC):
                    osem = sao4[hh]
                    W.w("dv", v, sm, p_avl[(hh, npairs - 1)])
                    W.w("dv", v, osem, osem.v)
                    edge("dv", v, sv, v.reciprocal(linv_sb[:], psG[:]))
                    v.tensor_mul(ot_sb[hh][:], psF[:], linv_sb[:]).then_inc(sv.h, 1)
                    sv.v += 1
                    assert sv.v == p_ot[hh]
            dve(_)

            def _(s, b=b, ts=ts):
                for hh in range(HPC):
                    osem = sao4[hh]
                    W.w("sy", s, sv, p_ot[hh])
                    blk = b * 4 + ts // 2
                    r0 = blk * 512 + hh * 128
                    c0 = (ts % 2) * 512
                    s.dma_start(out=a2a_src.ap()[r0:r0 + 128, c0:c0 + 512],
                                in_=ot_sb[hh][:]).then_inc(osem.h, 16)
                    osem.v += 16
            syn(_)

            assert (sa.v, sm.v, sg.v, sv.v) == (sa_p, sm_p, sg_p, sv_p)
            ntot = HPC * npairs
            for k in range(3):
                g = ntot - 1 - ((ntot - 1 - k) % 3)
                sc_hist[k] = p_exp[(g // npairs, g % npairs)]
            avl_hist[0] = p_avl[(HPC - 1, npairs - 2)]
            avl_hist[1] = p_avl[(HPC - 1, npairs - 1)]
            tok_lastexp = p_exp[(HPC - 1, npairs - 1)]
            tok_ot_dve = p_ot[HPC - 1]
            for hh in range(HPC):
                tok_q_consumed[hh] = p_avl[(hh, npairs - 1)]

        if "kt" in dbg or "v" in dbg or "q" in dbg:
            def _(sc):
                W.w("ac", sc, sv, sv.v)
                W.w("ac", sc, sa, sa.v - 0)
                if "kt" in dbg:
                    sc.dma_start(out=dbg["kt"], in_=kt_sb[:]).then_inc(so.h, 16); so.v += 16
                if "v" in dbg:
                    sc.dma_start(out=dbg["v"], in_=v_sb[:]).then_inc(so.h, 16); so.v += 16
                if "q" in dbg:
                    sc.dma_start(out=dbg["q"], in_=q_sb[:]).then_inc(so.h, 16); so.v += 16
            act(_)

        def _(g):
            for i in range(4):
                W.w("gp", g, sao4[i], 16 * 16)
            with allow_ici_all_to_all():
                g.collective_compute(
                    "AllToAll", ALU.bypass,
                    replica_groups=[[0, 1, 2, 3, 4, 5, 6, 7]],
                    ins=[a2a_src.ap().opt()],
                    outs=[a2a_dst.ap().opt()],
                ).then_inc(scc.h, 1)
        gps(_)
        scc.v += 1

        if "osrc" in dbg or "ogath" in dbg:
            def _(g):
                W.w("gp", g, scc, 1)
                if "osrc" in dbg:
                    g.dma_start(out=dbg["osrc"], in_=a2a_src.ap()).then_inc(so.h, 16); so.v += 16
                if "ogath" in dbg:
                    g.dma_start(out=dbg["ogath"], in_=a2a_dst.ap()).then_inc(so.h, 16); so.v += 16
            gps(_)

        stk1.close()
        og_sb = nc.alloc_sbuf_tensor("og_sb", [128, NDC * TS], BF16)
        z_sb = nc.alloc_sbuf_tensor("z_sb", [128, NDC * TS], F32)
        yz_bf = nc.alloc_sbuf_tensor("yz_bf", [128, NDC * TS], BF16)
        wst = [nc.alloc_sbuf_tensor(f"wst{p}", [128, 4096], F32) for p in range(2)]
        wbf = [nc.alloc_sbuf_tensor(f"wbf{p}", [128, 4096], BF16) for p in range(4)]
        sqs = [nc.alloc_sbuf_tensor(f"sqs{p}", [128, TS], BF16) for p in range(2)]
        m_sb = nc.alloc_sbuf_tensor("m_sb", [128, TS], F32)
        m2_sb = nc.alloc_sbuf_tensor("m2_sb", [128, TS], F32)
        ve_sb = nc.alloc_sbuf_tensor("ve_sb", [128, TS], F32)
        rstd_sb = nc.alloc_sbuf_tensor("rstd_sb", [128, TS], F32)

        tok_wcast = {0: 0, 1: 0, 2: 0, 3: 0}
        tok_wmm = {0: 0, 1: 0, 2: 0, 3: 0}
        tok_wld = {0: 0, 1: 0}
        tok_wstfree = {0: 0, 1: 0}
        tok_zadd = {0: 0, 1: 0}
        tok_ybf = {}
        tok_sq = {0: 0, 1: 0}
        nonlocal_holder = [0]
        tok_zout = 0
        tok_m_prev = 0
        tok_ve_prev = 0
        tok_lnapply_last = 0

        def emit_wload(jch, wsrc):
            p2 = jch % 2
            p4 = jch % 4
            wsem = swoA if p2 == 0 else swoB

            def _(s, jch=jch, p2=p2, wsrc=wsrc, wsem=wsem):
                W.w("sy", s, sv, tok_wstfree[p2])
                s.dma_start(out=wst[p2][:],
                            in_=wsrc[jch * 128:(jch + 1) * 128, :]).then_inc(wsem.h, 16)
            syn(_)
            wsem.v += 16
            ld_tok = wsem.v

            def _(v, p2=p2, p4=p4, ld_tok=ld_tok, wsem=wsem):
                W.w("dv", v, wsem, ld_tok)
                W.w("dv", v, sm, tok_wmm[p4])
                v.tensor_copy(wbf[p4][:], wst[p2][:]).then_inc(sv.h, 1)
            dve(_)
            sv.v += 1
            tok_wcast[p4] = sv.v
            tok_wstfree[p2] = sv.v

        def emit_stats(jch, phase):
            par = jch % 2

            def _(t, jch=jch, par=par, phase=phase):
                if jch == 0:
                    W.w("pe", t, sa, tok_m_prev)
                    W.w("pe", t, sv, tok_ve_prev)
                W.w("pe", t, sv, tok_zadd_d[jch])
                t.matmul(psSC[:, 1024:1536], ones_f32[:],
                         z_sb[:, jch * TS:(jch + 1) * TS],
                         start=(jch == 0), stop=(jch == NDC - 1))
                W.w("pe", t, sv, tok_sqw_d[jch])
                t.matmul(psSC[:, 1536:2048], ones_bf[:], sqs[par][:],
                         start=(jch == 0), stop=(jch == NDC - 1)).then_inc(sm.h, 1)
            pe(_)
            sm.v += 1
            tok_sq[par] = sm.v

        emit_wload(0, wo)
        emit_wload(1, wo)

        for u in range(2):
            def _(s, u=u):
                W.w("sy", s, scc, 1)
                W.w("sy", s, sm, nonlocal_holder[0])
                for ech in range(NDC):
                    s.dma_start(out=og_sb[:, ech * TS:(ech + 1) * TS],
                                in_=a2a_dst.ap()[ech * 128:(ech + 1) * 128, u * TS:(u + 1) * TS]
                                ).then_inc(sog.h, 16)
            syn(_)
            sog.v += 16 * NDC
            tok_ogld = sog.v

            def _(s, u=u):
                W.w("sy", s, so, tok_zout)
                s.dma_start(out=z_sb[:], in_=xres[u * 128:(u + 1) * 128, :]).then_inc(sxr.h, 16)
            syn(_)
            sxr.v += 16
            tok_xr = sxr.v

            for phase in range(2):
                wsrc = wo if phase == 0 else wf
                tok_zadd_d = {}
                tok_sqw_d = {}
                for jch in range(NDC):
                    par = jch % 2
                    p4 = jch % 4
                    nxt = jch + 2
                    if nxt < NDC:
                        emit_wload(nxt, wsrc)
                    elif phase == 0:
                        emit_wload(nxt - NDC, wf)
                    elif u == 0:
                        emit_wload(nxt - NDC, wo)

                    def _(t, jch=jch, par=par, p4=p4, phase=phase, tok_ogld=tok_ogld):
                        W.w("pe", t, sv, tok_wcast[p4])
                        W.w("pe", t, sa, tok_lastexp)
                        if phase == 0:
                            W.w("pe", t, sog, tok_ogld)
                        elif jch > 0:
                            W.w("pe", t, sg, tok_ybf[NDC - 1])
                        W.w("pe", t, sv, tok_zadd[par])
                        pd = psSC[:, par * 512:(par + 1) * 512]
                        rhs = og_sb if phase == 0 else yz_bf
                        for ech in range(NDC):
                            if phase == 1 and jch == 0:
                                W.w("pe", t, sg, tok_ybf[ech])
                            mi = t.matmul(pd, wbf[p4][:, ech * 128:(ech + 1) * 128],
                                          rhs[:, ech * TS:(ech + 1) * TS],
                                          start=(ech == 0), stop=(ech == NDC - 1))
                        mi.then_inc(sm.h, 1)
                    pe(_)
                    sm.v += 1
                    tok_wmm[p4] = sm.v
                    if phase == 0:
                        nonlocal_holder[0] = max(nonlocal_holder[0], sm.v)
                    tok_proj = sm.v

                    def _(v, jch=jch, par=par, tok_proj=tok_proj, tok_xr=tok_xr):
                        W.w("dv", v, sm, tok_proj)
                        W.w("dv", v, sxr, tok_xr)
                        pd = psSC[:, par * 512:(par + 1) * 512]
                        zc = z_sb[:, jch * TS:(jch + 1) * TS]
                        v.tensor_add(zc, pd, zc).then_inc(sv.h, 1)
                    dve(_)
                    sv.v += 1
                    tok_zadd[par] = sv.v
                    tok_zadd_d[jch] = sv.v

                    def _(v, jch=jch, par=par):
                        W.w("dv", v, sm, tok_sq[par])
                        v.tensor_mul(sqs[par][:], z_sb[:, jch * TS:(jch + 1) * TS],
                                     z_sb[:, jch * TS:(jch + 1) * TS]).then_inc(sv.h, 1)
                    dve(_)
                    sv.v += 1
                    tok_sqw_d[jch] = sv.v

                    if jch > 0:
                        emit_stats(jch - 1, phase)
                emit_stats(NDC - 1, phase)
                tok_stats = sm.v

                def _(sc, tok_stats=tok_stats):
                    W.w("ac", sc, sm, tok_stats)
                    W.w("ac", sc, sv, tok_lnapply_last)
                    sc.mul(m_sb[:], psSC[:, 1024:1536], 1.0 / D).then_inc(sa.h, 1)
                act(_)
                sa.v += 1
                tok_m = sa.v
                tok_m_prev = tok_m

                def _(v, tok_m=tok_m):
                    W.w("dv", v, sa, tok_m)
                    edge("dv", v, sv, v.tensor_mul(m2_sb[:], m_sb[:], m_sb[:]))
                    v.scalar_tensor_tensor(ve_sb[:], psSC[:, 1536:2048], 1.0 / D,
                                           m2_sb[:], ALU.mult, ALU.subtract).then_inc(sv.h, 1)
                dve(_)
                sv.v += 1
                tok_ve = sv.v
                tok_ve_prev = tok_ve

                def _(sc, tok_ve=tok_ve):
                    W.w("ac", sc, sv, tok_ve)
                    edge("ac", sc, sa, sc.activation(ve_sb[:], ve_sb[:], AF.Ln, bias=eps_col[:, 0:1]))
                    sc.activation(rstd_sb[:], ve_sb[:], AF.Exp, scale=-0.5).then_inc(sa.h, 1)
                act(_)
                sa.v += 1
                tok_rstd = sa.v

                tok_y_d = {}

                def _(v, tok_rstd=tok_rstd):
                    W.w("dv", v, sa, tok_rstd)
                    for jch in range(NDC):
                        zc = z_sb[:, jch * TS:(jch + 1) * TS]
                        edge("dv", v, sv, v.tensor_sub(zc, zc, m_sb[:]))
                        v.tensor_mul(zc, zc, rstd_sb[:]).then_inc(sv.h, 1)
                        sv.v += 1
                        tok_y_d[jch] = sv.v
                dve(_)
                tok_lnapply_last = sv.v

                if phase == 1:
                    def _(g, u=u):
                        for jch in range(NDC):
                            W.w("gp", g, sv, tok_y_d[jch])
                            r0 = (u * NDC + jch) * 128
                            g.dma_start(out=out[r0:r0 + 128, :],
                                        in_=z_sb[:, jch * TS:(jch + 1) * TS]).then_inc(so.h, 16)
                            so.v += 16
                    gps(_)

                if phase == 0:
                    def _(g):
                        W.w("gp", g, sm, tok_stats)
                        for jch in range(NDC):
                            W.w("gp", g, sv, tok_y_d[jch])
                            g.tensor_copy(yz_bf[:, jch * TS:(jch + 1) * TS],
                                          z_sb[:, jch * TS:(jch + 1) * TS]).then_inc(sg.h, 1)
                            sg.v += 1
                            tok_ybf[jch] = sg.v
                    gps(_)
                else:
                    tok_zout = so.v

            if "z" in dbg and u == 0:
                def _(sc):
                    W.w("ac", sc, sv, sv.v)
                    sc.dma_start(out=dbg["z"], in_=z_sb[:]).then_inc(so.h, 16); so.v += 16
                act(_)

        final_so = so.v

        def _(sc):
            W.w("ac", sc, so, final_so)
        act(_)

    return nc



def _stage_shared(X, Wq, Wk, Wv, Wo, Wf):
    sh = {}
    xt = np.empty((NSLAB, 4, 128, 8, TS), np.float32)
    for b in range(B):
        Xb = np.asarray(X[b])
        for ts in range(NTS):
            sl = Xb[ts * TS:(ts + 1) * TS, :]
            blk = sl.T.reshape(NDC, 128, TS)
            for qq in range(4):
                xt[b * NTS + ts, qq] = blk[8 * qq:8 * qq + 8].transpose(1, 0, 2)
    sh["xt"] = np.ascontiguousarray(xt.reshape(NSLAB * 4 * 128, 8 * TS))

    def wblock(Wm):
        out = np.empty((NDC, 128, NDC, 128), np.float32)
        for jch in range(NDC):
            blk = np.asarray(Wm[:, jch * 128:(jch + 1) * 128]).reshape(NDC, 128, 128)
            out[jch] = blk.transpose(1, 0, 2)
        return np.ascontiguousarray(out.reshape(NDC * 128, NDC * 128))

    sh["wo"] = wblock(Wo)
    sh["wf"] = wblock(Wf)

    pos = np.arange(T, dtype=np.float64)
    inv = 10000.0 ** (-np.arange(64, dtype=np.float64) / 64.0)
    ang = np.outer(pos, inv)
    c64 = np.cos(ang).T.astype(np.float32)
    s64 = np.sin(ang).T.astype(np.float32)
    ct = np.concatenate([c64, c64], 0)
    st = np.concatenate([s64, -s64], 0)
    scl = np.float32(1.0 / np.sqrt(DH))
    sh["tabs"] = np.ascontiguousarray(np.concatenate([ct * scl, st * scl, ct, st], 0)).astype(ml_dtypes.bfloat16)

    mk = np.zeros((128, 4 * TS), np.float32)
    pp = np.arange(128)[:, None]
    ff = np.arange(TS)[None, :]
    for i in range(4):
        mk[:, i * TS:(i + 1) * TS] = (pp + 128 * i <= ff)
    sh["masks"] = mk.astype(ml_dtypes.bfloat16)

    R = np.zeros((128, 128), np.float32)
    R[np.arange(128), (np.arange(128) + 64) % 128] = 1.0
    sh["rmat"] = R.astype(ml_dtypes.bfloat16)
    return sh


def _stage_core(c, X, Wq, Wk, Wv):
    m = {}
    wqc = np.empty((HPC, 128, NDC, 128), np.float32)
    for hh in range(HPC):
        col0 = 512 * c + 128 * hh
        wqc[hh] = np.asarray(Wq[:, col0:col0 + 128]).reshape(NDC, 128, 128).transpose(1, 0, 2)
    m["wq"] = np.ascontiguousarray(wqc.reshape(HPC * 128, NDC * 128))
    m["wk"] = np.ascontiguousarray(
        np.asarray(Wk[:, 128 * c:128 * (c + 1)]).reshape(NDC, 128, 128).transpose(1, 0, 2).reshape(128, NDC * 128))
    m["wv"] = np.ascontiguousarray(
        np.asarray(Wv[:, 128 * c:128 * (c + 1)]).reshape(NDC, 128, 128).transpose(1, 0, 2).reshape(128, NDC * 128))
    b_o, q_o = c // 4, c % 4
    xr = np.empty((2, 128, NDC, TS), np.float32)
    Xb = np.asarray(X[b_o])
    for u in range(2):
        t0 = 1024 * q_o + 512 * u
        sl = Xb[t0:t0 + TS, :]
        xr[u] = sl.T.reshape(NDC, 128, TS).transpose(1, 0, 2)
    m["xres"] = np.ascontiguousarray(xr.reshape(2 * 128, NDC * TS))
    return m


def kernel(L0_attn_X, L0_attn_Wq, L0_attn_Wk, L0_attn_Wv, L0_attn_Wo, L0_ffn_weight):
    X = np.asarray(L0_attn_X, np.float32)
    sh = _stage_shared(X, L0_attn_Wq, L0_attn_Wk, L0_attn_Wv, L0_attn_Wo, L0_ffn_weight)
    in_maps = []
    for c in range(8):
        m = dict(sh)
        m.update(_stage_core(c, X, L0_attn_Wq, L0_attn_Wk, L0_attn_Wv))
        in_maps.append(m)

    nc = build_program(debug=_DEBUG)
    res = run_bass_kernel_spmd(nc, in_maps, core_ids=list(range(8)))

    outp = np.empty((B, T, D), np.float32)
    for c in range(8):
        b_o, q_o = c // 4, c % 4
        arr = np.asarray(res.results[c]["out"]).reshape(2, NDC, 128, TS)
        yt = arr.transpose(1, 2, 0, 3).reshape(D, 1024)
        outp[b_o, 1024 * q_o:1024 * (q_o + 1), :] = yt.T
    return outp


# version 15
# speedup vs baseline: 1.0720x; 1.0720x over previous
import contextlib
import numpy as np
import ml_dtypes
import concourse.bass as bass
import concourse.mybir as mybir
from concourse.bass_utils import run_bass_kernel_spmd
from concourse.aot_env import allow_ici_all_to_all

F32 = mybir.dt.float32
BF16 = mybir.dt.bfloat16
AF = mybir.ActivationFunctionType
ALU = mybir.AluOpType
AX = mybir.AxisListType

B, T, D, DH = 2, 4096, 4096, 128
NTS = 8
TS = 512
NDC = D // 128
HPC = 4
NSLAB = B * NTS

_DEBUG = ()


class _Sem:
    __slots__ = ("h", "v", "name")

    def __init__(self, nc, name):
        self.h = nc.alloc_semaphore(name)
        self.v = 0
        self.name = name


class _Waits:

    def __init__(self):
        self.seen = {}

    def w(self, ename, eng, sem, tok):
        if tok <= 0:
            return
        k = (ename, sem.name)
        if self.seen.get(k, 0) >= tok:
            return
        eng.wait_ge(sem.h, tok)
        self.seen[k] = tok


def build_program(debug=()):
    nc = bass.Bass("TRN2", target_bir_lowering=False, debug=False)

    xt = nc.dram_tensor("xt", [NSLAB * 4 * 128, 8 * TS], F32, kind="ExternalInput").ap()
    wq = nc.dram_tensor("wq", [HPC * 128, NDC * 128], F32, kind="ExternalInput").ap()
    wk = nc.dram_tensor("wk", [128, NDC * 128], F32, kind="ExternalInput").ap()
    wv = nc.dram_tensor("wv", [128, NDC * 128], F32, kind="ExternalInput").ap()
    wo = nc.dram_tensor("wo", [NDC * 128, NDC * 128], F32, kind="ExternalInput").ap()
    wf = nc.dram_tensor("wf", [NDC * 128, NDC * 128], F32, kind="ExternalInput").ap()
    xres = nc.dram_tensor("xres", [2 * 128, NDC * TS], F32, kind="ExternalInput").ap()
    tabs = nc.dram_tensor("tabs", [4 * 128, T], BF16, kind="ExternalInput").ap()
    masks = nc.dram_tensor("masks", [128, 4 * TS], BF16, kind="ExternalInput").ap()
    rmat = nc.dram_tensor("rmat", [128, 128], BF16, kind="ExternalInput").ap()
    out = nc.dram_tensor("out", [2 * NDC * 128, TS], F32, kind="ExternalOutput").ap()
    a2a_src = nc.dram_tensor("a2a_src", [8 * 512, 1024], BF16)
    a2a_dst = nc.dram_tensor("a2a_dst", [8 * 512, 1024], BF16)

    dbg = {}
    if "kt" in debug:
        dbg["kt"] = nc.dram_tensor("dbg_kt", [128, B * T], BF16, kind="ExternalOutput").ap()
    if "v" in debug:
        dbg["v"] = nc.dram_tensor("dbg_v", [128, B * T], BF16, kind="ExternalOutput").ap()
    if "q" in debug:
        dbg["q"] = nc.dram_tensor("dbg_q", [128, HPC * TS], BF16, kind="ExternalOutput").ap()
    if "q0" in debug:
        dbg["q0"] = nc.dram_tensor("dbg_q0", [128, HPC * TS], BF16, kind="ExternalOutput").ap()
        dbg["negc0"] = nc.dram_tensor("dbg_negc0", [128, 4], F32, kind="ExternalOutput").ap()
        dbg["stats0"] = nc.dram_tensor("dbg_stats0", [128, 16], F32, kind="ExternalOutput").ap()
    if "osrc" in debug:
        dbg["osrc"] = nc.dram_tensor("dbg_osrc", [8 * 512, 1024], BF16, kind="ExternalOutput").ap()
    if "ogath" in debug:
        dbg["ogath"] = nc.dram_tensor("dbg_ogath", [8 * 512, 1024], BF16, kind="ExternalOutput").ap()
    if "z" in debug:
        dbg["z"] = nc.dram_tensor("dbg_z", [128, NDC * TS], F32, kind="ExternalOutput").ap()

    psSC = nc.alloc_psum_tensor("psSC", [128, 3072], F32)
    psF = nc.alloc_psum_tensor("psF", [128, 512], F32)
    psG = nc.alloc_psum_tensor("psG", [128, 512], F32)
    psA = psSC[:, 0:512]
    psB = psSC[:, 512:1024]

    S = lambda n: _Sem(nc, n)
    sxA, sxB = S("sxA"), S("sxB")
    stabA, stabB = S("stabA"), S("stabB")
    sw0 = S("sw0")
    sog = S("sog")
    sxr = S("sxr")
    swoA, swoB = S("swoA"), S("swoB")
    sao4 = [S(f"sao{i}") for i in range(4)]
    so = S("so")
    scc = S("scc")
    sg, sv, sa, sm = S("sg"), S("sv"), S("sa"), S("sm")

    W = _Waits()

    def edge(ename, eng, sem, inst):
        inst.then_inc(sem.h, 1)
        sem.v += 1
        eng.wait_ge(sem.h, sem.v)
        W.seen[(ename, sem.name)] = sem.v

    ones_bf = nc.alloc_sbuf_tensor("ones_bf", [128, 128], BF16)
    ones_f32 = nc.alloc_sbuf_tensor("ones_f32", [128, 128], F32)
    eps_col = nc.alloc_sbuf_tensor("eps_col", [128, 1], F32)

    stk1 = contextlib.ExitStack()
    sb = lambda n, s, d: stk1.enter_context(nc.sbuf_tensor(n, s, d))
    wq_bf = sb("wq_bf", [128, HPC * NDC * 128], BF16)
    wk_bf = sb("wk_bf", [128, NDC * 128], BF16)
    wv_bf = sb("wv_bf", [128, NDC * 128], BF16)
    kt_sb = sb("kt_sb", [128, B * T], BF16)
    v_sb = sb("v_sb", [128, B * T], BF16)
    xsA = sb("xsA", [128, 8 * TS], F32)
    xsB = sb("xsB", [128, 8 * TS], F32)
    xbf = sb("xbf", [128, NDC * TS], BF16)
    q_sb = sb("q_sb", [128, HPC * TS], BF16)
    ta_sb = sb("ta_sb", [128, TS], BF16)
    tb_sb = sb("tb_sb", [128, TS], BF16)
    sq_sb = sb("sq_sb", [128, TS], BF16)
    tab_sb = [sb(f"tab{p}", [128, 4 * TS], BF16) for p in range(2)]
    at_sb = [sb(f"at{p}", [128, 1024], BF16) for p in range(2)]
    ot_sb = [sb(f"ot{p}", [128, TS], BF16) for p in range(4)]
    linv_sb = sb("linv_sb", [128, TS], F32)
    masks_sb = sb("masks_sb", [128, 4 * TS], BF16)
    rmat_sb = sb("rmat_sb", [128, 128], BF16)
    qmax4 = sb("qmax4", [128, 4], F32)
    rmk = sb("rmk", [128, 1], F32)
    tmp1 = sb("tmp1", [128, 1], F32)
    prod4 = sb("prod4", [128, 4], F32)
    negc4 = sb("negc4", [128, 4], F32)
    stats_dbg = sb("stats_dbg", [128, 16], F32)

    with nc.Block() as block:
        pe = block.tensor
        act = block.scalar
        dve = block.vector
        gps = block.gpsimd
        syn = block.sync

        def _(g):
            g.memset(ones_bf[:], 1.0)
            g.memset(ones_f32[:], 1.0)
            g.memset(eps_col[:], 1e-5)
            g.memset(rmk[:], 0.0)
        gps(_)

        def _(s):
            s.dma_start(out=xsA[:], in_=wq[0:128, :]).then_inc(sw0.h, 16)
            s.dma_start(out=xsB[:], in_=wq[128:256, :]).then_inc(sw0.h, 16)
            s.dma_start(out=masks_sb[:], in_=masks).then_inc(sw0.h, 16)
            s.dma_start(out=rmat_sb[:], in_=rmat).then_inc(sw0.h, 16)
        syn(_)
        sw0.v += 64

        def _(g):
            W.w("gp", g, sw0, 64)
            g.tensor_copy(wq_bf[:, 0:4096], xsA[:])
            g.tensor_copy(wq_bf[:, 4096:8192], xsB[:]).then_inc(sg.h, 1)
        gps(_)
        sg.v += 1
        tok_wqcast = sg.v

        def _(s):
            W.w("sy", s, sg, tok_wqcast)
            s.dma_start(out=xsA[:], in_=wq[256:384, :]).then_inc(sw0.h, 16)
            s.dma_start(out=xsB[:], in_=wq[384:512, :]).then_inc(sw0.h, 16)
        syn(_)
        sw0.v += 32

        def _(g):
            W.w("gp", g, sw0, sw0.v)
            g.tensor_copy(wq_bf[:, 2 * 4096:3 * 4096], xsA[:])
            g.tensor_copy(wq_bf[:, 3 * 4096:4 * 4096], xsB[:]).then_inc(sg.h, 1)
        gps(_)
        sg.v += 1
        tok_wqcast2 = sg.v

        def _(s):
            W.w("sy", s, sg, tok_wqcast2)
            s.dma_start(out=xsA[:], in_=wk).then_inc(sw0.h, 16)
            s.dma_start(out=xsB[:], in_=wv).then_inc(sw0.h, 16)
        syn(_)
        sw0.v += 32

        def _(g):
            W.w("gp", g, sw0, sw0.v)
            g.tensor_copy(wk_bf[:], xsA[:])
            g.tensor_copy(wv_bf[:], xsB[:]).then_inc(sg.h, 1)
        gps(_)
        sg.v += 1
        tok_wkv = sg.v

        def _(s):
            W.w("sy", s, sg, tok_wkv)
            s.dma_start(out=xsA[:], in_=xt[0:128, :]).then_inc(sxA.h, 16)
            s.dma_start(out=xsB[:], in_=xt[128:256, :]).then_inc(sxB.h, 16)
            for i in range(4):
                s.dma_start(out=tab_sb[0][:, i * TS:(i + 1) * TS],
                            in_=tabs[i * 128:(i + 1) * 128, 0:TS]).then_inc(stabA.h, 16)
        syn(_)
        sxA.v += 16
        sxB.v += 16
        stabA.v += 64

        def _(v):
            W.w("dv", v, sxA, 16)
            v.tensor_copy(xbf[:, 0:4096], xsA[:]).then_inc(sv.h, 1)
            W.w("dv", v, sxB, 16)
            v.tensor_copy(xbf[:, 4096:8192], xsB[:]).then_inc(sv.h, 1)
        dve(_)
        sv.v += 2
        c01 = [sv.v - 1, sv.v]

        def _(s):
            W.w("sy", s, sv, c01[0])
            s.dma_start(out=xsA[:], in_=xt[256:384, :]).then_inc(sxA.h, 16)
            W.w("sy", s, sv, c01[1])
            s.dma_start(out=xsB[:], in_=xt[384:512, :]).then_inc(sxB.h, 16)
        syn(_)
        sxA.v += 16
        sxB.v += 16

        def _(v):
            W.w("dv", v, sxA, 32)
            v.tensor_copy(xbf[:, 8192:12288], xsA[:]).then_inc(sv.h, 1)
            W.w("dv", v, sxB, 32)
            v.tensor_copy(xbf[:, 12288:16384], xsB[:]).then_inc(sv.h, 1)
        dve(_)
        sv.v += 2
        tok_cast_next = sv.v
        xcast23 = [sv.v - 1, sv.v]

        xcast_tok = [0, 0]
        tok_xbf_consumed = 0
        tok_q_consumed = {}
        sc_hist = {0: 0, 1: 0, 2: 0}
        avl_hist = {0: 0, 1: 0}
        tok_ot_dve = 0
        tok_rope_add = 0
        tok_statsrd = 0
        tok_vevac = 0
        tok_ropemulA = 0
        tok_ropemulB = 0
        tok_tabread = [0, 0]
        tok_aosto = {}
        tok_lastexp = 0
        tok_rmk_read = 0
        tok_negc_h = {0: 0, 1: 0, 2: 0, 3: 0}

        for slab in range(NSLAB):
            b, ts = divmod(slab, NTS)
            t0 = ts * TS
            tpar = slab % 2
            tabsem = stabA if tpar == 0 else stabB
            tab = tab_sb[tpar]
            tab_tok = 64 * (slab // 2 + 1)

            tok_cast = tok_cast_next

            if ts == 0 and b > 0:
                def _(g):
                    W.w("gp", g, sv, max(tok_statsrd, tok_rmk_read))
                    g.memset(rmk[:], 0.0).then_inc(sg.h, 1)
                gps(_)
                sg.v += 1
                tok_rmk0 = sg.v
            else:
                tok_rmk0 = 0

            def _(t):
                W.w("pe", t, sv, tok_cast)
                W.w("pe", t, sa, tok_lastexp)
                W.w("pe", t, sv, tok_ropemulA)
                for dch in range(NDC):
                    mi = t.matmul(psA, wk_bf[:, dch * 128:(dch + 1) * 128],
                                  xbf[:, dch * TS:(dch + 1) * TS],
                                  start=(dch == 0), stop=(dch == NDC - 1))
                mi.then_inc(sm.h, 1)
            pe(_)
            sm.v += 1
            tokKproj = sm.v

            def _(v, tab=tab, tab_tok=tab_tok, tabsem=tabsem, tokKproj=tokKproj):
                W.w("dv", v, tabsem, tab_tok)
                W.w("dv", v, sm, tokKproj)
                v.tensor_mul(ta_sb[:], psA, tab[:, 2 * TS:3 * TS])
                v.tensor_mul(tb_sb[:], psA, tab[:, 3 * TS:4 * TS]).then_inc(sv.h, 1)
            dve(_)
            sv.v += 1
            tok_ropemulA = sv.v

            def _(t):
                W.w("pe", t, sv, tok_ropemulA)
                W.w("pe", t, sv, max(tok_ot_dve, tok_rope_add))
                t.matmul(psF[:], rmat_sb[:], tb_sb[:], start=True, stop=True).then_inc(sm.h, 1)
            pe(_)
            sm.v += 1
            tokKrope = sm.v

            def _(v, b=b, t0=t0, tokKrope=tokKrope):
                W.w("dv", v, sm, tokKrope)
                base = b * T + t0
                edge("dv", v, sv, v.tensor_add(kt_sb[:, base:base + TS], psF[:], ta_sb[:]))
                v.tensor_mul(sq_sb[:], kt_sb[:, base:base + TS], kt_sb[:, base:base + TS]).then_inc(sv.h, 1)
            dve(_)
            sv.v += 1
            tok_rope_add = sv.v
            tokKsq = sv.v

            def _(t, tokKsq=tokKsq):
                W.w("pe", t, sv, tokKsq)
                W.w("pe", t, sv, max(tok_statsrd, tok_ot_dve))
                t.matmul(psG[:], ones_bf[:], sq_sb[:], start=True, stop=True).then_inc(sm.h, 1)
            pe(_)
            sm.v += 1
            tokKnorm = sm.v

            def _(v, tok_rmk0=tok_rmk0, tokKnorm=tokKnorm):
                W.w("dv", v, sm, tokKnorm)
                if tok_rmk0:
                    W.w("dv", v, sg, tok_rmk0)
                v.reduce_max(tmp1[:], psG[:], axis=AX.X).then_inc(sv.h, 1)
            dve(_)
            sv.v += 1
            tok_kred = sv.v

            def _(v, tok_kred=tok_kred):
                W.w("dv", v, sv, tok_kred)
                v.tensor_max(rmk[:], rmk[:], tmp1[:]).then_inc(sv.h, 1)
            dve(_)
            sv.v += 1
            tok_statsrd = sv.v

            def _(t):
                W.w("pe", t, sv, max(tok_vevac, tok_ropemulB))
                for tc in range(4):
                    for dch in range(NDC):
                        mi = t.matmul(psSC[:, 512 + tc * 128:512 + (tc + 1) * 128],
                                      xbf[:, dch * TS + tc * 128:dch * TS + (tc + 1) * 128],
                                      wv_bf[:, dch * 128:(dch + 1) * 128],
                                      start=(dch == 0), stop=(dch == NDC - 1))
                mi.then_inc(sm.h, 1)
            pe(_)
            sm.v += 1
            tokVproj = sm.v

            def _(v, b=b, ts=ts, tokVproj=tokVproj):
                W.w("dv", v, sm, tokVproj)
                base = (b * NDC + 4 * ts) * 128
                v.tensor_copy(v_sb[:, base:base + 512], psB).then_inc(sv.h, 1)
            dve(_)
            sv.v += 1
            tok_vevac = sv.v

            tok_qrope_mm = tokKrope
            for hh in range(HPC):
                pp = psA if hh % 2 == 0 else psB

                def _(t, hh=hh, pp=pp):
                    if hh % 2 == 0:
                        W.w("pe", t, sv, tok_ropemulA)
                    else:
                        W.w("pe", t, sv, max(tok_vevac, tok_ropemulB))
                    for dch in range(NDC):
                        mi = t.matmul(pp, wq_bf[:, (hh * NDC + dch) * 128:(hh * NDC + dch + 1) * 128],
                                      xbf[:, dch * TS:(dch + 1) * TS],
                                      start=(dch == 0), stop=(dch == NDC - 1))
                    mi.then_inc(sm.h, 1)
                pe(_)
                sm.v += 1
                tokQproj = sm.v
                if hh == HPC - 1:
                    tok_xbf_consumed = sm.v

                def _(v, pp=pp, tokQproj=tokQproj, tok_qrope_mm=tok_qrope_mm, tab=tab):
                    W.w("dv", v, sm, tokQproj)
                    W.w("dv", v, sm, tok_qrope_mm)
                    v.tensor_mul(ta_sb[:], pp, tab[:, 0:TS])
                    v.tensor_mul(tb_sb[:], pp, tab[:, TS:2 * TS]).then_inc(sv.h, 1)
                dve(_)
                sv.v += 1
                if hh % 2 == 0:
                    tok_ropemulA = sv.v
                else:
                    tok_ropemulB = sv.v
                tok_qmul = sv.v

                def _(t, tok_qmul=tok_qmul):
                    W.w("pe", t, sv, tok_qmul)
                    W.w("pe", t, sv, tok_rope_add)
                    t.matmul(psF[:], rmat_sb[:], tb_sb[:], start=True, stop=True).then_inc(sm.h, 1)
                pe(_)
                sm.v += 1
                tok_qrope_mm = sm.v

                def _(v, hh=hh, tok_qrope_mm=tok_qrope_mm):
                    W.w("dv", v, sm, tok_qrope_mm)
                    W.w("dv", v, sm, tok_q_consumed.get(hh, 0))
                    edge("dv", v, sv, v.tensor_add(q_sb[:, hh * TS:(hh + 1) * TS], psF[:], ta_sb[:]))
                    v.tensor_mul(sq_sb[:], q_sb[:, hh * TS:(hh + 1) * TS], q_sb[:, hh * TS:(hh + 1) * TS]).then_inc(sv.h, 1)
                dve(_)
                sv.v += 1
                tok_rope_add = sv.v
                tokQsq = sv.v

                def _(t, tokQsq=tokQsq):
                    W.w("pe", t, sv, tokQsq)
                    W.w("pe", t, sv, max(tok_statsrd, tok_ot_dve))
                    t.matmul(psG[:], ones_bf[:], sq_sb[:], start=True, stop=True).then_inc(sm.h, 1)
                pe(_)
                sm.v += 1
                tokQnorm = sm.v

                def _(v, hh=hh, tokQnorm=tokQnorm):
                    W.w("dv", v, sm, tokQnorm)
                    v.reduce_max(qmax4[:, hh:hh + 1], psG[:], axis=AX.X).then_inc(sv.h, 1)
                dve(_)
                sv.v += 1
                tok_statsrd = sv.v

                def _(v, hh=hh):
                    W.w("dv", v, sv, tok_statsrd)
                    W.w("dv", v, sa, tok_negc_h[hh])
                    v.tensor_scalar(prod4[:, hh:hh + 1], qmax4[:, hh:hh + 1],
                                    rmk[:, 0:1], None, ALU.mult).then_inc(sv.h, 1)
                dve(_)
                sv.v += 1
                tok_prod_h = sv.v
                tok_rmk_read = sv.v

                def _(sc, hh=hh, tok_prod_h=tok_prod_h):
                    W.w("ac", sc, sv, tok_prod_h)
                    pc = prod4[:, hh:hh + 1]
                    edge("ac", sc, sa, sc.activation(pc, pc, AF.Ln))
                    edge("ac", sc, sa, sc.activation(pc, pc, AF.Exp, scale=0.5))
                    sc.mul(negc4[:, hh:hh + 1], pc, -1.01).then_inc(sa.h, 1)
                act(_)
                sa.v += 1
                tok_negc_h[hh] = sa.v
            tok_tabread[tpar] = tok_rope_add


            if "q0" in dbg and slab == 0:
                def _(v):
                    W.w("dv", v, sa, sa.v)
                    v.tensor_copy(stats_dbg[:, 0:4], qmax4[:])
                    v.tensor_copy(stats_dbg[:, 4:5], rmk[:])
                    v.tensor_copy(stats_dbg[:, 5:9], prod4[:])
                    v.tensor_copy(stats_dbg[:, 9:10], tmp1[:]).then_inc(sv.h, 1)
                dve(_)
                sv.v += 1

                def _(s):
                    W.w("sy", s, sv, sv.v)
                    W.w("sy", s, sa, sa.v)
                    s.dma_start(out=dbg["q0"], in_=q_sb[:]).then_inc(so.h, 16)
                    s.dma_start(out=dbg["negc0"], in_=negc4[:]).then_inc(so.h, 16)
                    s.dma_start(out=dbg["stats0"], in_=stats_dbg[:]).then_inc(so.h, 16)
                syn(_)
                so.v += 48

            if slab + 1 < NSLAB:
                npar = (slab + 1) % 2
                ntabsem = stabA if npar == 0 else stabB

                def _(s, slab=slab, npar=npar, ntabsem=ntabsem):
                    W.w("sy", s, sv, xcast23[0])
                    s.dma_start(out=xsA[:], in_=xt[(4 * slab + 4) * 128:(4 * slab + 5) * 128, :]).then_inc(sxA.h, 16)
                    W.w("sy", s, sv, xcast23[1])
                    s.dma_start(out=xsB[:], in_=xt[(4 * slab + 5) * 128:(4 * slab + 6) * 128, :]).then_inc(sxB.h, 16)
                    W.w("sy", s, sv, tok_tabread[npar])
                    nt0 = ((slab + 1) % NTS) * TS
                    for i in range(4):
                        s.dma_start(out=tab_sb[npar][:, i * TS:(i + 1) * TS],
                                    in_=tabs[i * 128:(i + 1) * 128, nt0:nt0 + TS]).then_inc(ntabsem.h, 16)
                syn(_)
                sxA.v += 16
                sxB.v += 16
                ntabsem.v += 64

                def _(v, slab=slab):
                    W.w("dv", v, sxA, 16 * (2 * slab + 3))
                    W.w("dv", v, sm, tok_xbf_consumed)
                    v.tensor_copy(xbf[:, 0:4096], xsA[:]).then_inc(sv.h, 1)
                    W.w("dv", v, sxB, 16 * (2 * slab + 3))
                    v.tensor_copy(xbf[:, 4096:8192], xsB[:]).then_inc(sv.h, 1)
                dve(_)
                sv.v += 2
                nc01 = [sv.v - 1, sv.v]

                def _(s, slab=slab, nc01=nc01):
                    W.w("sy", s, sv, nc01[0])
                    s.dma_start(out=xsA[:], in_=xt[(4 * slab + 6) * 128:(4 * slab + 7) * 128, :]).then_inc(sxA.h, 16)
                    W.w("sy", s, sv, nc01[1])
                    s.dma_start(out=xsB[:], in_=xt[(4 * slab + 7) * 128:(4 * slab + 8) * 128, :]).then_inc(sxB.h, 16)
                syn(_)
                sxA.v += 16
                sxB.v += 16

                def _(v, slab=slab):
                    W.w("dv", v, sxA, 16 * (2 * slab + 4))
                    v.tensor_copy(xbf[:, 8192:12288], xsA[:]).then_inc(sv.h, 1)
                    W.w("dv", v, sxB, 16 * (2 * slab + 4))
                    v.tensor_copy(xbf[:, 12288:16384], xsB[:]).then_inc(sv.h, 1)
                dve(_)
                sv.v += 2
                tok_cast_next = sv.v
                xcast23 = [sv.v - 1, sv.v]

            nsc = 4 * (ts + 1)
            npairs = nsc // 2

            sa_p, sm_p, sg_p, sv_p = sa.v, sm.v, sg.v, sv.v
            p_scores, p_exp, p_mask, p_avl, p_ot = {}, {}, {}, {}, {}
            for hh in range(HPC):
                sm_p += 1
                p_scores[(hh, 0)] = sm_p
                if npairs > 1:
                    sm_p += 1
                    p_scores[(hh, 1)] = sm_p
                for p in range(npairs):
                    sa_p += 1
                    p_exp[(hh, p)] = sa_p
                    if 2 * p >= nsc - 4:
                        sg_p += 1
                        p_mask[(hh, p)] = sg_p
                    if p + 2 < npairs:
                        sm_p += 1
                        p_scores[(hh, p + 2)] = sm_p
                    sm_p += 1
                    p_avl[(hh, p)] = sm_p
                sv_p += 2
                p_ot[hh] = sv_p

            def gbuf(hh, p):
                return (hh * npairs + p) % 3

            def scwar(hh, p):
                g = hh * npairs + p - 3
                if g >= 0:
                    return p_exp[(g // npairs, g % npairs)]
                return sc_hist[gbuf(hh, p)]

            def avlwar(hh, p):
                g = hh * npairs + p - 2
                if g >= 0:
                    return p_avl[(g // npairs, g % npairs)]
                return avl_hist[p % 2]

            def scslice(hh, p):
                k = gbuf(hh, p)
                return psSC[:, k * 1024:(k + 1) * 1024]

            def _(sc, npairs=npairs):
                for hh in range(HPC):
                    W.w("ac", sc, sa, tok_negc_h[hh])
                    for p in range(npairs):
                        atb = at_sb[p % 2]
                        W.w("ac", sc, sm, p_scores[(hh, p)])
                        W.w("ac", sc, sm, avlwar(hh, p))
                        sc.activation(atb[:], scslice(hh, p), AF.Exp,
                                      bias=negc4[:, hh:hh + 1], scale=1.0).then_inc(sa.h, 1)
                        sa.v += 1
            act(_)

            def _(t, b=b, npairs=npairs, nsc=nsc):
                first_use = set()

                def emit_scores(hh, p):
                    k = gbuf(hh, p)
                    W.w("pe", t, sa, scwar(hh, p))
                    if k in (0, 1) and k not in first_use:
                        W.w("pe", t, sv, max(tok_ropemulA, tok_ropemulB))
                    first_use.add(k)
                    sl = scslice(hh, p)
                    for half in range(2):
                        s0 = b * T + (2 * p + half) * 128
                        mi = t.matmul(sl[:, half * 512:(half + 1) * 512],
                                      kt_sb[:, s0:s0 + 128],
                                      q_sb[:, hh * TS:(hh + 1) * TS], start=True, stop=True)
                    mi.then_inc(sm.h, 1)
                    sm.v += 1
                    assert sm.v == p_scores[(hh, p)]

                for hh in range(HPC):
                    emit_scores(hh, 0)
                    if npairs > 1:
                        emit_scores(hh, 1)
                    for p in range(npairs):
                        atb = at_sb[p % 2]
                        diag = (2 * p >= nsc - 4)
                        if p + 2 < npairs:
                            emit_scores(hh, p + 2)
                        if diag:
                            W.w("pe", t, sg, p_mask[(hh, p)])
                        else:
                            W.w("pe", t, sa, p_exp[(hh, p)])
                        if p == 0:
                            W.w("pe", t, sv, p_ot[hh - 1] if hh > 0
                                else max(tok_ot_dve, tok_rope_add, tok_statsrd))
                        for half in range(2):
                            sc_i = 2 * p + half
                            vbase = (b * NDC + sc_i) * 128
                            last = (sc_i == nsc - 1)
                            t.matmul(psF[:], v_sb[:, vbase:vbase + 128],
                                     atb[:, half * 512:(half + 1) * 512],
                                     start=(sc_i == 0), stop=last)
                            mi = t.matmul(psG[:], ones_bf[:],
                                          atb[:, half * 512:(half + 1) * 512],
                                          start=(sc_i == 0), stop=last)
                        mi.then_inc(sm.h, 1)
                        sm.v += 1
                        assert sm.v == p_avl[(hh, p)]
            pe(_)

            def _(g, npairs=npairs, nsc=nsc):
                for hh in range(HPC):
                    for p in range(npairs):
                        if (hh, p) not in p_mask:
                            continue
                        atb = at_sb[p % 2]
                        W.w("gp", g, sa, p_exp[(hh, p)])
                        for half in range(2):
                            mii = 2 * p + half - (nsc - 4)
                            gi = g.tensor_mul(atb[:, half * 512:(half + 1) * 512],
                                              atb[:, half * 512:(half + 1) * 512],
                                              masks_sb[:, mii * TS:(mii + 1) * TS])
                        gi.then_inc(sg.h, 1)
                        sg.v += 1
            gps(_)

            def _(v, npairs=npairs):
                for hh in range(HPC):
                    osem = sao4[hh]
                    W.w("dv", v, sm, p_avl[(hh, npairs - 1)])
                    W.w("dv", v, osem, osem.v)
                    edge("dv", v, sv, v.reciprocal(linv_sb[:], psG[:]))
                    v.tensor_mul(ot_sb[hh][:], psF[:], linv_sb[:]).then_inc(sv.h, 1)
                    sv.v += 1
                    assert sv.v == p_ot[hh]
            dve(_)

            def _(s, b=b, ts=ts):
                for hh in range(HPC):
                    osem = sao4[hh]
                    W.w("sy", s, sv, p_ot[hh])
                    blk = b * 4 + ts // 2
                    r0 = blk * 512 + hh * 128
                    c0 = (ts % 2) * 512
                    s.dma_start(out=a2a_src.ap()[r0:r0 + 128, c0:c0 + 512],
                                in_=ot_sb[hh][:]).then_inc(osem.h, 16)
                    osem.v += 16
            syn(_)

            assert (sa.v, sm.v, sg.v, sv.v) == (sa_p, sm_p, sg_p, sv_p), (slab, (sa.v, sm.v, sg.v, sv.v), (sa_p, sm_p, sg_p, sv_p))
            ntot = HPC * npairs
            for k in range(3):
                g = ntot - 1 - ((ntot - 1 - k) % 3)
                sc_hist[k] = p_exp[(g // npairs, g % npairs)]
            avl_hist[0] = p_avl[(HPC - 1, npairs - 2)]
            avl_hist[1] = p_avl[(HPC - 1, npairs - 1)]
            tok_lastexp = p_exp[(HPC - 1, npairs - 1)]
            tok_ot_dve = p_ot[HPC - 1]
            for hh in range(HPC):
                tok_q_consumed[hh] = p_avl[(hh, npairs - 1)]

        if "kt" in dbg or "v" in dbg or "q" in dbg:
            def _(sc):
                W.w("ac", sc, sv, sv.v)
                W.w("ac", sc, sa, sa.v - 0)
                if "kt" in dbg:
                    sc.dma_start(out=dbg["kt"], in_=kt_sb[:]).then_inc(so.h, 16); so.v += 16
                if "v" in dbg:
                    sc.dma_start(out=dbg["v"], in_=v_sb[:]).then_inc(so.h, 16); so.v += 16
                if "q" in dbg:
                    sc.dma_start(out=dbg["q"], in_=q_sb[:]).then_inc(so.h, 16); so.v += 16
            act(_)

        def _(g):
            for i in range(4):
                W.w("gp", g, sao4[i], 16 * 16)
            with allow_ici_all_to_all():
                g.collective_compute(
                    "AllToAll", ALU.bypass,
                    replica_groups=[[0, 1, 2, 3, 4, 5, 6, 7]],
                    ins=[a2a_src.ap().opt()],
                    outs=[a2a_dst.ap().opt()],
                ).then_inc(scc.h, 1)
        gps(_)
        scc.v += 1

        if "osrc" in dbg or "ogath" in dbg:
            def _(g):
                W.w("gp", g, scc, 1)
                if "osrc" in dbg:
                    g.dma_start(out=dbg["osrc"], in_=a2a_src.ap()).then_inc(so.h, 16); so.v += 16
                if "ogath" in dbg:
                    g.dma_start(out=dbg["ogath"], in_=a2a_dst.ap()).then_inc(so.h, 16); so.v += 16
            gps(_)

        stk1.close()
        og_sb = nc.alloc_sbuf_tensor("og_sb", [128, NDC * TS], BF16)
        z_sb = nc.alloc_sbuf_tensor("z_sb", [128, NDC * TS], F32)
        yz_bf = nc.alloc_sbuf_tensor("yz_bf", [128, NDC * TS], BF16)
        wst = [nc.alloc_sbuf_tensor(f"wst{p}", [128, 4096], F32) for p in range(2)]
        wbf = [nc.alloc_sbuf_tensor(f"wbf{p}", [128, 4096], BF16) for p in range(4)]
        sqs = [nc.alloc_sbuf_tensor(f"sqs{p}", [128, TS], BF16) for p in range(2)]
        m_sb = nc.alloc_sbuf_tensor("m_sb", [128, TS], F32)
        m2_sb = nc.alloc_sbuf_tensor("m2_sb", [128, TS], F32)
        ve_sb = nc.alloc_sbuf_tensor("ve_sb", [128, TS], F32)
        rstd_sb = nc.alloc_sbuf_tensor("rstd_sb", [128, TS], F32)

        tok_wcast = {0: 0, 1: 0, 2: 0, 3: 0}
        tok_wmm = {0: 0, 1: 0, 2: 0, 3: 0}
        tok_wld = {0: 0, 1: 0}
        tok_wstfree = {0: 0, 1: 0}
        tok_zadd = {0: 0, 1: 0}
        tok_ybf = {}
        tok_sq = {0: 0, 1: 0}
        nonlocal_holder = [0]
        tok_zout = 0
        tok_m_prev = 0
        tok_ve_prev = 0
        tok_lnapply_last = 0

        def emit_wload(jch, wsrc):
            p2 = jch % 2
            p4 = jch % 4
            wsem = swoA if p2 == 0 else swoB

            def _(s, jch=jch, p2=p2, wsrc=wsrc, wsem=wsem):
                W.w("sy", s, sv, tok_wstfree[p2])
                s.dma_start(out=wst[p2][:],
                            in_=wsrc[jch * 128:(jch + 1) * 128, :]).then_inc(wsem.h, 16)
            syn(_)
            wsem.v += 16
            ld_tok = wsem.v

            def _(v, p2=p2, p4=p4, ld_tok=ld_tok, wsem=wsem):
                W.w("dv", v, wsem, ld_tok)
                W.w("dv", v, sm, tok_wmm[p4])
                v.tensor_copy(wbf[p4][:], wst[p2][:]).then_inc(sv.h, 1)
            dve(_)
            sv.v += 1
            tok_wcast[p4] = sv.v
            tok_wstfree[p2] = sv.v

        def emit_stats(jch, phase):
            par = jch % 2

            def _(t, jch=jch, par=par, phase=phase):
                if jch == 0:
                    W.w("pe", t, sa, tok_m_prev)
                    W.w("pe", t, sv, tok_ve_prev)
                W.w("pe", t, sv, tok_zadd_d[jch])
                t.matmul(psSC[:, 1024:1536], ones_f32[:],
                         z_sb[:, jch * TS:(jch + 1) * TS],
                         start=(jch == 0), stop=(jch == NDC - 1))
                W.w("pe", t, sv, tok_sqw_d[jch])
                t.matmul(psSC[:, 1536:2048], ones_bf[:], sqs[par][:],
                         start=(jch == 0), stop=(jch == NDC - 1)).then_inc(sm.h, 1)
            pe(_)
            sm.v += 1
            tok_sq[par] = sm.v

        emit_wload(0, wo)
        emit_wload(1, wo)

        for u in range(2):
            def _(s, u=u):
                W.w("sy", s, scc, 1)
                W.w("sy", s, sm, nonlocal_holder[0])
                for ech in range(NDC):
                    s.dma_start(out=og_sb[:, ech * TS:(ech + 1) * TS],
                                in_=a2a_dst.ap()[ech * 128:(ech + 1) * 128, u * TS:(u + 1) * TS]
                                ).then_inc(sog.h, 16)
            syn(_)
            sog.v += 16 * NDC
            tok_ogld = sog.v

            def _(s, u=u):
                W.w("sy", s, so, tok_zout)
                s.dma_start(out=z_sb[:], in_=xres[u * 128:(u + 1) * 128, :]).then_inc(sxr.h, 16)
            syn(_)
            sxr.v += 16
            tok_xr = sxr.v

            for phase in range(2):
                wsrc = wo if phase == 0 else wf
                tok_zadd_d = {}
                tok_sqw_d = {}
                for jch in range(NDC):
                    par = jch % 2
                    p4 = jch % 4
                    nxt = jch + 2
                    if nxt < NDC:
                        emit_wload(nxt, wsrc)
                    elif phase == 0:
                        emit_wload(nxt - NDC, wf)
                    elif u == 0:
                        emit_wload(nxt - NDC, wo)

                    def _(t, jch=jch, par=par, p4=p4, phase=phase, tok_ogld=tok_ogld):
                        W.w("pe", t, sv, tok_wcast[p4])
                        W.w("pe", t, sa, tok_lastexp)
                        if phase == 0:
                            W.w("pe", t, sog, tok_ogld)
                        elif jch > 0:
                            W.w("pe", t, sg, tok_ybf[NDC - 1])
                        W.w("pe", t, sv, tok_zadd[par])
                        pd = psSC[:, par * 512:(par + 1) * 512]
                        rhs = og_sb if phase == 0 else yz_bf
                        for ech in range(NDC):
                            if phase == 1 and jch == 0:
                                W.w("pe", t, sg, tok_ybf[ech])
                            mi = t.matmul(pd, wbf[p4][:, ech * 128:(ech + 1) * 128],
                                          rhs[:, ech * TS:(ech + 1) * TS],
                                          start=(ech == 0), stop=(ech == NDC - 1))
                        mi.then_inc(sm.h, 1)
                    pe(_)
                    sm.v += 1
                    tok_wmm[p4] = sm.v
                    if phase == 0:
                        nonlocal_holder[0] = max(nonlocal_holder[0], sm.v)
                    tok_proj = sm.v

                    def _(v, jch=jch, par=par, tok_proj=tok_proj, tok_xr=tok_xr):
                        W.w("dv", v, sm, tok_proj)
                        W.w("dv", v, sxr, tok_xr)
                        pd = psSC[:, par * 512:(par + 1) * 512]
                        zc = z_sb[:, jch * TS:(jch + 1) * TS]
                        v.tensor_add(zc, pd, zc).then_inc(sv.h, 1)
                    dve(_)
                    sv.v += 1
                    tok_zadd[par] = sv.v
                    tok_zadd_d[jch] = sv.v

                    def _(v, jch=jch, par=par):
                        W.w("dv", v, sm, tok_sq[par])
                        v.tensor_mul(sqs[par][:], z_sb[:, jch * TS:(jch + 1) * TS],
                                     z_sb[:, jch * TS:(jch + 1) * TS]).then_inc(sv.h, 1)
                    dve(_)
                    sv.v += 1
                    tok_sqw_d[jch] = sv.v

                    if jch > 0:
                        emit_stats(jch - 1, phase)
                emit_stats(NDC - 1, phase)
                tok_stats = sm.v

                def _(sc, tok_stats=tok_stats):
                    W.w("ac", sc, sm, tok_stats)
                    W.w("ac", sc, sv, tok_lnapply_last)
                    sc.mul(m_sb[:], psSC[:, 1024:1536], 1.0 / D).then_inc(sa.h, 1)
                act(_)
                sa.v += 1
                tok_m = sa.v
                tok_m_prev = tok_m

                def _(v, tok_m=tok_m):
                    W.w("dv", v, sa, tok_m)
                    edge("dv", v, sv, v.tensor_mul(m2_sb[:], m_sb[:], m_sb[:]))
                    v.scalar_tensor_tensor(ve_sb[:], psSC[:, 1536:2048], 1.0 / D,
                                           m2_sb[:], ALU.mult, ALU.subtract).then_inc(sv.h, 1)
                dve(_)
                sv.v += 1
                tok_ve = sv.v
                tok_ve_prev = tok_ve

                def _(sc, tok_ve=tok_ve):
                    W.w("ac", sc, sv, tok_ve)
                    edge("ac", sc, sa, sc.activation(ve_sb[:], ve_sb[:], AF.Ln, bias=eps_col[:, 0:1]))
                    sc.activation(rstd_sb[:], ve_sb[:], AF.Exp, scale=-0.5).then_inc(sa.h, 1)
                act(_)
                sa.v += 1
                tok_rstd = sa.v

                tok_y_d = {}

                def _(v, tok_rstd=tok_rstd):
                    W.w("dv", v, sa, tok_rstd)
                    for jch in range(NDC):
                        zc = z_sb[:, jch * TS:(jch + 1) * TS]
                        edge("dv", v, sv, v.tensor_sub(zc, zc, m_sb[:]))
                        v.tensor_mul(zc, zc, rstd_sb[:]).then_inc(sv.h, 1)
                        sv.v += 1
                        tok_y_d[jch] = sv.v
                dve(_)
                tok_lnapply_last = sv.v

                if phase == 1:
                    def _(g, u=u):
                        for jch in range(NDC):
                            W.w("gp", g, sv, tok_y_d[jch])
                            r0 = (u * NDC + jch) * 128
                            g.dma_start(out=out[r0:r0 + 128, :],
                                        in_=z_sb[:, jch * TS:(jch + 1) * TS]).then_inc(so.h, 16)
                            so.v += 16
                    gps(_)

                if phase == 0:
                    def _(g):
                        W.w("gp", g, sm, tok_stats)
                        for jch in range(NDC):
                            W.w("gp", g, sv, tok_y_d[jch])
                            g.tensor_copy(yz_bf[:, jch * TS:(jch + 1) * TS],
                                          z_sb[:, jch * TS:(jch + 1) * TS]).then_inc(sg.h, 1)
                            sg.v += 1
                            tok_ybf[jch] = sg.v
                    gps(_)
                else:
                    tok_zout = so.v

            if "z" in dbg and u == 0:
                def _(sc):
                    W.w("ac", sc, sv, sv.v)
                    sc.dma_start(out=dbg["z"], in_=z_sb[:]).then_inc(so.h, 16); so.v += 16
                act(_)

        final_so = so.v

        def _(sc):
            W.w("ac", sc, so, final_so)
        act(_)

    return nc



def _stage_shared(X, Wq, Wk, Wv, Wo, Wf):
    sh = {}
    xt = np.empty((NSLAB, 4, 128, 8, TS), np.float32)
    for b in range(B):
        Xb = np.asarray(X[b])
        for ts in range(NTS):
            sl = Xb[ts * TS:(ts + 1) * TS, :]
            blk = sl.T.reshape(NDC, 128, TS)
            for qq in range(4):
                xt[b * NTS + ts, qq] = blk[8 * qq:8 * qq + 8].transpose(1, 0, 2)
    sh["xt"] = np.ascontiguousarray(xt.reshape(NSLAB * 4 * 128, 8 * TS))

    def wblock(Wm):
        out = np.empty((NDC, 128, NDC, 128), np.float32)
        for jch in range(NDC):
            blk = np.asarray(Wm[:, jch * 128:(jch + 1) * 128]).reshape(NDC, 128, 128)
            out[jch] = blk.transpose(1, 0, 2)
        return np.ascontiguousarray(out.reshape(NDC * 128, NDC * 128))

    sh["wo"] = wblock(Wo)
    sh["wf"] = wblock(Wf)

    pos = np.arange(T, dtype=np.float64)
    inv = 10000.0 ** (-np.arange(64, dtype=np.float64) / 64.0)
    ang = np.outer(pos, inv)
    c64 = np.cos(ang).T.astype(np.float32)
    s64 = np.sin(ang).T.astype(np.float32)
    ct = np.concatenate([c64, c64], 0)
    st = np.concatenate([s64, -s64], 0)
    scl = np.float32(1.0 / np.sqrt(DH))
    sh["tabs"] = np.ascontiguousarray(np.concatenate([ct * scl, st * scl, ct, st], 0)).astype(ml_dtypes.bfloat16)

    mk = np.zeros((128, 4 * TS), np.float32)
    pp = np.arange(128)[:, None]
    ff = np.arange(TS)[None, :]
    for i in range(4):
        mk[:, i * TS:(i + 1) * TS] = (pp + 128 * i <= ff)
    sh["masks"] = mk.astype(ml_dtypes.bfloat16)

    R = np.zeros((128, 128), np.float32)
    R[np.arange(128), (np.arange(128) + 64) % 128] = 1.0
    sh["rmat"] = R.astype(ml_dtypes.bfloat16)
    return sh


def _stage_core(c, X, Wq, Wk, Wv):
    m = {}
    wqc = np.empty((HPC, 128, NDC, 128), np.float32)
    for hh in range(HPC):
        col0 = 512 * c + 128 * hh
        wqc[hh] = np.asarray(Wq[:, col0:col0 + 128]).reshape(NDC, 128, 128).transpose(1, 0, 2)
    m["wq"] = np.ascontiguousarray(wqc.reshape(HPC * 128, NDC * 128))
    m["wk"] = np.ascontiguousarray(
        np.asarray(Wk[:, 128 * c:128 * (c + 1)]).reshape(NDC, 128, 128).transpose(1, 0, 2).reshape(128, NDC * 128))
    m["wv"] = np.ascontiguousarray(
        np.asarray(Wv[:, 128 * c:128 * (c + 1)]).reshape(NDC, 128, 128).transpose(1, 0, 2).reshape(128, NDC * 128))
    b_o, q_o = c // 4, c % 4
    xr = np.empty((2, 128, NDC, TS), np.float32)
    Xb = np.asarray(X[b_o])
    for u in range(2):
        t0 = 1024 * q_o + 512 * u
        sl = Xb[t0:t0 + TS, :]
        xr[u] = sl.T.reshape(NDC, 128, TS).transpose(1, 0, 2)
    m["xres"] = np.ascontiguousarray(xr.reshape(2 * 128, NDC * TS))
    return m


def kernel(L0_attn_X, L0_attn_Wq, L0_attn_Wk, L0_attn_Wv, L0_attn_Wo, L0_ffn_weight):
    X = np.asarray(L0_attn_X, np.float32)
    sh = _stage_shared(X, L0_attn_Wq, L0_attn_Wk, L0_attn_Wv, L0_attn_Wo, L0_ffn_weight)
    in_maps = []
    for c in range(8):
        m = dict(sh)
        m.update(_stage_core(c, X, L0_attn_Wq, L0_attn_Wk, L0_attn_Wv))
        in_maps.append(m)

    nc = build_program(debug=_DEBUG)
    res = run_bass_kernel_spmd(nc, in_maps, core_ids=list(range(8)))

    outp = np.empty((B, T, D), np.float32)
    for c in range(8):
        b_o, q_o = c // 4, c % 4
        arr = np.asarray(res.results[c]["out"]).reshape(2, NDC, 128, TS)
        yt = arr.transpose(1, 2, 0, 3).reshape(D, 1024)
        outp[b_o, 1024 * q_o:1024 * (q_o + 1), :] = yt.T
    return outp


# version 17
# speedup vs baseline: 1.0958x; 1.0222x over previous
import contextlib
import numpy as np
import ml_dtypes
import concourse.bass as bass
import concourse.mybir as mybir
from concourse.bass_utils import run_bass_kernel_spmd
from concourse.aot_env import allow_ici_all_to_all

F32 = mybir.dt.float32
BF16 = mybir.dt.bfloat16
AF = mybir.ActivationFunctionType
ALU = mybir.AluOpType
AX = mybir.AxisListType

B, T, D, DH = 2, 4096, 4096, 128
NTS = 8
TS = 512
NDC = D // 128
HPC = 4
NSLAB = B * NTS

_DEBUG = ()


class _Sem:
    __slots__ = ("h", "v", "name")

    def __init__(self, nc, name):
        self.h = nc.alloc_semaphore(name)
        self.v = 0
        self.name = name


class _Waits:

    def __init__(self):
        self.seen = {}

    def w(self, ename, eng, sem, tok):
        if tok <= 0:
            return
        k = (ename, sem.name)
        if self.seen.get(k, 0) >= tok:
            return
        eng.wait_ge(sem.h, tok)
        self.seen[k] = tok


def build_program(debug=()):
    nc = bass.Bass("TRN2", target_bir_lowering=False, debug=False)

    xt = nc.dram_tensor("xt", [NSLAB * 4 * 128, 8 * TS], F32, kind="ExternalInput").ap()
    wq = nc.dram_tensor("wq", [HPC * 128, NDC * 128], F32, kind="ExternalInput").ap()
    wk = nc.dram_tensor("wk", [128, NDC * 128], F32, kind="ExternalInput").ap()
    wv = nc.dram_tensor("wv", [128, NDC * 128], F32, kind="ExternalInput").ap()
    wo = nc.dram_tensor("wo", [NDC * 128, NDC * 128], F32, kind="ExternalInput").ap()
    wf = nc.dram_tensor("wf", [NDC * 128, NDC * 128], F32, kind="ExternalInput").ap()
    xres = nc.dram_tensor("xres", [2 * 128, NDC * TS], F32, kind="ExternalInput").ap()
    tabs = nc.dram_tensor("tabs", [4 * 128, T], BF16, kind="ExternalInput").ap()
    masks = nc.dram_tensor("masks", [128, 4 * TS], BF16, kind="ExternalInput").ap()
    rmat = nc.dram_tensor("rmat", [128, 128], BF16, kind="ExternalInput").ap()
    out = nc.dram_tensor("out", [2 * NDC * 128, TS], F32, kind="ExternalOutput").ap()
    a2a_srcE = nc.dram_tensor("a2a_srcE", [8 * 512, 512], BF16)
    a2a_srcO = nc.dram_tensor("a2a_srcO", [8 * 512, 512], BF16)
    a2a_dstE = nc.dram_tensor("a2a_dstE", [8 * 512, 512], BF16)
    a2a_dstO = nc.dram_tensor("a2a_dstO", [8 * 512, 512], BF16)

    dbg = {}
    if "kt" in debug:
        dbg["kt"] = nc.dram_tensor("dbg_kt", [128, B * T], BF16, kind="ExternalOutput").ap()
    if "v" in debug:
        dbg["v"] = nc.dram_tensor("dbg_v", [128, B * T], BF16, kind="ExternalOutput").ap()
    if "q" in debug:
        dbg["q"] = nc.dram_tensor("dbg_q", [128, HPC * TS], BF16, kind="ExternalOutput").ap()
    if "q0" in debug:
        dbg["q0"] = nc.dram_tensor("dbg_q0", [128, HPC * TS], BF16, kind="ExternalOutput").ap()
        dbg["negc0"] = nc.dram_tensor("dbg_negc0", [128, 4], F32, kind="ExternalOutput").ap()
        dbg["stats0"] = nc.dram_tensor("dbg_stats0", [128, 16], F32, kind="ExternalOutput").ap()
    if "osrc" in debug:
        dbg["osrc"] = nc.dram_tensor("dbg_osrc", [8 * 512, 1024], BF16, kind="ExternalOutput").ap()
    if "ogath" in debug:
        dbg["ogath"] = nc.dram_tensor("dbg_ogath", [8 * 512, 1024], BF16, kind="ExternalOutput").ap()
    if "z" in debug:
        dbg["z"] = nc.dram_tensor("dbg_z", [128, NDC * TS], F32, kind="ExternalOutput").ap()

    psSC = nc.alloc_psum_tensor("psSC", [128, 3072], F32)
    psF = nc.alloc_psum_tensor("psF", [128, 512], F32)
    psG = nc.alloc_psum_tensor("psG", [128, 512], F32)
    psA = psSC[:, 0:512]
    psB = psSC[:, 512:1024]

    S = lambda n: _Sem(nc, n)
    sxA, sxB = S("sxA"), S("sxB")
    stabA, stabB = S("stabA"), S("stabB")
    sw0 = S("sw0")
    sog = S("sog")
    sxr = S("sxr")
    swoA, swoB = S("swoA"), S("swoB")
    sao8 = [S(f"sao{i}") for i in range(8)]
    so = S("so")
    scc = S("scc")
    sg, sv, sa, sm = S("sg"), S("sv"), S("sa"), S("sm")

    W = _Waits()

    def edge(ename, eng, sem, inst):
        inst.then_inc(sem.h, 1)
        sem.v += 1
        eng.wait_ge(sem.h, sem.v)
        W.seen[(ename, sem.name)] = sem.v

    ones_bf = nc.alloc_sbuf_tensor("ones_bf", [128, 128], BF16)
    ones_f32 = nc.alloc_sbuf_tensor("ones_f32", [128, 128], F32)
    eps_col = nc.alloc_sbuf_tensor("eps_col", [128, 1], F32)

    stk1 = contextlib.ExitStack()
    sb = lambda n, s, d: stk1.enter_context(nc.sbuf_tensor(n, s, d))
    wq_bf = sb("wq_bf", [128, HPC * NDC * 128], BF16)
    wk_bf = sb("wk_bf", [128, NDC * 128], BF16)
    wv_bf = sb("wv_bf", [128, NDC * 128], BF16)
    kt_sb = sb("kt_sb", [128, B * T], BF16)
    v_sb = sb("v_sb", [128, B * T], BF16)
    xsA = sb("xsA", [128, 8 * TS], F32)
    xsB = sb("xsB", [128, 8 * TS], F32)
    xbf = sb("xbf", [128, NDC * TS], BF16)
    q_sb = sb("q_sb", [128, HPC * TS], BF16)
    ta_sb = sb("ta_sb", [128, TS], BF16)
    tb_sb = sb("tb_sb", [128, TS], BF16)
    sq_sb = sb("sq_sb", [128, TS], BF16)
    tab_sb = [sb(f"tab{p}", [128, 4 * TS], BF16) for p in range(2)]
    at_sb = [sb(f"at{p}", [128, 1024], BF16) for p in range(2)]
    ot_sb = [sb(f"ot{p}", [128, TS], BF16) for p in range(4)]
    linv_sb = sb("linv_sb", [128, TS], F32)
    masks_sb = sb("masks_sb", [128, 4 * TS], BF16)
    rmat_sb = sb("rmat_sb", [128, 128], BF16)
    qmax4 = sb("qmax4", [128, 4], F32)
    rmk = sb("rmk", [128, 1], F32)
    tmp1 = sb("tmp1", [128, 1], F32)
    prod4 = sb("prod4", [128, 4], F32)
    negc4 = sb("negc4", [128, 4], F32)
    stats_dbg = sb("stats_dbg", [128, 16], F32)

    with nc.Block() as block:
        pe = block.tensor
        act = block.scalar
        dve = block.vector
        gps = block.gpsimd
        syn = block.sync

        def _(g):
            g.memset(ones_bf[:], 1.0)
            g.memset(ones_f32[:], 1.0)
            g.memset(eps_col[:], 1e-5)
            g.memset(rmk[:], 0.0)
        gps(_)

        def _(s):
            s.dma_start(out=xsA[:], in_=wq[0:128, :]).then_inc(sw0.h, 16)
            s.dma_start(out=xsB[:], in_=wq[128:256, :]).then_inc(sw0.h, 16)
            s.dma_start(out=masks_sb[:], in_=masks).then_inc(sw0.h, 16)
            s.dma_start(out=rmat_sb[:], in_=rmat).then_inc(sw0.h, 16)
        syn(_)
        sw0.v += 64

        def _(g):
            W.w("gp", g, sw0, 64)
            g.tensor_copy(wq_bf[:, 0:4096], xsA[:])
            g.tensor_copy(wq_bf[:, 4096:8192], xsB[:]).then_inc(sg.h, 1)
        gps(_)
        sg.v += 1
        tok_wqcast = sg.v

        def _(s):
            W.w("sy", s, sg, tok_wqcast)
            s.dma_start(out=xsA[:], in_=wq[256:384, :]).then_inc(sw0.h, 16)
            s.dma_start(out=xsB[:], in_=wq[384:512, :]).then_inc(sw0.h, 16)
        syn(_)
        sw0.v += 32

        def _(g):
            W.w("gp", g, sw0, sw0.v)
            g.tensor_copy(wq_bf[:, 2 * 4096:3 * 4096], xsA[:])
            g.tensor_copy(wq_bf[:, 3 * 4096:4 * 4096], xsB[:]).then_inc(sg.h, 1)
        gps(_)
        sg.v += 1
        tok_wqcast2 = sg.v

        def _(s):
            W.w("sy", s, sg, tok_wqcast2)
            s.dma_start(out=xsA[:], in_=wk).then_inc(sw0.h, 16)
            s.dma_start(out=xsB[:], in_=wv).then_inc(sw0.h, 16)
        syn(_)
        sw0.v += 32

        def _(g):
            W.w("gp", g, sw0, sw0.v)
            g.tensor_copy(wk_bf[:], xsA[:])
            g.tensor_copy(wv_bf[:], xsB[:]).then_inc(sg.h, 1)
        gps(_)
        sg.v += 1
        tok_wkv = sg.v

        def _(s):
            W.w("sy", s, sg, tok_wkv)
            s.dma_start(out=xsA[:], in_=xt[0:128, :]).then_inc(sxA.h, 16)
            s.dma_start(out=xsB[:], in_=xt[128:256, :]).then_inc(sxB.h, 16)
            for i in range(4):
                s.dma_start(out=tab_sb[0][:, i * TS:(i + 1) * TS],
                            in_=tabs[i * 128:(i + 1) * 128, 0:TS]).then_inc(stabA.h, 16)
        syn(_)
        sxA.v += 16
        sxB.v += 16
        stabA.v += 64

        def _(v):
            W.w("dv", v, sxA, 16)
            v.tensor_copy(xbf[:, 0:4096], xsA[:]).then_inc(sv.h, 1)
            W.w("dv", v, sxB, 16)
            v.tensor_copy(xbf[:, 4096:8192], xsB[:]).then_inc(sv.h, 1)
        dve(_)
        sv.v += 2
        c01 = [sv.v - 1, sv.v]

        def _(s):
            W.w("sy", s, sv, c01[0])
            s.dma_start(out=xsA[:], in_=xt[256:384, :]).then_inc(sxA.h, 16)
            W.w("sy", s, sv, c01[1])
            s.dma_start(out=xsB[:], in_=xt[384:512, :]).then_inc(sxB.h, 16)
        syn(_)
        sxA.v += 16
        sxB.v += 16

        def _(v):
            W.w("dv", v, sxA, 32)
            v.tensor_copy(xbf[:, 8192:12288], xsA[:]).then_inc(sv.h, 1)
            W.w("dv", v, sxB, 32)
            v.tensor_copy(xbf[:, 12288:16384], xsB[:]).then_inc(sv.h, 1)
        dve(_)
        sv.v += 2
        tok_cast_next = sv.v
        xcast23 = [sv.v - 1, sv.v]

        xcast_tok = [0, 0]
        tok_xbf_consumed = 0
        tok_q_consumed = {}
        sc_hist = {0: 0, 1: 0, 2: 0}
        avl_hist = {0: 0, 1: 0}
        tok_ot_dve = 0
        tok_rope_add = 0
        tok_statsrd = 0
        tok_vevac = 0
        tok_ropemulA = 0
        tok_ropemulB = 0
        tok_tabread = [0, 0]
        tok_aosto = {}
        tok_lastexp = 0
        tok_rmk_read = 0
        tok_negc_h = {0: 0, 1: 0, 2: 0, 3: 0}

        for slab in range(NSLAB):
            if slab == NSLAB - 1:
                def _(g):
                    for i in range(4):
                        W.w("gp", g, sao8[i], 16 * 8)
                    with allow_ici_all_to_all():
                        g.collective_compute(
                            "AllToAll", ALU.bypass,
                            replica_groups=[[0, 1, 2, 3, 4, 5, 6, 7]],
                            ins=[a2a_srcE.ap().opt()],
                            outs=[a2a_dstE.ap().opt()],
                        ).then_inc(scc.h, 1)
                gps(_)
                scc.v += 1
            b, ts = divmod(slab, NTS)
            t0 = ts * TS
            tpar = slab % 2
            tabsem = stabA if tpar == 0 else stabB
            tab = tab_sb[tpar]
            tab_tok = 64 * (slab // 2 + 1)

            tok_cast = tok_cast_next

            if ts == 0 and b > 0:
                def _(g):
                    W.w("gp", g, sv, max(tok_statsrd, tok_rmk_read))
                    g.memset(rmk[:], 0.0).then_inc(sg.h, 1)
                gps(_)
                sg.v += 1
                tok_rmk0 = sg.v
            else:
                tok_rmk0 = 0

            def _(t):
                W.w("pe", t, sv, tok_cast)
                W.w("pe", t, sa, tok_lastexp)
                W.w("pe", t, sv, tok_ropemulA)
                for dch in range(NDC):
                    mi = t.matmul(psA, wk_bf[:, dch * 128:(dch + 1) * 128],
                                  xbf[:, dch * TS:(dch + 1) * TS],
                                  start=(dch == 0), stop=(dch == NDC - 1))
                mi.then_inc(sm.h, 1)
            pe(_)
            sm.v += 1
            tokKproj = sm.v

            def _(v, tab=tab, tab_tok=tab_tok, tabsem=tabsem, tokKproj=tokKproj):
                W.w("dv", v, tabsem, tab_tok)
                W.w("dv", v, sm, tokKproj)
                v.tensor_mul(ta_sb[:], psA, tab[:, 2 * TS:3 * TS])
                v.tensor_mul(tb_sb[:], psA, tab[:, 3 * TS:4 * TS]).then_inc(sv.h, 1)
            dve(_)
            sv.v += 1
            tok_ropemulA = sv.v

            def _(t):
                W.w("pe", t, sv, tok_ropemulA)
                W.w("pe", t, sv, max(tok_ot_dve, tok_rope_add))
                t.matmul(psF[:], rmat_sb[:], tb_sb[:], start=True, stop=True).then_inc(sm.h, 1)
            pe(_)
            sm.v += 1
            tokKrope = sm.v

            def _(v, b=b, t0=t0, tokKrope=tokKrope):
                W.w("dv", v, sm, tokKrope)
                base = b * T + t0
                edge("dv", v, sv, v.tensor_add(kt_sb[:, base:base + TS], psF[:], ta_sb[:]))
                v.tensor_mul(sq_sb[:], kt_sb[:, base:base + TS], kt_sb[:, base:base + TS]).then_inc(sv.h, 1)
            dve(_)
            sv.v += 1
            tok_rope_add = sv.v
            tokKsq = sv.v

            def _(t, tokKsq=tokKsq):
                W.w("pe", t, sv, tokKsq)
                W.w("pe", t, sv, max(tok_statsrd, tok_ot_dve))
                t.matmul(psG[:], ones_bf[:], sq_sb[:], start=True, stop=True).then_inc(sm.h, 1)
            pe(_)
            sm.v += 1
            tokKnorm = sm.v

            def _(v, tok_rmk0=tok_rmk0, tokKnorm=tokKnorm):
                W.w("dv", v, sm, tokKnorm)
                if tok_rmk0:
                    W.w("dv", v, sg, tok_rmk0)
                v.reduce_max(tmp1[:], psG[:], axis=AX.X).then_inc(sv.h, 1)
            dve(_)
            sv.v += 1
            tok_kred = sv.v

            def _(v, tok_kred=tok_kred):
                W.w("dv", v, sv, tok_kred)
                v.tensor_max(rmk[:], rmk[:], tmp1[:]).then_inc(sv.h, 1)
            dve(_)
            sv.v += 1
            tok_statsrd = sv.v

            def _(t):
                W.w("pe", t, sv, max(tok_vevac, tok_ropemulB))
                for tc in range(4):
                    for dch in range(NDC):
                        mi = t.matmul(psSC[:, 512 + tc * 128:512 + (tc + 1) * 128],
                                      xbf[:, dch * TS + tc * 128:dch * TS + (tc + 1) * 128],
                                      wv_bf[:, dch * 128:(dch + 1) * 128],
                                      start=(dch == 0), stop=(dch == NDC - 1))
                mi.then_inc(sm.h, 1)
            pe(_)
            sm.v += 1
            tokVproj = sm.v

            def _(v, b=b, ts=ts, tokVproj=tokVproj):
                W.w("dv", v, sm, tokVproj)
                base = (b * NDC + 4 * ts) * 128
                v.tensor_copy(v_sb[:, base:base + 512], psB).then_inc(sv.h, 1)
            dve(_)
            sv.v += 1
            tok_vevac = sv.v

            tok_qrope_mm = tokKrope
            for hh in range(HPC):
                pp = psA if hh % 2 == 0 else psB

                def _(t, hh=hh, pp=pp):
                    if hh % 2 == 0:
                        W.w("pe", t, sv, tok_ropemulA)
                    else:
                        W.w("pe", t, sv, max(tok_vevac, tok_ropemulB))
                    for dch in range(NDC):
                        mi = t.matmul(pp, wq_bf[:, (hh * NDC + dch) * 128:(hh * NDC + dch + 1) * 128],
                                      xbf[:, dch * TS:(dch + 1) * TS],
                                      start=(dch == 0), stop=(dch == NDC - 1))
                    mi.then_inc(sm.h, 1)
                pe(_)
                sm.v += 1
                tokQproj = sm.v
                if hh == HPC - 1:
                    tok_xbf_consumed = sm.v

                def _(v, pp=pp, tokQproj=tokQproj, tok_qrope_mm=tok_qrope_mm, tab=tab):
                    W.w("dv", v, sm, tokQproj)
                    W.w("dv", v, sm, tok_qrope_mm)
                    v.tensor_mul(ta_sb[:], pp, tab[:, 0:TS])
                    v.tensor_mul(tb_sb[:], pp, tab[:, TS:2 * TS]).then_inc(sv.h, 1)
                dve(_)
                sv.v += 1
                if hh % 2 == 0:
                    tok_ropemulA = sv.v
                else:
                    tok_ropemulB = sv.v
                tok_qmul = sv.v

                def _(t, tok_qmul=tok_qmul):
                    W.w("pe", t, sv, tok_qmul)
                    W.w("pe", t, sv, tok_rope_add)
                    t.matmul(psF[:], rmat_sb[:], tb_sb[:], start=True, stop=True).then_inc(sm.h, 1)
                pe(_)
                sm.v += 1
                tok_qrope_mm = sm.v

                def _(v, hh=hh, tok_qrope_mm=tok_qrope_mm):
                    W.w("dv", v, sm, tok_qrope_mm)
                    W.w("dv", v, sm, tok_q_consumed.get(hh, 0))
                    edge("dv", v, sv, v.tensor_add(q_sb[:, hh * TS:(hh + 1) * TS], psF[:], ta_sb[:]))
                    v.tensor_mul(sq_sb[:], q_sb[:, hh * TS:(hh + 1) * TS], q_sb[:, hh * TS:(hh + 1) * TS]).then_inc(sv.h, 1)
                dve(_)
                sv.v += 1
                tok_rope_add = sv.v
                tokQsq = sv.v

                def _(t, tokQsq=tokQsq):
                    W.w("pe", t, sv, tokQsq)
                    W.w("pe", t, sv, max(tok_statsrd, tok_ot_dve))
                    t.matmul(psG[:], ones_bf[:], sq_sb[:], start=True, stop=True).then_inc(sm.h, 1)
                pe(_)
                sm.v += 1
                tokQnorm = sm.v

                def _(v, hh=hh, tokQnorm=tokQnorm):
                    W.w("dv", v, sm, tokQnorm)
                    v.reduce_max(qmax4[:, hh:hh + 1], psG[:], axis=AX.X).then_inc(sv.h, 1)
                dve(_)
                sv.v += 1
                tok_statsrd = sv.v

                def _(v, hh=hh):
                    W.w("dv", v, sv, tok_statsrd)
                    W.w("dv", v, sa, tok_negc_h[hh])
                    v.tensor_scalar(prod4[:, hh:hh + 1], qmax4[:, hh:hh + 1],
                                    rmk[:, 0:1], None, ALU.mult).then_inc(sv.h, 1)
                dve(_)
                sv.v += 1
                tok_prod_h = sv.v
                tok_rmk_read = sv.v

                def _(sc, hh=hh, tok_prod_h=tok_prod_h):
                    W.w("ac", sc, sv, tok_prod_h)
                    pc = prod4[:, hh:hh + 1]
                    edge("ac", sc, sa, sc.activation(pc, pc, AF.Ln))
                    edge("ac", sc, sa, sc.activation(pc, pc, AF.Exp, scale=0.5))
                    sc.mul(negc4[:, hh:hh + 1], pc, -1.01).then_inc(sa.h, 1)
                act(_)
                sa.v += 1
                tok_negc_h[hh] = sa.v
            tok_tabread[tpar] = tok_rope_add


            if "q0" in dbg and slab == 0:
                def _(v):
                    W.w("dv", v, sa, sa.v)
                    v.tensor_copy(stats_dbg[:, 0:4], qmax4[:])
                    v.tensor_copy(stats_dbg[:, 4:5], rmk[:])
                    v.tensor_copy(stats_dbg[:, 5:9], prod4[:])
                    v.tensor_copy(stats_dbg[:, 9:10], tmp1[:]).then_inc(sv.h, 1)
                dve(_)
                sv.v += 1

                def _(s):
                    W.w("sy", s, sv, sv.v)
                    W.w("sy", s, sa, sa.v)
                    s.dma_start(out=dbg["q0"], in_=q_sb[:]).then_inc(so.h, 16)
                    s.dma_start(out=dbg["negc0"], in_=negc4[:]).then_inc(so.h, 16)
                    s.dma_start(out=dbg["stats0"], in_=stats_dbg[:]).then_inc(so.h, 16)
                syn(_)
                so.v += 48

            if slab + 1 < NSLAB:
                npar = (slab + 1) % 2
                ntabsem = stabA if npar == 0 else stabB

                def _(s, slab=slab, npar=npar, ntabsem=ntabsem):
                    W.w("sy", s, sv, xcast23[0])
                    s.dma_start(out=xsA[:], in_=xt[(4 * slab + 4) * 128:(4 * slab + 5) * 128, :]).then_inc(sxA.h, 16)
                    W.w("sy", s, sv, xcast23[1])
                    s.dma_start(out=xsB[:], in_=xt[(4 * slab + 5) * 128:(4 * slab + 6) * 128, :]).then_inc(sxB.h, 16)
                    W.w("sy", s, sv, tok_tabread[npar])
                    nt0 = ((slab + 1) % NTS) * TS
                    for i in range(4):
                        s.dma_start(out=tab_sb[npar][:, i * TS:(i + 1) * TS],
                                    in_=tabs[i * 128:(i + 1) * 128, nt0:nt0 + TS]).then_inc(ntabsem.h, 16)
                syn(_)
                sxA.v += 16
                sxB.v += 16
                ntabsem.v += 64

                def _(v, slab=slab):
                    W.w("dv", v, sxA, 16 * (2 * slab + 3))
                    W.w("dv", v, sm, tok_xbf_consumed)
                    v.tensor_copy(xbf[:, 0:4096], xsA[:]).then_inc(sv.h, 1)
                    W.w("dv", v, sxB, 16 * (2 * slab + 3))
                    v.tensor_copy(xbf[:, 4096:8192], xsB[:]).then_inc(sv.h, 1)
                dve(_)
                sv.v += 2
                nc01 = [sv.v - 1, sv.v]

                def _(s, slab=slab, nc01=nc01):
                    W.w("sy", s, sv, nc01[0])
                    s.dma_start(out=xsA[:], in_=xt[(4 * slab + 6) * 128:(4 * slab + 7) * 128, :]).then_inc(sxA.h, 16)
                    W.w("sy", s, sv, nc01[1])
                    s.dma_start(out=xsB[:], in_=xt[(4 * slab + 7) * 128:(4 * slab + 8) * 128, :]).then_inc(sxB.h, 16)
                syn(_)
                sxA.v += 16
                sxB.v += 16

                def _(v, slab=slab):
                    W.w("dv", v, sxA, 16 * (2 * slab + 4))
                    v.tensor_copy(xbf[:, 8192:12288], xsA[:]).then_inc(sv.h, 1)
                    W.w("dv", v, sxB, 16 * (2 * slab + 4))
                    v.tensor_copy(xbf[:, 12288:16384], xsB[:]).then_inc(sv.h, 1)
                dve(_)
                sv.v += 2
                tok_cast_next = sv.v
                xcast23 = [sv.v - 1, sv.v]

            nsc = 4 * (ts + 1)
            npairs = nsc // 2

            sa_p, sm_p, sg_p, sv_p = sa.v, sm.v, sg.v, sv.v
            p_scores, p_exp, p_mask, p_avl, p_ot = {}, {}, {}, {}, {}
            for hh in range(HPC):
                sm_p += 1
                p_scores[(hh, 0)] = sm_p
                if npairs > 1:
                    sm_p += 1
                    p_scores[(hh, 1)] = sm_p
                for p in range(npairs):
                    sa_p += 1
                    p_exp[(hh, p)] = sa_p
                    if 2 * p >= nsc - 4:
                        sg_p += 1
                        p_mask[(hh, p)] = sg_p
                    if p + 2 < npairs:
                        sm_p += 1
                        p_scores[(hh, p + 2)] = sm_p
                    sm_p += 1
                    p_avl[(hh, p)] = sm_p
                sv_p += 2
                p_ot[hh] = sv_p

            def gbuf(hh, p):
                return (hh * npairs + p) % 3

            def scwar(hh, p):
                g = hh * npairs + p - 3
                if g >= 0:
                    return p_exp[(g // npairs, g % npairs)]
                return sc_hist[gbuf(hh, p)]

            def avlwar(hh, p):
                g = hh * npairs + p - 2
                if g >= 0:
                    return p_avl[(g // npairs, g % npairs)]
                return avl_hist[p % 2]

            def scslice(hh, p):
                k = gbuf(hh, p)
                return psSC[:, k * 1024:(k + 1) * 1024]

            def _(sc, npairs=npairs):
                for hh in range(HPC):
                    W.w("ac", sc, sa, tok_negc_h[hh])
                    for p in range(npairs):
                        atb = at_sb[p % 2]
                        W.w("ac", sc, sm, p_scores[(hh, p)])
                        W.w("ac", sc, sm, avlwar(hh, p))
                        sc.activation(atb[:], scslice(hh, p), AF.Exp,
                                      bias=negc4[:, hh:hh + 1], scale=1.0).then_inc(sa.h, 1)
                        sa.v += 1
            act(_)

            def _(t, b=b, npairs=npairs, nsc=nsc):
                first_use = set()

                def emit_scores(hh, p):
                    k = gbuf(hh, p)
                    W.w("pe", t, sa, scwar(hh, p))
                    if k in (0, 1) and k not in first_use:
                        W.w("pe", t, sv, max(tok_ropemulA, tok_ropemulB))
                    first_use.add(k)
                    sl = scslice(hh, p)
                    for half in range(2):
                        s0 = b * T + (2 * p + half) * 128
                        mi = t.matmul(sl[:, half * 512:(half + 1) * 512],
                                      kt_sb[:, s0:s0 + 128],
                                      q_sb[:, hh * TS:(hh + 1) * TS], start=True, stop=True)
                    mi.then_inc(sm.h, 1)
                    sm.v += 1
                    assert sm.v == p_scores[(hh, p)]

                for hh in range(HPC):
                    emit_scores(hh, 0)
                    if npairs > 1:
                        emit_scores(hh, 1)
                    for p in range(npairs):
                        atb = at_sb[p % 2]
                        diag = (2 * p >= nsc - 4)
                        if p + 2 < npairs:
                            emit_scores(hh, p + 2)
                        if diag:
                            W.w("pe", t, sg, p_mask[(hh, p)])
                        else:
                            W.w("pe", t, sa, p_exp[(hh, p)])
                        if p == 0:
                            W.w("pe", t, sv, p_ot[hh - 1] if hh > 0
                                else max(tok_ot_dve, tok_rope_add, tok_statsrd))
                        for half in range(2):
                            sc_i = 2 * p + half
                            vbase = (b * NDC + sc_i) * 128
                            last = (sc_i == nsc - 1)
                            t.matmul(psF[:], v_sb[:, vbase:vbase + 128],
                                     atb[:, half * 512:(half + 1) * 512],
                                     start=(sc_i == 0), stop=last)
                            mi = t.matmul(psG[:], ones_bf[:],
                                          atb[:, half * 512:(half + 1) * 512],
                                          start=(sc_i == 0), stop=last)
                        mi.then_inc(sm.h, 1)
                        sm.v += 1
                        assert sm.v == p_avl[(hh, p)]
            pe(_)

            def _(g, npairs=npairs, nsc=nsc):
                for hh in range(HPC):
                    for p in range(npairs):
                        if (hh, p) not in p_mask:
                            continue
                        atb = at_sb[p % 2]
                        W.w("gp", g, sa, p_exp[(hh, p)])
                        for half in range(2):
                            mii = 2 * p + half - (nsc - 4)
                            gi = g.tensor_mul(atb[:, half * 512:(half + 1) * 512],
                                              atb[:, half * 512:(half + 1) * 512],
                                              masks_sb[:, mii * TS:(mii + 1) * TS])
                        gi.then_inc(sg.h, 1)
                        sg.v += 1
            gps(_)

            def _(v, npairs=npairs):
                for hh in range(HPC):
                    osem = sao8[(1 - ts % 2) * 4 + hh]
                    W.w("dv", v, sm, p_avl[(hh, npairs - 1)])
                    W.w("dv", v, osem, osem.v)
                    edge("dv", v, sv, v.reciprocal(linv_sb[:], psG[:]))
                    v.tensor_mul(ot_sb[hh][:], psF[:], linv_sb[:]).then_inc(sv.h, 1)
                    sv.v += 1
                    assert sv.v == p_ot[hh]
            dve(_)

            def _(s, b=b, ts=ts):
                buf = a2a_srcE if ts % 2 == 0 else a2a_srcO
                for hh in range(HPC):
                    osem = sao8[(ts % 2) * 4 + hh]
                    W.w("sy", s, sv, p_ot[hh])
                    blk = b * 4 + ts // 2
                    r0 = blk * 512 + hh * 128
                    s.dma_start(out=buf.ap()[r0:r0 + 128, :],
                                in_=ot_sb[hh][:]).then_inc(osem.h, 16)
                    osem.v += 16
            syn(_)

            assert (sa.v, sm.v, sg.v, sv.v) == (sa_p, sm_p, sg_p, sv_p), (slab, (sa.v, sm.v, sg.v, sv.v), (sa_p, sm_p, sg_p, sv_p))
            ntot = HPC * npairs
            for k in range(3):
                g = ntot - 1 - ((ntot - 1 - k) % 3)
                sc_hist[k] = p_exp[(g // npairs, g % npairs)]
            avl_hist[0] = p_avl[(HPC - 1, npairs - 2)]
            avl_hist[1] = p_avl[(HPC - 1, npairs - 1)]
            tok_lastexp = p_exp[(HPC - 1, npairs - 1)]
            tok_ot_dve = p_ot[HPC - 1]
            for hh in range(HPC):
                tok_q_consumed[hh] = p_avl[(hh, npairs - 1)]

        if "kt" in dbg or "v" in dbg or "q" in dbg:
            def _(sc):
                W.w("ac", sc, sv, sv.v)
                W.w("ac", sc, sa, sa.v - 0)
                if "kt" in dbg:
                    sc.dma_start(out=dbg["kt"], in_=kt_sb[:]).then_inc(so.h, 16); so.v += 16
                if "v" in dbg:
                    sc.dma_start(out=dbg["v"], in_=v_sb[:]).then_inc(so.h, 16); so.v += 16
                if "q" in dbg:
                    sc.dma_start(out=dbg["q"], in_=q_sb[:]).then_inc(so.h, 16); so.v += 16
            act(_)

        def _(g):
            for i in range(4):
                W.w("gp", g, sao8[4 + i], 16 * 8)
            with allow_ici_all_to_all():
                g.collective_compute(
                    "AllToAll", ALU.bypass,
                    replica_groups=[[0, 1, 2, 3, 4, 5, 6, 7]],
                    ins=[a2a_srcO.ap().opt()],
                    outs=[a2a_dstO.ap().opt()],
                ).then_inc(scc.h, 1)
        gps(_)
        scc.v += 1

        if "osrc" in dbg or "ogath" in dbg:
            def _(g):
                W.w("gp", g, scc, 2)
                if "osrc" in dbg:
                    g.dma_start(out=dbg["osrc"].rearrange("r (h c) -> r h c", h=2)[:, 0, :],
                                in_=a2a_srcE.ap()).then_inc(so.h, 16); so.v += 16
                    g.dma_start(out=dbg["osrc"].rearrange("r (h c) -> r h c", h=2)[:, 1, :],
                                in_=a2a_srcO.ap()).then_inc(so.h, 16); so.v += 16
                if "ogath" in dbg:
                    g.dma_start(out=dbg["ogath"].rearrange("r (h c) -> r h c", h=2)[:, 0, :],
                                in_=a2a_dstE.ap()).then_inc(so.h, 16); so.v += 16
                    g.dma_start(out=dbg["ogath"].rearrange("r (h c) -> r h c", h=2)[:, 1, :],
                                in_=a2a_dstO.ap()).then_inc(so.h, 16); so.v += 16
            gps(_)

        stk1.close()
        og_sb = nc.alloc_sbuf_tensor("og_sb", [128, NDC * TS], BF16)
        z_sb = nc.alloc_sbuf_tensor("z_sb", [128, NDC * TS], F32)
        yz_bf = nc.alloc_sbuf_tensor("yz_bf", [128, NDC * TS], BF16)
        wst = [nc.alloc_sbuf_tensor(f"wst{p}", [128, 4096], F32) for p in range(2)]
        wbf = [nc.alloc_sbuf_tensor(f"wbf{p}", [128, 4096], BF16) for p in range(4)]
        sqs = [nc.alloc_sbuf_tensor(f"sqs{p}", [128, TS], BF16) for p in range(2)]
        m_sb = nc.alloc_sbuf_tensor("m_sb", [128, TS], F32)
        m2_sb = nc.alloc_sbuf_tensor("m2_sb", [128, TS], F32)
        ve_sb = nc.alloc_sbuf_tensor("ve_sb", [128, TS], F32)
        rstd_sb = nc.alloc_sbuf_tensor("rstd_sb", [128, TS], F32)

        tok_wcast = {0: 0, 1: 0, 2: 0, 3: 0}
        tok_wmm = {0: 0, 1: 0, 2: 0, 3: 0}
        tok_wld = {0: 0, 1: 0}
        tok_wstfree = {0: 0, 1: 0}
        tok_zadd = {0: 0, 1: 0}
        tok_ybf = {}
        tok_sq = {0: 0, 1: 0}
        nonlocal_holder = [0]
        tok_store_d = {}
        tok_zout = 0
        tok_m_prev = 0
        tok_ve_prev = 0
        tok_lnapply_last = 0

        def emit_wload(jch, wsrc):
            p2 = jch % 2
            p4 = jch % 4
            wsem = swoA if p2 == 0 else swoB

            def _(s, jch=jch, p2=p2, wsrc=wsrc, wsem=wsem):
                W.w("sy", s, sv, tok_wstfree[p2])
                s.dma_start(out=wst[p2][:],
                            in_=wsrc[jch * 128:(jch + 1) * 128, :]).then_inc(wsem.h, 16)
            syn(_)
            wsem.v += 16
            ld_tok = wsem.v

            def _(v, p2=p2, p4=p4, ld_tok=ld_tok, wsem=wsem):
                W.w("dv", v, wsem, ld_tok)
                W.w("dv", v, sm, tok_wmm[p4])
                v.tensor_copy(wbf[p4][:], wst[p2][:]).then_inc(sv.h, 1)
            dve(_)
            sv.v += 1
            tok_wcast[p4] = sv.v
            tok_wstfree[p2] = sv.v

        def emit_stats(jch, phase):
            par = jch % 2

            def _(t, jch=jch, par=par, phase=phase):
                if jch == 0:
                    W.w("pe", t, sa, tok_m_prev)
                    W.w("pe", t, sv, tok_ve_prev)
                W.w("pe", t, sv, tok_zadd_d[jch])
                t.matmul(psSC[:, 1024:1536], ones_f32[:],
                         z_sb[:, jch * TS:(jch + 1) * TS],
                         start=(jch == 0), stop=(jch == NDC - 1))
                W.w("pe", t, sv, tok_sqw_d[jch])
                t.matmul(psSC[:, 1536:2048], ones_bf[:], sqs[par][:],
                         start=(jch == 0), stop=(jch == NDC - 1)).then_inc(sm.h, 1)
            pe(_)
            sm.v += 1
            tok_sq[par] = sm.v

        emit_wload(0, wo)
        emit_wload(1, wo)

        for u in range(2):
            def _(s, u=u):
                W.w("sy", s, scc, 1 + u)
                W.w("sy", s, sm, nonlocal_holder[0])
                dst = a2a_dstE if u == 0 else a2a_dstO
                for ech in range(NDC):
                    s.dma_start(out=og_sb[:, ech * TS:(ech + 1) * TS],
                                in_=dst.ap()[ech * 128:(ech + 1) * 128, :]
                                ).then_inc(sog.h, 16)
            syn(_)
            sog.v += 16 * NDC
            tok_ogld = sog.v

            if u == 0:
                def _(s, u=u):
                    s.dma_start(out=z_sb[:], in_=xres[u * 128:(u + 1) * 128, :]).then_inc(sxr.h, 16)
                syn(_)
                sxr.v += 16
                tok_xr_d = {jch: sxr.v for jch in range(NDC)}
            else:
                tok_xr_d = {}

                def _(s, u=u):
                    for jch in range(NDC):
                        W.w("sy", s, so, tok_store_d[jch])
                        s.dma_start(out=z_sb[:, jch * TS:(jch + 1) * TS],
                                    in_=xres[u * 128:(u + 1) * 128, jch * TS:(jch + 1) * TS]
                                    ).then_inc(sxr.h, 16)
                        sxr.v += 16
                        tok_xr_d[jch] = sxr.v
                syn(_)

            for phase in range(2):
                wsrc = wo if phase == 0 else wf
                tok_zadd_d = {}
                tok_sqw_d = {}
                for jch in range(NDC):
                    par = jch % 2
                    p4 = jch % 4
                    nxt = jch + 2
                    if nxt < NDC:
                        emit_wload(nxt, wsrc)
                    elif phase == 0:
                        emit_wload(nxt - NDC, wf)
                    elif u == 0:
                        emit_wload(nxt - NDC, wo)

                    def _(t, jch=jch, par=par, p4=p4, phase=phase, tok_ogld=tok_ogld):
                        W.w("pe", t, sv, tok_wcast[p4])
                        W.w("pe", t, sa, tok_lastexp)
                        if phase == 0:
                            W.w("pe", t, sog, tok_ogld)
                        elif jch > 0:
                            W.w("pe", t, sg, tok_ybf[NDC - 1])
                        W.w("pe", t, sv, tok_zadd[par])
                        pd = psSC[:, par * 512:(par + 1) * 512]
                        rhs = og_sb if phase == 0 else yz_bf
                        for ech in range(NDC):
                            if phase == 1 and jch == 0:
                                W.w("pe", t, sg, tok_ybf[ech])
                            mi = t.matmul(pd, wbf[p4][:, ech * 128:(ech + 1) * 128],
                                          rhs[:, ech * TS:(ech + 1) * TS],
                                          start=(ech == 0), stop=(ech == NDC - 1))
                        mi.then_inc(sm.h, 1)
                    pe(_)
                    sm.v += 1
                    tok_wmm[p4] = sm.v
                    if phase == 0:
                        nonlocal_holder[0] = max(nonlocal_holder[0], sm.v)
                    tok_proj = sm.v

                    def _(v, jch=jch, par=par, tok_proj=tok_proj):
                        W.w("dv", v, sm, tok_proj)
                        W.w("dv", v, sxr, tok_xr_d[jch])
                        pd = psSC[:, par * 512:(par + 1) * 512]
                        zc = z_sb[:, jch * TS:(jch + 1) * TS]
                        v.tensor_add(zc, pd, zc).then_inc(sv.h, 1)
                    dve(_)
                    sv.v += 1
                    tok_zadd[par] = sv.v
                    tok_zadd_d[jch] = sv.v

                    def _(v, jch=jch, par=par):
                        W.w("dv", v, sm, tok_sq[par])
                        v.tensor_mul(sqs[par][:], z_sb[:, jch * TS:(jch + 1) * TS],
                                     z_sb[:, jch * TS:(jch + 1) * TS]).then_inc(sv.h, 1)
                    dve(_)
                    sv.v += 1
                    tok_sqw_d[jch] = sv.v

                    if jch > 0:
                        emit_stats(jch - 1, phase)
                emit_stats(NDC - 1, phase)
                tok_stats = sm.v

                def _(sc, tok_stats=tok_stats):
                    W.w("ac", sc, sm, tok_stats)
                    W.w("ac", sc, sv, tok_lnapply_last)
                    sc.mul(m_sb[:], psSC[:, 1024:1536], 1.0 / D).then_inc(sa.h, 1)
                act(_)
                sa.v += 1
                tok_m = sa.v
                tok_m_prev = tok_m

                def _(v, tok_m=tok_m):
                    W.w("dv", v, sa, tok_m)
                    edge("dv", v, sv, v.tensor_mul(m2_sb[:], m_sb[:], m_sb[:]))
                    v.scalar_tensor_tensor(ve_sb[:], psSC[:, 1536:2048], 1.0 / D,
                                           m2_sb[:], ALU.mult, ALU.subtract).then_inc(sv.h, 1)
                dve(_)
                sv.v += 1
                tok_ve = sv.v
                tok_ve_prev = tok_ve

                def _(sc, tok_ve=tok_ve):
                    W.w("ac", sc, sv, tok_ve)
                    edge("ac", sc, sa, sc.activation(ve_sb[:], ve_sb[:], AF.Ln, bias=eps_col[:, 0:1]))
                    sc.activation(rstd_sb[:], ve_sb[:], AF.Exp, scale=-0.5).then_inc(sa.h, 1)
                act(_)
                sa.v += 1
                tok_rstd = sa.v

                tok_y_d = {}

                def _(v, tok_rstd=tok_rstd):
                    W.w("dv", v, sa, tok_rstd)
                    for jch in range(NDC):
                        zc = z_sb[:, jch * TS:(jch + 1) * TS]
                        edge("dv", v, sv, v.tensor_sub(zc, zc, m_sb[:]))
                        v.tensor_mul(zc, zc, rstd_sb[:]).then_inc(sv.h, 1)
                        sv.v += 1
                        tok_y_d[jch] = sv.v
                dve(_)
                tok_lnapply_last = sv.v

                if phase == 1:
                    def _(g, u=u):
                        for jch in range(NDC):
                            W.w("gp", g, sv, tok_y_d[jch])
                            r0 = (u * NDC + jch) * 128
                            g.dma_start(out=out[r0:r0 + 128, :],
                                        in_=z_sb[:, jch * TS:(jch + 1) * TS]).then_inc(so.h, 16)
                            so.v += 16
                            tok_store_d[jch] = so.v
                    gps(_)

                if phase == 0:
                    def _(g):
                        W.w("gp", g, sm, tok_stats)
                        for jch in range(NDC):
                            W.w("gp", g, sv, tok_y_d[jch])
                            g.tensor_copy(yz_bf[:, jch * TS:(jch + 1) * TS],
                                          z_sb[:, jch * TS:(jch + 1) * TS]).then_inc(sg.h, 1)
                            sg.v += 1
                            tok_ybf[jch] = sg.v
                    gps(_)
                else:
                    tok_zout = so.v

            if "z" in dbg and u == 0:
                def _(sc):
                    W.w("ac", sc, sv, sv.v)
                    sc.dma_start(out=dbg["z"], in_=z_sb[:]).then_inc(so.h, 16); so.v += 16
                act(_)

        final_so = so.v

        def _(sc):
            W.w("ac", sc, so, final_so)
        act(_)

    return nc



def _stage_shared(X, Wq, Wk, Wv, Wo, Wf):
    sh = {}
    xt = np.empty((NSLAB, 4, 128, 8, TS), np.float32)
    for b in range(B):
        Xb = np.asarray(X[b])
        for ts in range(NTS):
            sl = Xb[ts * TS:(ts + 1) * TS, :]
            blk = sl.T.reshape(NDC, 128, TS)
            for qq in range(4):
                xt[b * NTS + ts, qq] = blk[8 * qq:8 * qq + 8].transpose(1, 0, 2)
    sh["xt"] = np.ascontiguousarray(xt.reshape(NSLAB * 4 * 128, 8 * TS))

    def wblock(Wm):
        out = np.empty((NDC, 128, NDC, 128), np.float32)
        for jch in range(NDC):
            blk = np.asarray(Wm[:, jch * 128:(jch + 1) * 128]).reshape(NDC, 128, 128)
            out[jch] = blk.transpose(1, 0, 2)
        return np.ascontiguousarray(out.reshape(NDC * 128, NDC * 128))

    sh["wo"] = wblock(Wo)
    sh["wf"] = wblock(Wf)

    pos = np.arange(T, dtype=np.float64)
    inv = 10000.0 ** (-np.arange(64, dtype=np.float64) / 64.0)
    ang = np.outer(pos, inv)
    c64 = np.cos(ang).T.astype(np.float32)
    s64 = np.sin(ang).T.astype(np.float32)
    ct = np.concatenate([c64, c64], 0)
    st = np.concatenate([s64, -s64], 0)
    scl = np.float32(1.0 / np.sqrt(DH))
    sh["tabs"] = np.ascontiguousarray(np.concatenate([ct * scl, st * scl, ct, st], 0)).astype(ml_dtypes.bfloat16)

    mk = np.zeros((128, 4 * TS), np.float32)
    pp = np.arange(128)[:, None]
    ff = np.arange(TS)[None, :]
    for i in range(4):
        mk[:, i * TS:(i + 1) * TS] = (pp + 128 * i <= ff)
    sh["masks"] = mk.astype(ml_dtypes.bfloat16)

    R = np.zeros((128, 128), np.float32)
    R[np.arange(128), (np.arange(128) + 64) % 128] = 1.0
    sh["rmat"] = R.astype(ml_dtypes.bfloat16)
    return sh


def _stage_core(c, X, Wq, Wk, Wv):
    m = {}
    wqc = np.empty((HPC, 128, NDC, 128), np.float32)
    for hh in range(HPC):
        col0 = 512 * c + 128 * hh
        wqc[hh] = np.asarray(Wq[:, col0:col0 + 128]).reshape(NDC, 128, 128).transpose(1, 0, 2)
    m["wq"] = np.ascontiguousarray(wqc.reshape(HPC * 128, NDC * 128))
    m["wk"] = np.ascontiguousarray(
        np.asarray(Wk[:, 128 * c:128 * (c + 1)]).reshape(NDC, 128, 128).transpose(1, 0, 2).reshape(128, NDC * 128))
    m["wv"] = np.ascontiguousarray(
        np.asarray(Wv[:, 128 * c:128 * (c + 1)]).reshape(NDC, 128, 128).transpose(1, 0, 2).reshape(128, NDC * 128))
    b_o, q_o = c // 4, c % 4
    xr = np.empty((2, 128, NDC, TS), np.float32)
    Xb = np.asarray(X[b_o])
    for u in range(2):
        t0 = 1024 * q_o + 512 * u
        sl = Xb[t0:t0 + TS, :]
        xr[u] = sl.T.reshape(NDC, 128, TS).transpose(1, 0, 2)
    m["xres"] = np.ascontiguousarray(xr.reshape(2 * 128, NDC * TS))
    return m


def kernel(L0_attn_X, L0_attn_Wq, L0_attn_Wk, L0_attn_Wv, L0_attn_Wo, L0_ffn_weight):
    X = np.asarray(L0_attn_X, np.float32)
    sh = _stage_shared(X, L0_attn_Wq, L0_attn_Wk, L0_attn_Wv, L0_attn_Wo, L0_ffn_weight)
    in_maps = []
    for c in range(8):
        m = dict(sh)
        m.update(_stage_core(c, X, L0_attn_Wq, L0_attn_Wk, L0_attn_Wv))
        in_maps.append(m)

    nc = build_program(debug=_DEBUG)
    res = run_bass_kernel_spmd(nc, in_maps, core_ids=list(range(8)))

    outp = np.empty((B, T, D), np.float32)
    for c in range(8):
        b_o, q_o = c // 4, c % 4
        arr = np.asarray(res.results[c]["out"]).reshape(2, NDC, 128, TS)
        yt = arr.transpose(1, 2, 0, 3).reshape(D, 1024)
        outp[b_o, 1024 * q_o:1024 * (q_o + 1), :] = yt.T
    return outp


# version 18
# speedup vs baseline: 1.1252x; 1.0268x over previous
import contextlib
import numpy as np
import ml_dtypes
import concourse.bass as bass
import concourse.mybir as mybir
from concourse.bass_utils import run_bass_kernel_spmd
from concourse.aot_env import allow_ici_all_to_all

F32 = mybir.dt.float32
BF16 = mybir.dt.bfloat16
AF = mybir.ActivationFunctionType
ALU = mybir.AluOpType
AX = mybir.AxisListType

B, T, D, DH = 2, 4096, 4096, 128
NTS = 8
TS = 512
NDC = D // 128
HPC = 4
NSLAB = B * NTS

_DEBUG = ()


class _Sem:
    __slots__ = ("h", "v", "name")

    def __init__(self, nc, name):
        self.h = nc.alloc_semaphore(name)
        self.v = 0
        self.name = name


class _Waits:

    def __init__(self):
        self.seen = {}

    def w(self, ename, eng, sem, tok):
        if tok <= 0:
            return
        k = (ename, sem.name)
        if self.seen.get(k, 0) >= tok:
            return
        eng.wait_ge(sem.h, tok)
        self.seen[k] = tok


def build_program(debug=()):
    nc = bass.Bass("TRN2", target_bir_lowering=False, debug=False)

    xt = nc.dram_tensor("xt", [NSLAB * 4 * 128, 8 * TS], F32, kind="ExternalInput").ap()
    wq = nc.dram_tensor("wq", [HPC * 128, NDC * 128], F32, kind="ExternalInput").ap()
    wk = nc.dram_tensor("wk", [128, NDC * 128], F32, kind="ExternalInput").ap()
    wv = nc.dram_tensor("wv", [128, NDC * 128], F32, kind="ExternalInput").ap()
    wo = nc.dram_tensor("wo", [NDC * 128, NDC * 128], F32, kind="ExternalInput").ap()
    wf = nc.dram_tensor("wf", [NDC * 128, NDC * 128], F32, kind="ExternalInput").ap()
    xres = nc.dram_tensor("xres", [2 * 128, NDC * TS], F32, kind="ExternalInput").ap()
    tabs = nc.dram_tensor("tabs", [4 * 128, T], BF16, kind="ExternalInput").ap()
    masks = nc.dram_tensor("masks", [128, 4 * TS], BF16, kind="ExternalInput").ap()
    rmat = nc.dram_tensor("rmat", [128, 128], BF16, kind="ExternalInput").ap()
    out = nc.dram_tensor("out", [2 * NDC * 128, TS], F32, kind="ExternalOutput").ap()
    a2a_srcE = nc.dram_tensor("a2a_srcE", [8 * 512, 512], BF16)
    a2a_srcO = nc.dram_tensor("a2a_srcO", [8 * 512, 512], BF16)
    a2a_dstE = nc.dram_tensor("a2a_dstE", [8 * 512, 512], BF16)
    a2a_dstO = nc.dram_tensor("a2a_dstO", [8 * 512, 512], BF16)

    dbg = {}
    if "kt" in debug:
        dbg["kt"] = nc.dram_tensor("dbg_kt", [128, B * T], BF16, kind="ExternalOutput").ap()
    if "v" in debug:
        dbg["v"] = nc.dram_tensor("dbg_v", [128, B * T], BF16, kind="ExternalOutput").ap()
    if "q" in debug:
        dbg["q"] = nc.dram_tensor("dbg_q", [128, HPC * TS], BF16, kind="ExternalOutput").ap()
    if "q0" in debug:
        dbg["q0"] = nc.dram_tensor("dbg_q0", [128, HPC * TS], BF16, kind="ExternalOutput").ap()
        dbg["negc0"] = nc.dram_tensor("dbg_negc0", [128, 4], F32, kind="ExternalOutput").ap()
        dbg["stats0"] = nc.dram_tensor("dbg_stats0", [128, 16], F32, kind="ExternalOutput").ap()
    if "osrc" in debug:
        dbg["osrc"] = nc.dram_tensor("dbg_osrc", [8 * 512, 1024], BF16, kind="ExternalOutput").ap()
    if "ogath" in debug:
        dbg["ogath"] = nc.dram_tensor("dbg_ogath", [8 * 512, 1024], BF16, kind="ExternalOutput").ap()
    if "z" in debug:
        dbg["z"] = nc.dram_tensor("dbg_z", [128, NDC * TS], F32, kind="ExternalOutput").ap()

    psSC = nc.alloc_psum_tensor("psSC", [128, 3072], F32)
    psF = nc.alloc_psum_tensor("psF", [128, 512], F32)
    psG = nc.alloc_psum_tensor("psG", [128, 512], F32)
    psA = psSC[:, 0:512]
    psB = psSC[:, 512:1024]

    S = lambda n: _Sem(nc, n)
    sxA, sxB = S("sxA"), S("sxB")
    stabA, stabB = S("stabA"), S("stabB")
    sw0 = S("sw0")
    sog = S("sog")
    sxr = S("sxr")
    swoA, swoB = S("swoA"), S("swoB")
    sao8 = [S(f"sao{i}") for i in range(8)]
    so = S("so")
    scc = S("scc")
    sg, sv, sa, sm = S("sg"), S("sv"), S("sa"), S("sm")

    W = _Waits()

    def edge(ename, eng, sem, inst):
        inst.then_inc(sem.h, 1)
        sem.v += 1
        eng.wait_ge(sem.h, sem.v)
        W.seen[(ename, sem.name)] = sem.v

    ones_bf = nc.alloc_sbuf_tensor("ones_bf", [128, 128], BF16)
    ones_f32 = nc.alloc_sbuf_tensor("ones_f32", [128, 128], F32)
    eps_col = nc.alloc_sbuf_tensor("eps_col", [128, 1], F32)

    stk1 = contextlib.ExitStack()
    sb = lambda n, s, d: stk1.enter_context(nc.sbuf_tensor(n, s, d))
    wq_bf = sb("wq_bf", [128, HPC * NDC * 128], BF16)
    wk_bf = sb("wk_bf", [128, NDC * 128], BF16)
    wv_bf = sb("wv_bf", [128, NDC * 128], BF16)
    kt_sb = sb("kt_sb", [128, B * T], BF16)
    v_sb = sb("v_sb", [128, B * T], BF16)
    xsA = sb("xsA", [128, 8 * TS], F32)
    xsB = sb("xsB", [128, 8 * TS], F32)
    xbf = sb("xbf", [128, NDC * TS], BF16)
    q_sb = sb("q_sb", [128, HPC * TS], BF16)
    ta_sb = sb("ta_sb", [128, TS], BF16)
    tb_sb = sb("tb_sb", [128, TS], BF16)
    sq_sb = sb("sq_sb", [128, TS], BF16)
    tab_sb = [sb(f"tab{p}", [128, 4 * TS], BF16) for p in range(2)]
    at_sb = [sb(f"at{p}", [128, 1024], BF16) for p in range(2)]
    ot_sb = [sb(f"ot{p}", [128, TS], BF16) for p in range(4)]
    linv_sb = sb("linv_sb", [128, TS], F32)
    masks_sb = sb("masks_sb", [128, 4 * TS], BF16)
    rmat_sb = sb("rmat_sb", [128, 128], BF16)
    qmax4 = sb("qmax4", [128, 4], F32)
    rmk = sb("rmk", [128, 1], F32)
    tmp1 = sb("tmp1", [128, 1], F32)
    prod4 = sb("prod4", [128, 4], F32)
    negc4 = sb("negc4", [128, 4], F32)
    stats_dbg = sb("stats_dbg", [128, 16], F32)

    with nc.Block() as block:
        pe = block.tensor
        act = block.scalar
        dve = block.vector
        gps = block.gpsimd
        syn = block.sync

        def _(g):
            g.memset(ones_bf[:], 1.0)
            g.memset(ones_f32[:], 1.0)
            g.memset(eps_col[:], 1e-5)
            g.memset(rmk[:], 0.0)
        gps(_)

        def _(s):
            s.dma_start(out=xsA[:], in_=wq[0:128, :]).then_inc(sw0.h, 16)
            s.dma_start(out=xsB[:], in_=wq[128:256, :]).then_inc(sw0.h, 16)
            s.dma_start(out=masks_sb[:], in_=masks).then_inc(sw0.h, 16)
            s.dma_start(out=rmat_sb[:], in_=rmat).then_inc(sw0.h, 16)
        syn(_)
        sw0.v += 64

        def _(g):
            W.w("gp", g, sw0, 64)
            g.tensor_copy(wq_bf[:, 0:4096], xsA[:])
            g.tensor_copy(wq_bf[:, 4096:8192], xsB[:]).then_inc(sg.h, 1)
        gps(_)
        sg.v += 1
        tok_wqcast = sg.v

        def _(s):
            W.w("sy", s, sg, tok_wqcast)
            s.dma_start(out=xsA[:], in_=wq[256:384, :]).then_inc(sw0.h, 16)
            s.dma_start(out=xsB[:], in_=wq[384:512, :]).then_inc(sw0.h, 16)
        syn(_)
        sw0.v += 32

        def _(g):
            W.w("gp", g, sw0, sw0.v)
            g.tensor_copy(wq_bf[:, 2 * 4096:3 * 4096], xsA[:])
            g.tensor_copy(wq_bf[:, 3 * 4096:4 * 4096], xsB[:]).then_inc(sg.h, 1)
        gps(_)
        sg.v += 1
        tok_wqcast2 = sg.v

        def _(s):
            W.w("sy", s, sg, tok_wqcast2)
            s.dma_start(out=xsA[:], in_=wk).then_inc(sw0.h, 16)
            s.dma_start(out=xsB[:], in_=wv).then_inc(sw0.h, 16)
        syn(_)
        sw0.v += 32

        def _(g):
            W.w("gp", g, sw0, sw0.v)
            g.tensor_copy(wk_bf[:], xsA[:])
            g.tensor_copy(wv_bf[:], xsB[:]).then_inc(sg.h, 1)
        gps(_)
        sg.v += 1
        tok_wkv = sg.v

        def _(s):
            W.w("sy", s, sg, tok_wkv)
            s.dma_start(out=xsA[:], in_=xt[0:128, :]).then_inc(sxA.h, 16)
            s.dma_start(out=xsB[:], in_=xt[128:256, :]).then_inc(sxB.h, 16)
            for i in range(4):
                s.dma_start(out=tab_sb[0][:, i * TS:(i + 1) * TS],
                            in_=tabs[i * 128:(i + 1) * 128, 0:TS]).then_inc(stabA.h, 16)
        syn(_)
        sxA.v += 16
        sxB.v += 16
        stabA.v += 64

        def _(v):
            W.w("dv", v, sxA, 16)
            v.tensor_copy(xbf[:, 0:4096], xsA[:]).then_inc(sv.h, 1)
            W.w("dv", v, sxB, 16)
            v.tensor_copy(xbf[:, 4096:8192], xsB[:]).then_inc(sv.h, 1)
        dve(_)
        sv.v += 2
        c01 = [sv.v - 1, sv.v]

        def _(s):
            W.w("sy", s, sv, c01[0])
            s.dma_start(out=xsA[:], in_=xt[256:384, :]).then_inc(sxA.h, 16)
            W.w("sy", s, sv, c01[1])
            s.dma_start(out=xsB[:], in_=xt[384:512, :]).then_inc(sxB.h, 16)
        syn(_)
        sxA.v += 16
        sxB.v += 16

        def _(v):
            W.w("dv", v, sxA, 32)
            v.tensor_copy(xbf[:, 8192:12288], xsA[:]).then_inc(sv.h, 1)
            W.w("dv", v, sxB, 32)
            v.tensor_copy(xbf[:, 12288:16384], xsB[:]).then_inc(sv.h, 1)
        dve(_)
        sv.v += 2
        tok_cast_next = sv.v
        xcast23 = [sv.v - 1, sv.v]

        xcast_tok = [0, 0]
        tok_xbf_consumed = 0
        tok_q_consumed = {}
        sc_hist = {0: 0, 1: 0, 2: 0}
        avl_hist = {0: 0, 1: 0}
        tok_ot_dve = 0
        tok_rope_add = 0
        tok_statsrd = 0
        tok_vevac = 0
        tok_ropemulA = 0
        tok_ropemulB = 0
        tok_tabread = [0, 0]
        tok_aosto = {}
        tok_lastexp = 0
        tok_rmk_read = 0
        tok_negc_h = {0: 0, 1: 0, 2: 0, 3: 0}

        for slab in range(NSLAB):
            if slab == NSLAB - 1:
                def _(g):
                    for i in range(4):
                        W.w("gp", g, sao8[i], 16 * 8)
                    with allow_ici_all_to_all():
                        g.collective_compute(
                            "AllToAll", ALU.bypass,
                            replica_groups=[[0, 1, 2, 3, 4, 5, 6, 7]],
                            ins=[a2a_srcE.ap().opt()],
                            outs=[a2a_dstE.ap().opt()],
                        ).then_inc(scc.h, 1)
                gps(_)
                scc.v += 1
            b, ts = divmod(slab, NTS)
            t0 = ts * TS
            tpar = slab % 2
            tabsem = stabA if tpar == 0 else stabB
            tab = tab_sb[tpar]
            tab_tok = 64 * (slab // 2 + 1)

            tok_cast = tok_cast_next

            if ts == 0 and b > 0:
                def _(g):
                    W.w("gp", g, sv, max(tok_statsrd, tok_rmk_read))
                    g.memset(rmk[:], 0.0).then_inc(sg.h, 1)
                gps(_)
                sg.v += 1
                tok_rmk0 = sg.v
            else:
                tok_rmk0 = 0

            def _(t):
                W.w("pe", t, sv, tok_cast)
                W.w("pe", t, sa, tok_lastexp)
                W.w("pe", t, sv, tok_ropemulA)
                for dch in range(NDC):
                    mi = t.matmul(psA, wk_bf[:, dch * 128:(dch + 1) * 128],
                                  xbf[:, dch * TS:(dch + 1) * TS],
                                  start=(dch == 0), stop=(dch == NDC - 1))
                mi.then_inc(sm.h, 1)
            pe(_)
            sm.v += 1
            tokKproj = sm.v

            def _(v, tab=tab, tab_tok=tab_tok, tabsem=tabsem, tokKproj=tokKproj):
                W.w("dv", v, tabsem, tab_tok)
                W.w("dv", v, sm, tokKproj)
                v.tensor_mul(ta_sb[:], psA, tab[:, 2 * TS:3 * TS])
                v.tensor_mul(tb_sb[:], psA, tab[:, 3 * TS:4 * TS]).then_inc(sv.h, 1)
            dve(_)
            sv.v += 1
            tok_ropemulA = sv.v

            def _(t):
                W.w("pe", t, sv, tok_ropemulA)
                W.w("pe", t, sv, max(tok_ot_dve, tok_rope_add))
                t.matmul(psF[:], rmat_sb[:], tb_sb[:], start=True, stop=True).then_inc(sm.h, 1)
            pe(_)
            sm.v += 1
            tokKrope = sm.v

            def _(v, b=b, t0=t0, tokKrope=tokKrope):
                W.w("dv", v, sm, tokKrope)
                base = b * T + t0
                edge("dv", v, sv, v.tensor_add(kt_sb[:, base:base + TS], psF[:], ta_sb[:]))
                v.tensor_mul(sq_sb[:], kt_sb[:, base:base + TS], kt_sb[:, base:base + TS]).then_inc(sv.h, 1)
            dve(_)
            sv.v += 1
            tok_rope_add = sv.v
            tokKsq = sv.v

            def _(t, tokKsq=tokKsq):
                W.w("pe", t, sv, tokKsq)
                W.w("pe", t, sv, max(tok_statsrd, tok_ot_dve))
                t.matmul(psG[:], ones_bf[:], sq_sb[:], start=True, stop=True).then_inc(sm.h, 1)
            pe(_)
            sm.v += 1
            tokKnorm = sm.v

            def _(v, tok_rmk0=tok_rmk0, tokKnorm=tokKnorm):
                W.w("dv", v, sm, tokKnorm)
                if tok_rmk0:
                    W.w("dv", v, sg, tok_rmk0)
                v.reduce_max(tmp1[:], psG[:], axis=AX.X).then_inc(sv.h, 1)
            dve(_)
            sv.v += 1
            tok_kred = sv.v

            def _(v, tok_kred=tok_kred):
                W.w("dv", v, sv, tok_kred)
                v.tensor_max(rmk[:], rmk[:], tmp1[:]).then_inc(sv.h, 1)
            dve(_)
            sv.v += 1
            tok_statsrd = sv.v

            def _(t):
                W.w("pe", t, sv, max(tok_vevac, tok_ropemulB))
                for tc in range(4):
                    for dch in range(NDC):
                        mi = t.matmul(psSC[:, 512 + tc * 128:512 + (tc + 1) * 128],
                                      xbf[:, dch * TS + tc * 128:dch * TS + (tc + 1) * 128],
                                      wv_bf[:, dch * 128:(dch + 1) * 128],
                                      start=(dch == 0), stop=(dch == NDC - 1))
                mi.then_inc(sm.h, 1)
            pe(_)
            sm.v += 1
            tokVproj = sm.v

            def _(v, b=b, ts=ts, tokVproj=tokVproj):
                W.w("dv", v, sm, tokVproj)
                base = (b * NDC + 4 * ts) * 128
                v.tensor_copy(v_sb[:, base:base + 512], psB).then_inc(sv.h, 1)
            dve(_)
            sv.v += 1
            tok_vevac = sv.v

            tok_qrope_mm = tokKrope
            for hh in range(HPC):
                pp = psA if hh % 2 == 0 else psB

                def _(t, hh=hh, pp=pp):
                    if hh % 2 == 0:
                        W.w("pe", t, sv, tok_ropemulA)
                    else:
                        W.w("pe", t, sv, max(tok_vevac, tok_ropemulB))
                    for dch in range(NDC):
                        mi = t.matmul(pp, wq_bf[:, (hh * NDC + dch) * 128:(hh * NDC + dch + 1) * 128],
                                      xbf[:, dch * TS:(dch + 1) * TS],
                                      start=(dch == 0), stop=(dch == NDC - 1))
                    mi.then_inc(sm.h, 1)
                pe(_)
                sm.v += 1
                tokQproj = sm.v
                if hh == HPC - 1:
                    tok_xbf_consumed = sm.v

                def _(v, pp=pp, tokQproj=tokQproj, tok_qrope_mm=tok_qrope_mm, tab=tab):
                    W.w("dv", v, sm, tokQproj)
                    W.w("dv", v, sm, tok_qrope_mm)
                    v.tensor_mul(ta_sb[:], pp, tab[:, 0:TS])
                    v.tensor_mul(tb_sb[:], pp, tab[:, TS:2 * TS]).then_inc(sv.h, 1)
                dve(_)
                sv.v += 1
                if hh % 2 == 0:
                    tok_ropemulA = sv.v
                else:
                    tok_ropemulB = sv.v
                tok_qmul = sv.v

                def _(t, tok_qmul=tok_qmul):
                    W.w("pe", t, sv, tok_qmul)
                    W.w("pe", t, sv, tok_rope_add)
                    t.matmul(psF[:], rmat_sb[:], tb_sb[:], start=True, stop=True).then_inc(sm.h, 1)
                pe(_)
                sm.v += 1
                tok_qrope_mm = sm.v

                def _(v, hh=hh, tok_qrope_mm=tok_qrope_mm):
                    W.w("dv", v, sm, tok_qrope_mm)
                    W.w("dv", v, sm, tok_q_consumed.get(hh, 0))
                    edge("dv", v, sv, v.tensor_add(q_sb[:, hh * TS:(hh + 1) * TS], psF[:], ta_sb[:]))
                    v.tensor_mul(sq_sb[:], q_sb[:, hh * TS:(hh + 1) * TS], q_sb[:, hh * TS:(hh + 1) * TS]).then_inc(sv.h, 1)
                dve(_)
                sv.v += 1
                tok_rope_add = sv.v
                tokQsq = sv.v

                def _(t, tokQsq=tokQsq):
                    W.w("pe", t, sv, tokQsq)
                    W.w("pe", t, sv, max(tok_statsrd, tok_ot_dve))
                    t.matmul(psG[:], ones_bf[:], sq_sb[:], start=True, stop=True).then_inc(sm.h, 1)
                pe(_)
                sm.v += 1
                tokQnorm = sm.v

                def _(v, hh=hh, tokQnorm=tokQnorm):
                    W.w("dv", v, sm, tokQnorm)
                    v.reduce_max(qmax4[:, hh:hh + 1], psG[:], axis=AX.X).then_inc(sv.h, 1)
                dve(_)
                sv.v += 1
                tok_statsrd = sv.v

                def _(v, hh=hh):
                    W.w("dv", v, sv, tok_statsrd)
                    W.w("dv", v, sa, tok_negc_h[hh])
                    v.tensor_scalar(prod4[:, hh:hh + 1], qmax4[:, hh:hh + 1],
                                    rmk[:, 0:1], None, ALU.mult).then_inc(sv.h, 1)
                dve(_)
                sv.v += 1
                tok_prod_h = sv.v
                tok_rmk_read = sv.v

                def _(sc, hh=hh, tok_prod_h=tok_prod_h):
                    W.w("ac", sc, sv, tok_prod_h)
                    pc = prod4[:, hh:hh + 1]
                    edge("ac", sc, sa, sc.activation(pc, pc, AF.Ln))
                    edge("ac", sc, sa, sc.activation(pc, pc, AF.Exp, scale=0.5))
                    sc.mul(negc4[:, hh:hh + 1], pc, -1.01).then_inc(sa.h, 1)
                act(_)
                sa.v += 1
                tok_negc_h[hh] = sa.v
            tok_tabread[tpar] = tok_rope_add


            if "q0" in dbg and slab == 0:
                def _(v):
                    W.w("dv", v, sa, sa.v)
                    v.tensor_copy(stats_dbg[:, 0:4], qmax4[:])
                    v.tensor_copy(stats_dbg[:, 4:5], rmk[:])
                    v.tensor_copy(stats_dbg[:, 5:9], prod4[:])
                    v.tensor_copy(stats_dbg[:, 9:10], tmp1[:]).then_inc(sv.h, 1)
                dve(_)
                sv.v += 1

                def _(s):
                    W.w("sy", s, sv, sv.v)
                    W.w("sy", s, sa, sa.v)
                    s.dma_start(out=dbg["q0"], in_=q_sb[:]).then_inc(so.h, 16)
                    s.dma_start(out=dbg["negc0"], in_=negc4[:]).then_inc(so.h, 16)
                    s.dma_start(out=dbg["stats0"], in_=stats_dbg[:]).then_inc(so.h, 16)
                syn(_)
                so.v += 48

            if slab + 1 < NSLAB:
                npar = (slab + 1) % 2
                ntabsem = stabA if npar == 0 else stabB

                def _(s, slab=slab, npar=npar, ntabsem=ntabsem):
                    W.w("sy", s, sv, xcast23[0])
                    s.dma_start(out=xsA[:], in_=xt[(4 * slab + 4) * 128:(4 * slab + 5) * 128, :]).then_inc(sxA.h, 16)
                    W.w("sy", s, sv, xcast23[1])
                    s.dma_start(out=xsB[:], in_=xt[(4 * slab + 5) * 128:(4 * slab + 6) * 128, :]).then_inc(sxB.h, 16)
                    W.w("sy", s, sv, tok_tabread[npar])
                    nt0 = ((slab + 1) % NTS) * TS
                    for i in range(4):
                        s.dma_start(out=tab_sb[npar][:, i * TS:(i + 1) * TS],
                                    in_=tabs[i * 128:(i + 1) * 128, nt0:nt0 + TS]).then_inc(ntabsem.h, 16)
                syn(_)
                sxA.v += 16
                sxB.v += 16
                ntabsem.v += 64

                def _(v, slab=slab):
                    W.w("dv", v, sxA, 16 * (2 * slab + 3))
                    W.w("dv", v, sm, tok_xbf_consumed)
                    v.tensor_copy(xbf[:, 0:4096], xsA[:]).then_inc(sv.h, 1)
                    W.w("dv", v, sxB, 16 * (2 * slab + 3))
                    v.tensor_copy(xbf[:, 4096:8192], xsB[:]).then_inc(sv.h, 1)
                dve(_)
                sv.v += 2
                nc01 = [sv.v - 1, sv.v]

                def _(s, slab=slab, nc01=nc01):
                    W.w("sy", s, sv, nc01[0])
                    s.dma_start(out=xsA[:], in_=xt[(4 * slab + 6) * 128:(4 * slab + 7) * 128, :]).then_inc(sxA.h, 16)
                    W.w("sy", s, sv, nc01[1])
                    s.dma_start(out=xsB[:], in_=xt[(4 * slab + 7) * 128:(4 * slab + 8) * 128, :]).then_inc(sxB.h, 16)
                syn(_)
                sxA.v += 16
                sxB.v += 16

                def _(v, slab=slab):
                    W.w("dv", v, sxA, 16 * (2 * slab + 4))
                    v.tensor_copy(xbf[:, 8192:12288], xsA[:]).then_inc(sv.h, 1)
                    W.w("dv", v, sxB, 16 * (2 * slab + 4))
                    v.tensor_copy(xbf[:, 12288:16384], xsB[:]).then_inc(sv.h, 1)
                dve(_)
                sv.v += 2
                tok_cast_next = sv.v
                xcast23 = [sv.v - 1, sv.v]

            nsc = 4 * (ts + 1)
            npairs = nsc // 2

            sa_p, sm_p, sg_p, sv_p = sa.v, sm.v, sg.v, sv.v
            p_scores, p_exp, p_mask, p_avl, p_ot = {}, {}, {}, {}, {}
            for hh in range(HPC):
                sm_p += 1
                p_scores[(hh, 0)] = sm_p
                if npairs > 1:
                    sm_p += 1
                    p_scores[(hh, 1)] = sm_p
                for p in range(npairs):
                    sa_p += 1
                    p_exp[(hh, p)] = sa_p
                    if 2 * p >= nsc - 4:
                        sg_p += 1
                        p_mask[(hh, p)] = sg_p
                    if p + 2 < npairs:
                        sm_p += 1
                        p_scores[(hh, p + 2)] = sm_p
                    sm_p += 1
                    p_avl[(hh, p)] = sm_p
                sv_p += 2
                p_ot[hh] = sv_p

            def gbuf(hh, p):
                return (hh * npairs + p) % 3

            def scwar(hh, p):
                g = hh * npairs + p - 3
                if g >= 0:
                    return p_exp[(g // npairs, g % npairs)]
                return sc_hist[gbuf(hh, p)]

            def avlwar(hh, p):
                g = hh * npairs + p - 2
                if g >= 0:
                    return p_avl[(g // npairs, g % npairs)]
                return avl_hist[p % 2]

            def scslice(hh, p):
                k = gbuf(hh, p)
                return psSC[:, k * 1024:(k + 1) * 1024]

            def _(sc, npairs=npairs):
                for hh in range(HPC):
                    W.w("ac", sc, sa, tok_negc_h[hh])
                    for p in range(npairs):
                        atb = at_sb[p % 2]
                        W.w("ac", sc, sm, p_scores[(hh, p)])
                        W.w("ac", sc, sm, avlwar(hh, p))
                        sc.activation(atb[:], scslice(hh, p), AF.Exp,
                                      bias=negc4[:, hh:hh + 1], scale=1.0).then_inc(sa.h, 1)
                        sa.v += 1
            act(_)

            def _(t, b=b, npairs=npairs, nsc=nsc):
                first_use = set()

                def emit_scores(hh, p):
                    k = gbuf(hh, p)
                    W.w("pe", t, sa, scwar(hh, p))
                    if k in (0, 1) and k not in first_use:
                        W.w("pe", t, sv, max(tok_ropemulA, tok_ropemulB))
                    first_use.add(k)
                    sl = scslice(hh, p)
                    for half in range(2):
                        s0 = b * T + (2 * p + half) * 128
                        mi = t.matmul(sl[:, half * 512:(half + 1) * 512],
                                      kt_sb[:, s0:s0 + 128],
                                      q_sb[:, hh * TS:(hh + 1) * TS], start=True, stop=True)
                    mi.then_inc(sm.h, 1)
                    sm.v += 1
                    assert sm.v == p_scores[(hh, p)]

                for hh in range(HPC):
                    emit_scores(hh, 0)
                    if npairs > 1:
                        emit_scores(hh, 1)
                    for p in range(npairs):
                        atb = at_sb[p % 2]
                        diag = (2 * p >= nsc - 4)
                        if p + 2 < npairs:
                            emit_scores(hh, p + 2)
                        if diag:
                            W.w("pe", t, sg, p_mask[(hh, p)])
                        else:
                            W.w("pe", t, sa, p_exp[(hh, p)])
                        if p == 0:
                            W.w("pe", t, sv, p_ot[hh - 1] if hh > 0
                                else max(tok_ot_dve, tok_rope_add, tok_statsrd))
                        for half in range(2):
                            sc_i = 2 * p + half
                            vbase = (b * NDC + sc_i) * 128
                            last = (sc_i == nsc - 1)
                            t.matmul(psF[:], v_sb[:, vbase:vbase + 128],
                                     atb[:, half * 512:(half + 1) * 512],
                                     start=(sc_i == 0), stop=last)
                            mi = t.matmul(psG[:], ones_bf[:],
                                          atb[:, half * 512:(half + 1) * 512],
                                          start=(sc_i == 0), stop=last)
                        mi.then_inc(sm.h, 1)
                        sm.v += 1
                        assert sm.v == p_avl[(hh, p)]
            pe(_)

            def _(g, npairs=npairs, nsc=nsc):
                for hh in range(HPC):
                    for p in range(npairs):
                        if (hh, p) not in p_mask:
                            continue
                        atb = at_sb[p % 2]
                        W.w("gp", g, sa, p_exp[(hh, p)])
                        for half in range(2):
                            mii = 2 * p + half - (nsc - 4)
                            c0 = half * 512
                            if mii > 0:
                                g.memset(atb[:, c0:c0 + mii * 128], 0.0)
                            gi = g.tensor_mul(
                                atb[:, c0 + mii * 128:c0 + (mii + 1) * 128],
                                atb[:, c0 + mii * 128:c0 + (mii + 1) * 128],
                                masks_sb[:, mii * TS + mii * 128:mii * TS + (mii + 1) * 128])
                        gi.then_inc(sg.h, 1)
                        sg.v += 1
            gps(_)

            def _(v, npairs=npairs):
                for hh in range(HPC):
                    osem = sao8[(1 - ts % 2) * 4 + hh]
                    W.w("dv", v, sm, p_avl[(hh, npairs - 1)])
                    W.w("dv", v, osem, osem.v)
                    edge("dv", v, sv, v.reciprocal(linv_sb[:], psG[:]))
                    v.tensor_mul(ot_sb[hh][:], psF[:], linv_sb[:]).then_inc(sv.h, 1)
                    sv.v += 1
                    assert sv.v == p_ot[hh]
            dve(_)

            def _(s, b=b, ts=ts):
                buf = a2a_srcE if ts % 2 == 0 else a2a_srcO
                for hh in range(HPC):
                    osem = sao8[(ts % 2) * 4 + hh]
                    W.w("sy", s, sv, p_ot[hh])
                    blk = b * 4 + ts // 2
                    r0 = blk * 512 + hh * 128
                    s.dma_start(out=buf.ap()[r0:r0 + 128, :],
                                in_=ot_sb[hh][:]).then_inc(osem.h, 16)
                    osem.v += 16
            syn(_)

            assert (sa.v, sm.v, sg.v, sv.v) == (sa_p, sm_p, sg_p, sv_p), (slab, (sa.v, sm.v, sg.v, sv.v), (sa_p, sm_p, sg_p, sv_p))
            ntot = HPC * npairs
            for k in range(3):
                g = ntot - 1 - ((ntot - 1 - k) % 3)
                sc_hist[k] = p_exp[(g // npairs, g % npairs)]
            avl_hist[0] = p_avl[(HPC - 1, npairs - 2)]
            avl_hist[1] = p_avl[(HPC - 1, npairs - 1)]
            tok_lastexp = p_exp[(HPC - 1, npairs - 1)]
            tok_ot_dve = p_ot[HPC - 1]
            for hh in range(HPC):
                tok_q_consumed[hh] = p_avl[(hh, npairs - 1)]

        if "kt" in dbg or "v" in dbg or "q" in dbg:
            def _(sc):
                W.w("ac", sc, sv, sv.v)
                W.w("ac", sc, sa, sa.v - 0)
                if "kt" in dbg:
                    sc.dma_start(out=dbg["kt"], in_=kt_sb[:]).then_inc(so.h, 16); so.v += 16
                if "v" in dbg:
                    sc.dma_start(out=dbg["v"], in_=v_sb[:]).then_inc(so.h, 16); so.v += 16
                if "q" in dbg:
                    sc.dma_start(out=dbg["q"], in_=q_sb[:]).then_inc(so.h, 16); so.v += 16
            act(_)

        def _(g):
            for i in range(4):
                W.w("gp", g, sao8[4 + i], 16 * 8)
            with allow_ici_all_to_all():
                g.collective_compute(
                    "AllToAll", ALU.bypass,
                    replica_groups=[[0, 1, 2, 3, 4, 5, 6, 7]],
                    ins=[a2a_srcO.ap().opt()],
                    outs=[a2a_dstO.ap().opt()],
                ).then_inc(scc.h, 1)
        gps(_)
        scc.v += 1

        if "osrc" in dbg or "ogath" in dbg:
            def _(g):
                W.w("gp", g, scc, 2)
                if "osrc" in dbg:
                    g.dma_start(out=dbg["osrc"].rearrange("r (h c) -> r h c", h=2)[:, 0, :],
                                in_=a2a_srcE.ap()).then_inc(so.h, 16); so.v += 16
                    g.dma_start(out=dbg["osrc"].rearrange("r (h c) -> r h c", h=2)[:, 1, :],
                                in_=a2a_srcO.ap()).then_inc(so.h, 16); so.v += 16
                if "ogath" in dbg:
                    g.dma_start(out=dbg["ogath"].rearrange("r (h c) -> r h c", h=2)[:, 0, :],
                                in_=a2a_dstE.ap()).then_inc(so.h, 16); so.v += 16
                    g.dma_start(out=dbg["ogath"].rearrange("r (h c) -> r h c", h=2)[:, 1, :],
                                in_=a2a_dstO.ap()).then_inc(so.h, 16); so.v += 16
            gps(_)

        stk1.close()
        og_sb = nc.alloc_sbuf_tensor("og_sb", [128, NDC * TS], BF16)
        z_sb = nc.alloc_sbuf_tensor("z_sb", [128, NDC * TS], F32)
        yz_bf = nc.alloc_sbuf_tensor("yz_bf", [128, NDC * TS], BF16)
        wst = [nc.alloc_sbuf_tensor(f"wst{p}", [128, 4096], F32) for p in range(2)]
        wbf = [nc.alloc_sbuf_tensor(f"wbf{p}", [128, 4096], BF16) for p in range(4)]
        sqs = [nc.alloc_sbuf_tensor(f"sqs{p}", [128, TS], BF16) for p in range(2)]
        m_sb = nc.alloc_sbuf_tensor("m_sb", [128, TS], F32)
        m2_sb = nc.alloc_sbuf_tensor("m2_sb", [128, TS], F32)
        ve_sb = nc.alloc_sbuf_tensor("ve_sb", [128, TS], F32)
        rstd_sb = nc.alloc_sbuf_tensor("rstd_sb", [128, TS], F32)

        tok_wcast = {0: 0, 1: 0, 2: 0, 3: 0}
        tok_wmm = {0: 0, 1: 0, 2: 0, 3: 0}
        tok_wld = {0: 0, 1: 0}
        tok_wstfree = {0: 0, 1: 0}
        tok_zadd = {0: 0, 1: 0}
        tok_ybf = {}
        tok_sq = {0: 0, 1: 0}
        nonlocal_holder = [0]
        tok_store_d = {}
        tok_zout = 0
        tok_m_prev = 0
        tok_ve_prev = 0
        tok_lnapply_last = 0

        def emit_wload(jch, wsrc):
            p2 = jch % 2
            p4 = jch % 4
            wsem = swoA if p2 == 0 else swoB

            def _(s, jch=jch, p2=p2, wsrc=wsrc, wsem=wsem):
                W.w("sy", s, sv, tok_wstfree[p2])
                s.dma_start(out=wst[p2][:],
                            in_=wsrc[jch * 128:(jch + 1) * 128, :]).then_inc(wsem.h, 16)
            syn(_)
            wsem.v += 16
            ld_tok = wsem.v

            def _(v, p2=p2, p4=p4, ld_tok=ld_tok, wsem=wsem):
                W.w("dv", v, wsem, ld_tok)
                W.w("dv", v, sm, tok_wmm[p4])
                v.tensor_copy(wbf[p4][:], wst[p2][:]).then_inc(sv.h, 1)
            dve(_)
            sv.v += 1
            tok_wcast[p4] = sv.v
            tok_wstfree[p2] = sv.v

        def emit_stats(jch, phase):
            par = jch % 2

            def _(t, jch=jch, par=par, phase=phase):
                if jch == 0:
                    W.w("pe", t, sa, tok_m_prev)
                    W.w("pe", t, sv, tok_ve_prev)
                W.w("pe", t, sv, tok_zadd_d[jch])
                t.matmul(psSC[:, 1024:1536], ones_f32[:],
                         z_sb[:, jch * TS:(jch + 1) * TS],
                         start=(jch == 0), stop=(jch == NDC - 1))
                W.w("pe", t, sv, tok_sqw_d[jch])
                t.matmul(psSC[:, 1536:2048], ones_bf[:], sqs[par][:],
                         start=(jch == 0), stop=(jch == NDC - 1)).then_inc(sm.h, 1)
            pe(_)
            sm.v += 1
            tok_sq[par] = sm.v

        emit_wload(0, wo)
        emit_wload(1, wo)

        for u in range(2):
            def _(s, u=u):
                W.w("sy", s, scc, 1 + u)
                W.w("sy", s, sm, nonlocal_holder[0])
                dst = a2a_dstE if u == 0 else a2a_dstO
                for ech in range(NDC):
                    s.dma_start(out=og_sb[:, ech * TS:(ech + 1) * TS],
                                in_=dst.ap()[ech * 128:(ech + 1) * 128, :]
                                ).then_inc(sog.h, 16)
            syn(_)
            sog.v += 16 * NDC
            tok_ogld = sog.v

            if u == 0:
                def _(s, u=u):
                    s.dma_start(out=z_sb[:], in_=xres[u * 128:(u + 1) * 128, :]).then_inc(sxr.h, 16)
                syn(_)
                sxr.v += 16
                tok_xr_d = {jch: sxr.v for jch in range(NDC)}
            else:
                tok_xr_d = {}

                def _(s, u=u):
                    for jch in range(NDC):
                        W.w("sy", s, so, tok_store_d[jch])
                        s.dma_start(out=z_sb[:, jch * TS:(jch + 1) * TS],
                                    in_=xres[u * 128:(u + 1) * 128, jch * TS:(jch + 1) * TS]
                                    ).then_inc(sxr.h, 16)
                        sxr.v += 16
                        tok_xr_d[jch] = sxr.v
                syn(_)

            for phase in range(2):
                wsrc = wo if phase == 0 else wf
                tok_zadd_d = {}
                tok_sqw_d = {}
                for jch in range(NDC):
                    par = jch % 2
                    p4 = jch % 4
                    nxt = jch + 2
                    if nxt < NDC:
                        emit_wload(nxt, wsrc)
                    elif phase == 0:
                        emit_wload(nxt - NDC, wf)
                    elif u == 0:
                        emit_wload(nxt - NDC, wo)

                    def _(t, jch=jch, par=par, p4=p4, phase=phase, tok_ogld=tok_ogld):
                        W.w("pe", t, sv, tok_wcast[p4])
                        W.w("pe", t, sa, tok_lastexp)
                        if phase == 0:
                            W.w("pe", t, sog, tok_ogld)
                        elif jch > 0:
                            W.w("pe", t, sg, tok_ybf[NDC - 1])
                        W.w("pe", t, sv, tok_zadd[par])
                        pd = psSC[:, par * 512:(par + 1) * 512]
                        rhs = og_sb if phase == 0 else yz_bf
                        for ech in range(NDC):
                            if phase == 1 and jch == 0:
                                W.w("pe", t, sg, tok_ybf[ech])
                            mi = t.matmul(pd, wbf[p4][:, ech * 128:(ech + 1) * 128],
                                          rhs[:, ech * TS:(ech + 1) * TS],
                                          start=(ech == 0), stop=(ech == NDC - 1))
                        mi.then_inc(sm.h, 1)
                    pe(_)
                    sm.v += 1
                    tok_wmm[p4] = sm.v
                    if phase == 0:
                        nonlocal_holder[0] = max(nonlocal_holder[0], sm.v)
                    tok_proj = sm.v

                    def _(v, jch=jch, par=par, tok_proj=tok_proj):
                        W.w("dv", v, sm, tok_proj)
                        W.w("dv", v, sxr, tok_xr_d[jch])
                        pd = psSC[:, par * 512:(par + 1) * 512]
                        zc = z_sb[:, jch * TS:(jch + 1) * TS]
                        v.tensor_add(zc, pd, zc).then_inc(sv.h, 1)
                    dve(_)
                    sv.v += 1
                    tok_zadd[par] = sv.v
                    tok_zadd_d[jch] = sv.v

                    def _(v, jch=jch, par=par):
                        W.w("dv", v, sm, tok_sq[par])
                        v.tensor_mul(sqs[par][:], z_sb[:, jch * TS:(jch + 1) * TS],
                                     z_sb[:, jch * TS:(jch + 1) * TS]).then_inc(sv.h, 1)
                    dve(_)
                    sv.v += 1
                    tok_sqw_d[jch] = sv.v

                    if jch > 0:
                        emit_stats(jch - 1, phase)
                emit_stats(NDC - 1, phase)
                tok_stats = sm.v

                def _(sc, tok_stats=tok_stats):
                    W.w("ac", sc, sm, tok_stats)
                    W.w("ac", sc, sv, tok_lnapply_last)
                    sc.mul(m_sb[:], psSC[:, 1024:1536], 1.0 / D).then_inc(sa.h, 1)
                act(_)
                sa.v += 1
                tok_m = sa.v
                tok_m_prev = tok_m

                def _(v, tok_m=tok_m):
                    W.w("dv", v, sa, tok_m)
                    edge("dv", v, sv, v.tensor_mul(m2_sb[:], m_sb[:], m_sb[:]))
                    v.scalar_tensor_tensor(ve_sb[:], psSC[:, 1536:2048], 1.0 / D,
                                           m2_sb[:], ALU.mult, ALU.subtract).then_inc(sv.h, 1)
                dve(_)
                sv.v += 1
                tok_ve = sv.v
                tok_ve_prev = tok_ve

                def _(sc, tok_ve=tok_ve):
                    W.w("ac", sc, sv, tok_ve)
                    edge("ac", sc, sa, sc.activation(ve_sb[:], ve_sb[:], AF.Ln, bias=eps_col[:, 0:1]))
                    sc.activation(rstd_sb[:], ve_sb[:], AF.Exp, scale=-0.5).then_inc(sa.h, 1)
                act(_)
                sa.v += 1
                tok_rstd = sa.v

                tok_y_d = {}

                def _(v, tok_rstd=tok_rstd):
                    W.w("dv", v, sa, tok_rstd)
                    for jch in range(NDC):
                        zc = z_sb[:, jch * TS:(jch + 1) * TS]
                        edge("dv", v, sv, v.tensor_sub(zc, zc, m_sb[:]))
                        v.tensor_mul(zc, zc, rstd_sb[:]).then_inc(sv.h, 1)
                        sv.v += 1
                        tok_y_d[jch] = sv.v
                dve(_)
                tok_lnapply_last = sv.v

                if phase == 1:
                    def _(g, u=u):
                        for jch in range(NDC):
                            W.w("gp", g, sv, tok_y_d[jch])
                            r0 = (u * NDC + jch) * 128
                            g.dma_start(out=out[r0:r0 + 128, :],
                                        in_=z_sb[:, jch * TS:(jch + 1) * TS]).then_inc(so.h, 16)
                            so.v += 16
                            tok_store_d[jch] = so.v
                    gps(_)

                if phase == 0:
                    def _(g):
                        W.w("gp", g, sm, tok_stats)
                        for jch in range(NDC):
                            W.w("gp", g, sv, tok_y_d[jch])
                            g.tensor_copy(yz_bf[:, jch * TS:(jch + 1) * TS],
                                          z_sb[:, jch * TS:(jch + 1) * TS]).then_inc(sg.h, 1)
                            sg.v += 1
                            tok_ybf[jch] = sg.v
                    gps(_)
                else:
                    tok_zout = so.v

            if "z" in dbg and u == 0:
                def _(sc):
                    W.w("ac", sc, sv, sv.v)
                    sc.dma_start(out=dbg["z"], in_=z_sb[:]).then_inc(so.h, 16); so.v += 16
                act(_)

        final_so = so.v

        def _(sc):
            W.w("ac", sc, so, final_so)
        act(_)

    return nc



def _stage_shared(X, Wq, Wk, Wv, Wo, Wf):
    sh = {}
    xt = np.empty((NSLAB, 4, 128, 8, TS), np.float32)
    for b in range(B):
        Xb = np.asarray(X[b])
        for ts in range(NTS):
            sl = Xb[ts * TS:(ts + 1) * TS, :]
            blk = sl.T.reshape(NDC, 128, TS)
            for qq in range(4):
                xt[b * NTS + ts, qq] = blk[8 * qq:8 * qq + 8].transpose(1, 0, 2)
    sh["xt"] = np.ascontiguousarray(xt.reshape(NSLAB * 4 * 128, 8 * TS))

    def wblock(Wm):
        out = np.empty((NDC, 128, NDC, 128), np.float32)
        for jch in range(NDC):
            blk = np.asarray(Wm[:, jch * 128:(jch + 1) * 128]).reshape(NDC, 128, 128)
            out[jch] = blk.transpose(1, 0, 2)
        return np.ascontiguousarray(out.reshape(NDC * 128, NDC * 128))

    sh["wo"] = wblock(Wo)
    sh["wf"] = wblock(Wf)

    pos = np.arange(T, dtype=np.float64)
    inv = 10000.0 ** (-np.arange(64, dtype=np.float64) / 64.0)
    ang = np.outer(pos, inv)
    c64 = np.cos(ang).T.astype(np.float32)
    s64 = np.sin(ang).T.astype(np.float32)
    ct = np.concatenate([c64, c64], 0)
    st = np.concatenate([s64, -s64], 0)
    scl = np.float32(1.0 / np.sqrt(DH))
    sh["tabs"] = np.ascontiguousarray(np.concatenate([ct * scl, st * scl, ct, st], 0)).astype(ml_dtypes.bfloat16)

    mk = np.zeros((128, 4 * TS), np.float32)
    pp = np.arange(128)[:, None]
    ff = np.arange(TS)[None, :]
    for i in range(4):
        mk[:, i * TS:(i + 1) * TS] = (pp + 128 * i <= ff)
    sh["masks"] = mk.astype(ml_dtypes.bfloat16)

    R = np.zeros((128, 128), np.float32)
    R[np.arange(128), (np.arange(128) + 64) % 128] = 1.0
    sh["rmat"] = R.astype(ml_dtypes.bfloat16)
    return sh


def _stage_core(c, X, Wq, Wk, Wv):
    m = {}
    wqc = np.empty((HPC, 128, NDC, 128), np.float32)
    for hh in range(HPC):
        col0 = 512 * c + 128 * hh
        wqc[hh] = np.asarray(Wq[:, col0:col0 + 128]).reshape(NDC, 128, 128).transpose(1, 0, 2)
    m["wq"] = np.ascontiguousarray(wqc.reshape(HPC * 128, NDC * 128))
    m["wk"] = np.ascontiguousarray(
        np.asarray(Wk[:, 128 * c:128 * (c + 1)]).reshape(NDC, 128, 128).transpose(1, 0, 2).reshape(128, NDC * 128))
    m["wv"] = np.ascontiguousarray(
        np.asarray(Wv[:, 128 * c:128 * (c + 1)]).reshape(NDC, 128, 128).transpose(1, 0, 2).reshape(128, NDC * 128))
    b_o, q_o = c // 4, c % 4
    xr = np.empty((2, 128, NDC, TS), np.float32)
    Xb = np.asarray(X[b_o])
    for u in range(2):
        t0 = 1024 * q_o + 512 * u
        sl = Xb[t0:t0 + TS, :]
        xr[u] = sl.T.reshape(NDC, 128, TS).transpose(1, 0, 2)
    m["xres"] = np.ascontiguousarray(xr.reshape(2 * 128, NDC * TS))
    return m


def kernel(L0_attn_X, L0_attn_Wq, L0_attn_Wk, L0_attn_Wv, L0_attn_Wo, L0_ffn_weight):
    X = np.asarray(L0_attn_X, np.float32)
    sh = _stage_shared(X, L0_attn_Wq, L0_attn_Wk, L0_attn_Wv, L0_attn_Wo, L0_ffn_weight)
    in_maps = []
    for c in range(8):
        m = dict(sh)
        m.update(_stage_core(c, X, L0_attn_Wq, L0_attn_Wk, L0_attn_Wv))
        in_maps.append(m)

    nc = build_program(debug=_DEBUG)
    res = run_bass_kernel_spmd(nc, in_maps, core_ids=list(range(8)))

    outp = np.empty((B, T, D), np.float32)
    for c in range(8):
        b_o, q_o = c // 4, c % 4
        arr = np.asarray(res.results[c]["out"]).reshape(2, NDC, 128, TS)
        yt = arr.transpose(1, 2, 0, 3).reshape(D, 1024)
        outp[b_o, 1024 * q_o:1024 * (q_o + 1), :] = yt.T
    return outp


# version 19
# speedup vs baseline: 1.1296x; 1.0039x over previous
import contextlib
import numpy as np
import ml_dtypes
import concourse.bass as bass
import concourse.mybir as mybir
from concourse.bass_utils import run_bass_kernel_spmd
from concourse.aot_env import allow_ici_all_to_all

F32 = mybir.dt.float32
BF16 = mybir.dt.bfloat16
AF = mybir.ActivationFunctionType
ALU = mybir.AluOpType
AX = mybir.AxisListType

B, T, D, DH = 2, 4096, 4096, 128
NTS = 8
TS = 512
NDC = D // 128
HPC = 4
NSLAB = B * NTS

_DEBUG = ()


class _Sem:
    __slots__ = ("h", "v", "name")

    def __init__(self, nc, name):
        self.h = nc.alloc_semaphore(name)
        self.v = 0
        self.name = name


class _Waits:

    def __init__(self):
        self.seen = {}

    def w(self, ename, eng, sem, tok):
        if tok <= 0:
            return
        k = (ename, sem.name)
        if self.seen.get(k, 0) >= tok:
            return
        eng.wait_ge(sem.h, tok)
        self.seen[k] = tok


def build_program(debug=()):
    nc = bass.Bass("TRN2", target_bir_lowering=False, debug=False)

    xt = nc.dram_tensor("xt", [NSLAB * 4 * 128, 8 * TS], F32, kind="ExternalInput").ap()
    wq = nc.dram_tensor("wq", [HPC * 128, NDC * 128], F32, kind="ExternalInput").ap()
    wk = nc.dram_tensor("wk", [128, NDC * 128], F32, kind="ExternalInput").ap()
    wv = nc.dram_tensor("wv", [128, NDC * 128], F32, kind="ExternalInput").ap()
    wo = nc.dram_tensor("wo", [NDC * 128, NDC * 128], F32, kind="ExternalInput").ap()
    wf = nc.dram_tensor("wf", [NDC * 128, NDC * 128], F32, kind="ExternalInput").ap()
    xres = nc.dram_tensor("xres", [2 * 128, NDC * TS], F32, kind="ExternalInput").ap()
    tabs = nc.dram_tensor("tabs", [4 * 128, T], BF16, kind="ExternalInput").ap()
    masks = nc.dram_tensor("masks", [128, 4 * TS], BF16, kind="ExternalInput").ap()
    rmat = nc.dram_tensor("rmat", [128, 128], BF16, kind="ExternalInput").ap()
    out = nc.dram_tensor("out", [2 * NDC * 128, TS], F32, kind="ExternalOutput").ap()
    a2a_srcE = nc.dram_tensor("a2a_srcE", [8 * 512, 512], BF16)
    a2a_srcO = nc.dram_tensor("a2a_srcO", [8 * 512, 512], BF16)
    a2a_dstE = nc.dram_tensor("a2a_dstE", [8 * 512, 512], BF16)
    a2a_dstO = nc.dram_tensor("a2a_dstO", [8 * 512, 512], BF16)

    dbg = {}
    if "kt" in debug:
        dbg["kt"] = nc.dram_tensor("dbg_kt", [128, B * T], BF16, kind="ExternalOutput").ap()
    if "v" in debug:
        dbg["v"] = nc.dram_tensor("dbg_v", [128, B * T], BF16, kind="ExternalOutput").ap()
    if "q" in debug:
        dbg["q"] = nc.dram_tensor("dbg_q", [128, HPC * TS], BF16, kind="ExternalOutput").ap()
    if "q0" in debug:
        dbg["q0"] = nc.dram_tensor("dbg_q0", [128, HPC * TS], BF16, kind="ExternalOutput").ap()
        dbg["negc0"] = nc.dram_tensor("dbg_negc0", [128, 4], F32, kind="ExternalOutput").ap()
        dbg["stats0"] = nc.dram_tensor("dbg_stats0", [128, 16], F32, kind="ExternalOutput").ap()
    if "osrc" in debug:
        dbg["osrc"] = nc.dram_tensor("dbg_osrc", [8 * 512, 1024], BF16, kind="ExternalOutput").ap()
    if "ogath" in debug:
        dbg["ogath"] = nc.dram_tensor("dbg_ogath", [8 * 512, 1024], BF16, kind="ExternalOutput").ap()
    if "z" in debug:
        dbg["z"] = nc.dram_tensor("dbg_z", [128, NDC * TS], F32, kind="ExternalOutput").ap()

    psSC = nc.alloc_psum_tensor("psSC", [128, 3072], F32)
    psF = nc.alloc_psum_tensor("psF", [128, 512], F32)
    psG = nc.alloc_psum_tensor("psG", [128, 512], F32)
    psA = psSC[:, 0:512]
    psB = psSC[:, 512:1024]

    S = lambda n: _Sem(nc, n)
    sxA, sxB = S("sxA"), S("sxB")
    stabA, stabB = S("stabA"), S("stabB")
    sw0 = S("sw0")
    sog = S("sog")
    sxr = S("sxr")
    swoA, swoB = S("swoA"), S("swoB")
    sao8 = [S(f"sao{i}") for i in range(8)]
    so = S("so")
    scc = S("scc")
    sg, sv, sa, sm = S("sg"), S("sv"), S("sa"), S("sm")

    W = _Waits()

    def edge(ename, eng, sem, inst):
        inst.then_inc(sem.h, 1)
        sem.v += 1
        eng.wait_ge(sem.h, sem.v)
        W.seen[(ename, sem.name)] = sem.v

    ones_bf = nc.alloc_sbuf_tensor("ones_bf", [128, 128], BF16)
    ones_f32 = nc.alloc_sbuf_tensor("ones_f32", [128, 128], F32)
    eps_col = nc.alloc_sbuf_tensor("eps_col", [128, 1], F32)

    stk1 = contextlib.ExitStack()
    sb = lambda n, s, d: stk1.enter_context(nc.sbuf_tensor(n, s, d))
    wq_bf = sb("wq_bf", [128, HPC * NDC * 128], BF16)
    wk_bf = sb("wk_bf", [128, NDC * 128], BF16)
    wv_bf = sb("wv_bf", [128, NDC * 128], BF16)
    kt_sb = sb("kt_sb", [128, B * T], BF16)
    v_sb = sb("v_sb", [128, B * T], BF16)
    xsA = sb("xsA", [128, 8 * TS], F32)
    xsB = sb("xsB", [128, 8 * TS], F32)
    xbf = sb("xbf", [128, NDC * TS], BF16)
    q_sb = sb("q_sb", [128, HPC * TS], BF16)
    ta_sb = sb("ta_sb", [128, TS], BF16)
    tb_sb = sb("tb_sb", [128, TS], BF16)
    sq_sb = sb("sq_sb", [128, TS], BF16)
    tab_sb = [sb(f"tab{p}", [128, 4 * TS], BF16) for p in range(2)]
    at_sb = [sb(f"at{p}", [128, 1024], BF16) for p in range(2)]
    ot_sb = [sb(f"ot{p}", [128, TS], BF16) for p in range(4)]
    linv_sb = sb("linv_sb", [128, TS], F32)
    masks_sb = sb("masks_sb", [128, 4 * TS], BF16)
    rmat_sb = sb("rmat_sb", [128, 128], BF16)
    qmax4 = sb("qmax4", [128, 4], F32)
    rmk = sb("rmk", [128, 1], F32)
    tmp1 = sb("tmp1", [128, 1], F32)
    prod4 = sb("prod4", [128, 4], F32)
    negc4 = sb("negc4", [128, 4], F32)
    stats_dbg = sb("stats_dbg", [128, 16], F32)

    with nc.Block() as block:
        pe = block.tensor
        act = block.scalar
        dve = block.vector
        gps = block.gpsimd
        syn = block.sync

        def _(g):
            g.memset(ones_bf[:], 1.0)
            g.memset(ones_f32[:], 1.0)
            g.memset(eps_col[:], 1e-5)
            g.memset(rmk[:], 0.0)
        gps(_)

        def _(s):
            s.dma_start(out=xsA[:], in_=wq[0:128, :]).then_inc(sw0.h, 16)
            s.dma_start(out=xsB[:], in_=wq[128:256, :]).then_inc(sw0.h, 16)
            s.dma_start(out=masks_sb[:], in_=masks).then_inc(sw0.h, 16)
            s.dma_start(out=rmat_sb[:], in_=rmat).then_inc(sw0.h, 16)
        syn(_)
        sw0.v += 64

        def _(g):
            W.w("gp", g, sw0, 64)
            g.tensor_copy(wq_bf[:, 0:4096], xsA[:])
            g.tensor_copy(wq_bf[:, 4096:8192], xsB[:]).then_inc(sg.h, 1)
        gps(_)
        sg.v += 1
        tok_wqcast = sg.v

        def _(s):
            W.w("sy", s, sg, tok_wqcast)
            s.dma_start(out=xsA[:], in_=wq[256:384, :]).then_inc(sw0.h, 16)
            s.dma_start(out=xsB[:], in_=wq[384:512, :]).then_inc(sw0.h, 16)
        syn(_)
        sw0.v += 32

        def _(g):
            W.w("gp", g, sw0, sw0.v)
            g.tensor_copy(wq_bf[:, 2 * 4096:3 * 4096], xsA[:])
            g.tensor_copy(wq_bf[:, 3 * 4096:4 * 4096], xsB[:]).then_inc(sg.h, 1)
        gps(_)
        sg.v += 1
        tok_wqcast2 = sg.v

        def _(s):
            W.w("sy", s, sg, tok_wqcast2)
            s.dma_start(out=xsA[:], in_=wk).then_inc(sw0.h, 16)
            s.dma_start(out=xsB[:], in_=wv).then_inc(sw0.h, 16)
        syn(_)
        sw0.v += 32

        def _(g):
            W.w("gp", g, sw0, sw0.v)
            g.tensor_copy(wk_bf[:], xsA[:])
            g.tensor_copy(wv_bf[:], xsB[:]).then_inc(sg.h, 1)
        gps(_)
        sg.v += 1
        tok_wkv = sg.v

        def _(s):
            W.w("sy", s, sg, tok_wkv)
            s.dma_start(out=xsA[:], in_=xt[0:128, :]).then_inc(sxA.h, 16)
            s.dma_start(out=xsB[:], in_=xt[128:256, :]).then_inc(sxB.h, 16)
            for i in range(4):
                s.dma_start(out=tab_sb[0][:, i * TS:(i + 1) * TS],
                            in_=tabs[i * 128:(i + 1) * 128, 0:TS]).then_inc(stabA.h, 16)
        syn(_)
        sxA.v += 16
        sxB.v += 16
        stabA.v += 64

        def _(v):
            W.w("dv", v, sxA, 16)
            v.tensor_copy(xbf[:, 0:4096], xsA[:]).then_inc(sv.h, 1)
            W.w("dv", v, sxB, 16)
            v.tensor_copy(xbf[:, 4096:8192], xsB[:]).then_inc(sv.h, 1)
        dve(_)
        sv.v += 2
        c01 = [sv.v - 1, sv.v]

        def _(s):
            W.w("sy", s, sv, c01[0])
            s.dma_start(out=xsA[:], in_=xt[256:384, :]).then_inc(sxA.h, 16)
            W.w("sy", s, sv, c01[1])
            s.dma_start(out=xsB[:], in_=xt[384:512, :]).then_inc(sxB.h, 16)
        syn(_)
        sxA.v += 16
        sxB.v += 16

        def _(v):
            W.w("dv", v, sxA, 32)
            v.tensor_copy(xbf[:, 8192:12288], xsA[:]).then_inc(sv.h, 1)
            W.w("dv", v, sxB, 32)
            v.tensor_copy(xbf[:, 12288:16384], xsB[:]).then_inc(sv.h, 1)
        dve(_)
        sv.v += 2
        tok_cast_next = sv.v
        xcast23 = [sv.v - 1, sv.v]

        xcast_tok = [0, 0]
        tok_xbf_consumed = 0
        tok_q_consumed = {}
        sc_hist = {0: 0, 1: 0, 2: 0}
        avl_hist = {0: 0, 1: 0}
        tok_ot_dve = 0
        tok_rope_add = 0
        tok_statsrd = 0
        tok_vevac = 0
        tok_ropemulA = 0
        tok_ropemulB = 0
        tok_tabread = [0, 0]
        tok_aosto = {}
        tok_lastexp = 0
        tok_rmk_read = 0
        tok_negc_h = {0: 0, 1: 0, 2: 0, 3: 0}

        for slab in range(NSLAB):
            if slab == NSLAB - 1:
                def _(g):
                    for i in range(4):
                        W.w("gp", g, sao8[i], 16 * 8)
                    with allow_ici_all_to_all():
                        g.collective_compute(
                            "AllToAll", ALU.bypass,
                            replica_groups=[[0, 1, 2, 3, 4, 5, 6, 7]],
                            ins=[a2a_srcE.ap().opt()],
                            outs=[a2a_dstE.ap().opt()],
                        ).then_inc(scc.h, 1)
                gps(_)
                scc.v += 1
            b, ts = divmod(slab, NTS)
            t0 = ts * TS
            tpar = slab % 2
            tabsem = stabA if tpar == 0 else stabB
            tab = tab_sb[tpar]
            tab_tok = 64 * (slab // 2 + 1)

            tok_cast = tok_cast_next

            if ts == 0 and b > 0:
                def _(g):
                    W.w("gp", g, sv, max(tok_statsrd, tok_rmk_read))
                    g.memset(rmk[:], 0.0).then_inc(sg.h, 1)
                gps(_)
                sg.v += 1
                tok_rmk0 = sg.v
            else:
                tok_rmk0 = 0

            def _(t):
                W.w("pe", t, sv, tok_cast)
                W.w("pe", t, sa, tok_lastexp)
                W.w("pe", t, sv, tok_ropemulA)
                for dch in range(NDC):
                    mi = t.matmul(psA, wk_bf[:, dch * 128:(dch + 1) * 128],
                                  xbf[:, dch * TS:(dch + 1) * TS],
                                  start=(dch == 0), stop=(dch == NDC - 1))
                mi.then_inc(sm.h, 1)
            pe(_)
            sm.v += 1
            tokKproj = sm.v

            def _(v, tab=tab, tab_tok=tab_tok, tabsem=tabsem, tokKproj=tokKproj):
                W.w("dv", v, tabsem, tab_tok)
                W.w("dv", v, sm, tokKproj)
                v.tensor_mul(ta_sb[:], psA, tab[:, 2 * TS:3 * TS])
                v.tensor_mul(tb_sb[:], psA, tab[:, 3 * TS:4 * TS]).then_inc(sv.h, 1)
            dve(_)
            sv.v += 1
            tok_ropemulA = sv.v

            def _(t):
                W.w("pe", t, sv, tok_ropemulA)
                W.w("pe", t, sv, max(tok_ot_dve, tok_rope_add))
                t.matmul(psF[:], rmat_sb[:], tb_sb[:], start=True, stop=True).then_inc(sm.h, 1)
            pe(_)
            sm.v += 1
            tokKrope = sm.v

            def _(v, b=b, t0=t0, tokKrope=tokKrope):
                W.w("dv", v, sm, tokKrope)
                base = b * T + t0
                edge("dv", v, sv, v.tensor_add(kt_sb[:, base:base + TS], psF[:], ta_sb[:]))
                v.tensor_mul(sq_sb[:], kt_sb[:, base:base + TS], kt_sb[:, base:base + TS]).then_inc(sv.h, 1)
            dve(_)
            sv.v += 1
            tok_rope_add = sv.v
            tokKsq = sv.v

            def _(t, tokKsq=tokKsq):
                W.w("pe", t, sv, tokKsq)
                W.w("pe", t, sv, max(tok_statsrd, tok_ot_dve))
                t.matmul(psG[:], ones_bf[:], sq_sb[:], start=True, stop=True).then_inc(sm.h, 1)
            pe(_)
            sm.v += 1
            tokKnorm = sm.v

            def _(v, tok_rmk0=tok_rmk0, tokKnorm=tokKnorm):
                W.w("dv", v, sm, tokKnorm)
                if tok_rmk0:
                    W.w("dv", v, sg, tok_rmk0)
                v.reduce_max(tmp1[:], psG[:], axis=AX.X).then_inc(sv.h, 1)
            dve(_)
            sv.v += 1
            tok_kred = sv.v

            def _(v, tok_kred=tok_kred):
                W.w("dv", v, sv, tok_kred)
                v.tensor_max(rmk[:], rmk[:], tmp1[:]).then_inc(sv.h, 1)
            dve(_)
            sv.v += 1
            tok_statsrd = sv.v

            def _(t):
                W.w("pe", t, sv, max(tok_vevac, tok_ropemulB))
                for tc in range(4):
                    for dch in range(NDC):
                        mi = t.matmul(psSC[:, 512 + tc * 128:512 + (tc + 1) * 128],
                                      xbf[:, dch * TS + tc * 128:dch * TS + (tc + 1) * 128],
                                      wv_bf[:, dch * 128:(dch + 1) * 128],
                                      start=(dch == 0), stop=(dch == NDC - 1))
                mi.then_inc(sm.h, 1)
            pe(_)
            sm.v += 1
            tokVproj = sm.v

            def _(v, b=b, ts=ts, tokVproj=tokVproj):
                W.w("dv", v, sm, tokVproj)
                base = (b * NDC + 4 * ts) * 128
                v.tensor_copy(v_sb[:, base:base + 512], psB).then_inc(sv.h, 1)
            dve(_)
            sv.v += 1
            tok_vevac = sv.v

            tok_qrope_mm = tokKrope
            for hh in range(HPC):
                pp = psA if hh % 2 == 0 else psB

                def _(t, hh=hh, pp=pp):
                    if hh % 2 == 0:
                        W.w("pe", t, sv, tok_ropemulA)
                    else:
                        W.w("pe", t, sv, max(tok_vevac, tok_ropemulB))
                    for dch in range(NDC):
                        mi = t.matmul(pp, wq_bf[:, (hh * NDC + dch) * 128:(hh * NDC + dch + 1) * 128],
                                      xbf[:, dch * TS:(dch + 1) * TS],
                                      start=(dch == 0), stop=(dch == NDC - 1))
                    mi.then_inc(sm.h, 1)
                pe(_)
                sm.v += 1
                tokQproj = sm.v
                if hh == HPC - 1:
                    tok_xbf_consumed = sm.v

                def _(v, pp=pp, tokQproj=tokQproj, tok_qrope_mm=tok_qrope_mm, tab=tab):
                    W.w("dv", v, sm, tokQproj)
                    W.w("dv", v, sm, tok_qrope_mm)
                    v.tensor_mul(ta_sb[:], pp, tab[:, 0:TS])
                    v.tensor_mul(tb_sb[:], pp, tab[:, TS:2 * TS]).then_inc(sv.h, 1)
                dve(_)
                sv.v += 1
                if hh % 2 == 0:
                    tok_ropemulA = sv.v
                else:
                    tok_ropemulB = sv.v
                tok_qmul = sv.v

                def _(t, tok_qmul=tok_qmul):
                    W.w("pe", t, sv, tok_qmul)
                    W.w("pe", t, sv, tok_rope_add)
                    t.matmul(psF[:], rmat_sb[:], tb_sb[:], start=True, stop=True).then_inc(sm.h, 1)
                pe(_)
                sm.v += 1
                tok_qrope_mm = sm.v

                def _(v, hh=hh, tok_qrope_mm=tok_qrope_mm):
                    W.w("dv", v, sm, tok_qrope_mm)
                    W.w("dv", v, sm, tok_q_consumed.get(hh, 0))
                    edge("dv", v, sv, v.tensor_add(q_sb[:, hh * TS:(hh + 1) * TS], psF[:], ta_sb[:]))
                    v.tensor_mul(sq_sb[:], q_sb[:, hh * TS:(hh + 1) * TS], q_sb[:, hh * TS:(hh + 1) * TS]).then_inc(sv.h, 1)
                dve(_)
                sv.v += 1
                tok_rope_add = sv.v
                tokQsq = sv.v

                def _(t, tokQsq=tokQsq):
                    W.w("pe", t, sv, tokQsq)
                    W.w("pe", t, sv, max(tok_statsrd, tok_ot_dve))
                    t.matmul(psG[:], ones_bf[:], sq_sb[:], start=True, stop=True).then_inc(sm.h, 1)
                pe(_)
                sm.v += 1
                tokQnorm = sm.v

                def _(v, hh=hh, tokQnorm=tokQnorm):
                    W.w("dv", v, sm, tokQnorm)
                    v.reduce_max(qmax4[:, hh:hh + 1], psG[:], axis=AX.X).then_inc(sv.h, 1)
                dve(_)
                sv.v += 1
                tok_statsrd = sv.v

                def _(v, hh=hh):
                    W.w("dv", v, sv, tok_statsrd)
                    W.w("dv", v, sa, tok_negc_h[hh])
                    v.tensor_scalar(prod4[:, hh:hh + 1], qmax4[:, hh:hh + 1],
                                    rmk[:, 0:1], None, ALU.mult).then_inc(sv.h, 1)
                dve(_)
                sv.v += 1
                tok_prod_h = sv.v
                tok_rmk_read = sv.v

                def _(sc, hh=hh, tok_prod_h=tok_prod_h):
                    W.w("ac", sc, sv, tok_prod_h)
                    pc = prod4[:, hh:hh + 1]
                    edge("ac", sc, sa, sc.activation(pc, pc, AF.Ln))
                    edge("ac", sc, sa, sc.activation(pc, pc, AF.Exp, scale=0.5))
                    sc.mul(negc4[:, hh:hh + 1], pc, -1.01).then_inc(sa.h, 1)
                act(_)
                sa.v += 1
                tok_negc_h[hh] = sa.v
            tok_tabread[tpar] = tok_rope_add


            if "q0" in dbg and slab == 0:
                def _(v):
                    W.w("dv", v, sa, sa.v)
                    v.tensor_copy(stats_dbg[:, 0:4], qmax4[:])
                    v.tensor_copy(stats_dbg[:, 4:5], rmk[:])
                    v.tensor_copy(stats_dbg[:, 5:9], prod4[:])
                    v.tensor_copy(stats_dbg[:, 9:10], tmp1[:]).then_inc(sv.h, 1)
                dve(_)
                sv.v += 1

                def _(s):
                    W.w("sy", s, sv, sv.v)
                    W.w("sy", s, sa, sa.v)
                    s.dma_start(out=dbg["q0"], in_=q_sb[:]).then_inc(so.h, 16)
                    s.dma_start(out=dbg["negc0"], in_=negc4[:]).then_inc(so.h, 16)
                    s.dma_start(out=dbg["stats0"], in_=stats_dbg[:]).then_inc(so.h, 16)
                syn(_)
                so.v += 48

            if slab + 1 < NSLAB:
                npar = (slab + 1) % 2
                ntabsem = stabA if npar == 0 else stabB

                def _(s, slab=slab, npar=npar, ntabsem=ntabsem):
                    W.w("sy", s, sv, xcast23[0])
                    s.dma_start(out=xsA[:], in_=xt[(4 * slab + 4) * 128:(4 * slab + 5) * 128, :]).then_inc(sxA.h, 16)
                    W.w("sy", s, sv, xcast23[1])
                    s.dma_start(out=xsB[:], in_=xt[(4 * slab + 5) * 128:(4 * slab + 6) * 128, :]).then_inc(sxB.h, 16)
                    W.w("sy", s, sv, tok_tabread[npar])
                    nt0 = ((slab + 1) % NTS) * TS
                    for i in range(4):
                        s.dma_start(out=tab_sb[npar][:, i * TS:(i + 1) * TS],
                                    in_=tabs[i * 128:(i + 1) * 128, nt0:nt0 + TS]).then_inc(ntabsem.h, 16)
                syn(_)
                sxA.v += 16
                sxB.v += 16
                ntabsem.v += 64

                def _(v, slab=slab):
                    W.w("dv", v, sxA, 16 * (2 * slab + 3))
                    W.w("dv", v, sm, tok_xbf_consumed)
                    v.tensor_copy(xbf[:, 0:4096], xsA[:]).then_inc(sv.h, 1)
                    W.w("dv", v, sxB, 16 * (2 * slab + 3))
                    v.tensor_copy(xbf[:, 4096:8192], xsB[:]).then_inc(sv.h, 1)
                dve(_)
                sv.v += 2
                nc01 = [sv.v - 1, sv.v]

                def _(s, slab=slab, nc01=nc01):
                    W.w("sy", s, sv, nc01[0])
                    s.dma_start(out=xsA[:], in_=xt[(4 * slab + 6) * 128:(4 * slab + 7) * 128, :]).then_inc(sxA.h, 16)
                    W.w("sy", s, sv, nc01[1])
                    s.dma_start(out=xsB[:], in_=xt[(4 * slab + 7) * 128:(4 * slab + 8) * 128, :]).then_inc(sxB.h, 16)
                syn(_)
                sxA.v += 16
                sxB.v += 16

                def _(v, slab=slab):
                    W.w("dv", v, sxA, 16 * (2 * slab + 4))
                    v.tensor_copy(xbf[:, 8192:12288], xsA[:]).then_inc(sv.h, 1)
                    W.w("dv", v, sxB, 16 * (2 * slab + 4))
                    v.tensor_copy(xbf[:, 12288:16384], xsB[:]).then_inc(sv.h, 1)
                dve(_)
                sv.v += 2
                tok_cast_next = sv.v
                xcast23 = [sv.v - 1, sv.v]

            nsc = 4 * (ts + 1)
            npairs = nsc // 2

            sa_p, sm_p, sg_p, sv_p = sa.v, sm.v, sg.v, sv.v
            p_scores, p_exp, p_mask, p_avl, p_ot = {}, {}, {}, {}, {}
            for hh in range(HPC):
                sm_p += 1
                p_scores[(hh, 0)] = sm_p
                if npairs > 1:
                    sm_p += 1
                    p_scores[(hh, 1)] = sm_p
                for p in range(npairs):
                    sa_p += 1
                    p_exp[(hh, p)] = sa_p
                    if 2 * p >= nsc - 4:
                        sg_p += 1
                        p_mask[(hh, p)] = sg_p
                    if p + 2 < npairs:
                        sm_p += 1
                        p_scores[(hh, p + 2)] = sm_p
                    sm_p += 1
                    p_avl[(hh, p)] = sm_p
                sv_p += 2
                p_ot[hh] = sv_p

            def gbuf(hh, p):
                return (hh * npairs + p) % 3

            def scwar(hh, p):
                g = hh * npairs + p - 3
                if g >= 0:
                    return p_exp[(g // npairs, g % npairs)]
                return sc_hist[gbuf(hh, p)]

            def avlwar(hh, p):
                g = hh * npairs + p - 2
                if g >= 0:
                    return p_avl[(g // npairs, g % npairs)]
                return avl_hist[p % 2]

            def scslice(hh, p):
                k = gbuf(hh, p)
                return psSC[:, k * 1024:(k + 1) * 1024]

            def _(sc, npairs=npairs):
                for hh in range(HPC):
                    W.w("ac", sc, sa, tok_negc_h[hh])
                    for p in range(npairs):
                        atb = at_sb[p % 2]
                        W.w("ac", sc, sm, p_scores[(hh, p)])
                        W.w("ac", sc, sm, avlwar(hh, p))
                        sc.activation(atb[:], scslice(hh, p), AF.Exp,
                                      bias=negc4[:, hh:hh + 1], scale=1.0).then_inc(sa.h, 1)
                        sa.v += 1
            act(_)

            def _(t, b=b, npairs=npairs, nsc=nsc):
                first_use = set()

                def emit_scores(hh, p):
                    k = gbuf(hh, p)
                    W.w("pe", t, sa, scwar(hh, p))
                    if k in (0, 1) and k not in first_use:
                        W.w("pe", t, sv, max(tok_ropemulA, tok_ropemulB))
                    first_use.add(k)
                    sl = scslice(hh, p)
                    for half in range(2):
                        s0 = b * T + (2 * p + half) * 128
                        mi = t.matmul(sl[:, half * 512:(half + 1) * 512],
                                      kt_sb[:, s0:s0 + 128],
                                      q_sb[:, hh * TS:(hh + 1) * TS], start=True, stop=True)
                    mi.then_inc(sm.h, 1)
                    sm.v += 1
                    assert sm.v == p_scores[(hh, p)]

                for hh in range(HPC):
                    emit_scores(hh, 0)
                    if npairs > 1:
                        emit_scores(hh, 1)
                    for p in range(npairs):
                        atb = at_sb[p % 2]
                        diag = (2 * p >= nsc - 4)
                        if p + 2 < npairs:
                            emit_scores(hh, p + 2)
                        if diag:
                            W.w("pe", t, sg, p_mask[(hh, p)])
                        else:
                            W.w("pe", t, sa, p_exp[(hh, p)])
                        if p == 0:
                            W.w("pe", t, sv, p_ot[hh - 1] if hh > 0
                                else max(tok_ot_dve, tok_rope_add, tok_statsrd))
                        for half in range(2):
                            sc_i = 2 * p + half
                            vbase = (b * NDC + sc_i) * 128
                            last = (sc_i == nsc - 1)
                            t.matmul(psF[:], v_sb[:, vbase:vbase + 128],
                                     atb[:, half * 512:(half + 1) * 512],
                                     start=(sc_i == 0), stop=last)
                            mi = t.matmul(psG[:], ones_bf[:],
                                          atb[:, half * 512:(half + 1) * 512],
                                          start=(sc_i == 0), stop=last)
                        mi.then_inc(sm.h, 1)
                        sm.v += 1
                        assert sm.v == p_avl[(hh, p)]
            pe(_)

            def _(g, npairs=npairs, nsc=nsc):
                for hh in range(HPC):
                    for p in range(npairs):
                        if (hh, p) not in p_mask:
                            continue
                        atb = at_sb[p % 2]
                        W.w("gp", g, sa, p_exp[(hh, p)])
                        for half in range(2):
                            mii = 2 * p + half - (nsc - 4)
                            c0 = half * 512
                            if mii > 0:
                                g.memset(atb[:, c0:c0 + mii * 128], 0.0)
                            gi = g.tensor_mul(
                                atb[:, c0 + mii * 128:c0 + (mii + 1) * 128],
                                atb[:, c0 + mii * 128:c0 + (mii + 1) * 128],
                                masks_sb[:, mii * TS + mii * 128:mii * TS + (mii + 1) * 128])
                        gi.then_inc(sg.h, 1)
                        sg.v += 1
            gps(_)

            def _(v, npairs=npairs):
                for hh in range(HPC):
                    osem = sao8[(1 - ts % 2) * 4 + hh]
                    W.w("dv", v, sm, p_avl[(hh, npairs - 1)])
                    W.w("dv", v, osem, osem.v)
                    edge("dv", v, sv, v.reciprocal(linv_sb[:], psG[:]))
                    v.tensor_mul(ot_sb[hh][:], psF[:], linv_sb[:]).then_inc(sv.h, 1)
                    sv.v += 1
                    assert sv.v == p_ot[hh]
            dve(_)

            def _(s, b=b, ts=ts):
                buf = a2a_srcE if ts % 2 == 0 else a2a_srcO
                for hh in range(HPC):
                    osem = sao8[(ts % 2) * 4 + hh]
                    W.w("sy", s, sv, p_ot[hh])
                    blk = b * 4 + ts // 2
                    r0 = blk * 512 + hh * 128
                    s.dma_start(out=buf.ap()[r0:r0 + 128, :],
                                in_=ot_sb[hh][:]).then_inc(osem.h, 16)
                    osem.v += 16
            syn(_)

            assert (sa.v, sm.v, sg.v, sv.v) == (sa_p, sm_p, sg_p, sv_p), (slab, (sa.v, sm.v, sg.v, sv.v), (sa_p, sm_p, sg_p, sv_p))
            ntot = HPC * npairs
            for k in range(3):
                g = ntot - 1 - ((ntot - 1 - k) % 3)
                sc_hist[k] = p_exp[(g // npairs, g % npairs)]
            avl_hist[0] = p_avl[(HPC - 1, npairs - 2)]
            avl_hist[1] = p_avl[(HPC - 1, npairs - 1)]
            tok_lastexp = p_exp[(HPC - 1, npairs - 1)]
            tok_ot_dve = p_ot[HPC - 1]
            for hh in range(HPC):
                tok_q_consumed[hh] = p_avl[(hh, npairs - 1)]

        if "kt" in dbg or "v" in dbg or "q" in dbg:
            def _(sc):
                W.w("ac", sc, sv, sv.v)
                W.w("ac", sc, sa, sa.v - 0)
                if "kt" in dbg:
                    sc.dma_start(out=dbg["kt"], in_=kt_sb[:]).then_inc(so.h, 16); so.v += 16
                if "v" in dbg:
                    sc.dma_start(out=dbg["v"], in_=v_sb[:]).then_inc(so.h, 16); so.v += 16
                if "q" in dbg:
                    sc.dma_start(out=dbg["q"], in_=q_sb[:]).then_inc(so.h, 16); so.v += 16
            act(_)

        def _(g):
            for i in range(4):
                W.w("gp", g, sao8[4 + i], 16 * 8)
            with allow_ici_all_to_all():
                g.collective_compute(
                    "AllToAll", ALU.bypass,
                    replica_groups=[[0, 1, 2, 3, 4, 5, 6, 7]],
                    ins=[a2a_srcO.ap().opt()],
                    outs=[a2a_dstO.ap().opt()],
                ).then_inc(scc.h, 1)
        gps(_)
        scc.v += 1

        if "osrc" in dbg or "ogath" in dbg:
            def _(g):
                W.w("gp", g, scc, 2)
                if "osrc" in dbg:
                    g.dma_start(out=dbg["osrc"].rearrange("r (h c) -> r h c", h=2)[:, 0, :],
                                in_=a2a_srcE.ap()).then_inc(so.h, 16); so.v += 16
                    g.dma_start(out=dbg["osrc"].rearrange("r (h c) -> r h c", h=2)[:, 1, :],
                                in_=a2a_srcO.ap()).then_inc(so.h, 16); so.v += 16
                if "ogath" in dbg:
                    g.dma_start(out=dbg["ogath"].rearrange("r (h c) -> r h c", h=2)[:, 0, :],
                                in_=a2a_dstE.ap()).then_inc(so.h, 16); so.v += 16
                    g.dma_start(out=dbg["ogath"].rearrange("r (h c) -> r h c", h=2)[:, 1, :],
                                in_=a2a_dstO.ap()).then_inc(so.h, 16); so.v += 16
            gps(_)

        stk1.close()
        og_sb = nc.alloc_sbuf_tensor("og_sb", [128, NDC * TS], BF16)
        z_sb = nc.alloc_sbuf_tensor("z_sb", [128, NDC * TS], F32)
        yz_bf = nc.alloc_sbuf_tensor("yz_bf", [128, NDC * TS], BF16)
        wst = [nc.alloc_sbuf_tensor(f"wst{p}", [128, 4096], F32) for p in range(2)]
        wbf = [nc.alloc_sbuf_tensor(f"wbf{p}", [128, 4096], BF16) for p in range(4)]
        sqs = [nc.alloc_sbuf_tensor(f"sqs{p}", [128, TS], BF16) for p in range(4)]
        m_sb = nc.alloc_sbuf_tensor("m_sb", [128, TS], F32)
        m2_sb = nc.alloc_sbuf_tensor("m2_sb", [128, TS], F32)
        ve_sb = nc.alloc_sbuf_tensor("ve_sb", [128, TS], F32)
        rstd_sb = nc.alloc_sbuf_tensor("rstd_sb", [128, TS], F32)

        tok_wcast = {0: 0, 1: 0, 2: 0, 3: 0}
        tok_wmm = {0: 0, 1: 0, 2: 0, 3: 0}
        tok_wld = {0: 0, 1: 0}
        tok_wstfree = {0: 0, 1: 0}
        tok_zadd = {0: 0, 1: 0}
        tok_ybf = {}
        tok_sq = {0: 0, 1: 0, 2: 0, 3: 0}
        nonlocal_holder = [0]
        tok_store_d = {}
        tok_zout = 0
        tok_m_prev = 0
        tok_ve_prev = 0
        tok_lnapply_last = 0

        def emit_wload(jch, wsrc):
            p2 = jch % 2
            p4 = jch % 4
            wsem = swoA if p2 == 0 else swoB

            def _(s, jch=jch, p2=p2, wsrc=wsrc, wsem=wsem):
                W.w("sy", s, sv, tok_wstfree[p2])
                s.dma_start(out=wst[p2][:],
                            in_=wsrc[jch * 128:(jch + 1) * 128, :]).then_inc(wsem.h, 16)
            syn(_)
            wsem.v += 16
            ld_tok = wsem.v

            def _(v, p2=p2, p4=p4, ld_tok=ld_tok, wsem=wsem):
                W.w("dv", v, wsem, ld_tok)
                W.w("dv", v, sm, tok_wmm[p4])
                v.tensor_copy(wbf[p4][:], wst[p2][:]).then_inc(sv.h, 1)
            dve(_)
            sv.v += 1
            tok_wcast[p4] = sv.v
            tok_wstfree[p2] = sv.v

        def emit_stats(jch, phase):
            par = jch % 4

            def _(t, jch=jch, par=par, phase=phase):
                if jch == 0:
                    W.w("pe", t, sa, tok_m_prev)
                    W.w("pe", t, sv, tok_ve_prev)
                W.w("pe", t, sv, tok_zadd_d[jch])
                t.matmul(psSC[:, 1024:1536], ones_f32[:],
                         z_sb[:, jch * TS:(jch + 1) * TS],
                         start=(jch == 0), stop=(jch == NDC - 1))
                W.w("pe", t, sv, tok_sqw_d[jch])
                t.matmul(psSC[:, 1536:2048], ones_bf[:], sqs[par][:],
                         start=(jch == 0), stop=(jch == NDC - 1)).then_inc(sm.h, 1)
            pe(_)
            sm.v += 1
            tok_sq[par] = sm.v

        emit_wload(0, wo)
        emit_wload(1, wo)

        for u in range(2):
            def _(s, u=u):
                W.w("sy", s, scc, 1 + u)
                W.w("sy", s, sm, nonlocal_holder[0])
                dst = a2a_dstE if u == 0 else a2a_dstO
                for ech in range(NDC):
                    s.dma_start(out=og_sb[:, ech * TS:(ech + 1) * TS],
                                in_=dst.ap()[ech * 128:(ech + 1) * 128, :]
                                ).then_inc(sog.h, 16)
            syn(_)
            sog.v += 16 * NDC
            tok_ogld = sog.v

            if u == 0:
                def _(s, u=u):
                    s.dma_start(out=z_sb[:], in_=xres[u * 128:(u + 1) * 128, :]).then_inc(sxr.h, 16)
                syn(_)
                sxr.v += 16
                tok_xr_d = {jch: sxr.v for jch in range(NDC)}
            else:
                tok_xr_d = {}

                def _(s, u=u):
                    for jch in range(NDC):
                        W.w("sy", s, so, tok_store_d[jch])
                        s.dma_start(out=z_sb[:, jch * TS:(jch + 1) * TS],
                                    in_=xres[u * 128:(u + 1) * 128, jch * TS:(jch + 1) * TS]
                                    ).then_inc(sxr.h, 16)
                        sxr.v += 16
                        tok_xr_d[jch] = sxr.v
                syn(_)

            for phase in range(2):
                wsrc = wo if phase == 0 else wf
                tok_zadd_d = {}
                tok_sqw_d = {}
                for jch in range(NDC):
                    par = jch % 2
                    p4 = jch % 4
                    nxt = jch + 2
                    if nxt < NDC:
                        emit_wload(nxt, wsrc)
                    elif phase == 0:
                        emit_wload(nxt - NDC, wf)
                    elif u == 0:
                        emit_wload(nxt - NDC, wo)

                    def _(t, jch=jch, par=par, p4=p4, phase=phase, tok_ogld=tok_ogld):
                        W.w("pe", t, sv, tok_wcast[p4])
                        W.w("pe", t, sa, tok_lastexp)
                        if phase == 0:
                            W.w("pe", t, sog, tok_ogld)
                        elif jch > 0:
                            W.w("pe", t, sg, tok_ybf[NDC - 1])
                        W.w("pe", t, sv, tok_zadd[par])
                        pd = psSC[:, par * 512:(par + 1) * 512]
                        rhs = og_sb if phase == 0 else yz_bf
                        for ech in range(NDC):
                            if phase == 1 and jch == 0:
                                W.w("pe", t, sg, tok_ybf[ech])
                            mi = t.matmul(pd, wbf[p4][:, ech * 128:(ech + 1) * 128],
                                          rhs[:, ech * TS:(ech + 1) * TS],
                                          start=(ech == 0), stop=(ech == NDC - 1))
                        mi.then_inc(sm.h, 1)
                    pe(_)
                    sm.v += 1
                    tok_wmm[p4] = sm.v
                    if phase == 0:
                        nonlocal_holder[0] = max(nonlocal_holder[0], sm.v)
                    tok_proj = sm.v

                    def _(v, jch=jch, par=par, tok_proj=tok_proj):
                        W.w("dv", v, sm, tok_proj)
                        W.w("dv", v, sxr, tok_xr_d[jch])
                        pd = psSC[:, par * 512:(par + 1) * 512]
                        zc = z_sb[:, jch * TS:(jch + 1) * TS]
                        v.tensor_add(zc, pd, zc).then_inc(sv.h, 1)
                    dve(_)
                    sv.v += 1
                    tok_zadd[par] = sv.v
                    tok_zadd_d[jch] = sv.v

                    def _(v, jch=jch):
                        W.w("dv", v, sm, tok_sq[jch % 4])
                        v.tensor_mul(sqs[jch % 4][:], z_sb[:, jch * TS:(jch + 1) * TS],
                                     z_sb[:, jch * TS:(jch + 1) * TS]).then_inc(sv.h, 1)
                    dve(_)
                    sv.v += 1
                    tok_sqw_d[jch] = sv.v

                    if jch > 1:
                        emit_stats(jch - 2, phase)
                emit_stats(NDC - 2, phase)
                emit_stats(NDC - 1, phase)
                tok_stats = sm.v

                def _(sc, tok_stats=tok_stats):
                    W.w("ac", sc, sm, tok_stats)
                    W.w("ac", sc, sv, tok_lnapply_last)
                    sc.mul(m_sb[:], psSC[:, 1024:1536], 1.0 / D).then_inc(sa.h, 1)
                act(_)
                sa.v += 1
                tok_m = sa.v
                tok_m_prev = tok_m

                def _(v, tok_m=tok_m):
                    W.w("dv", v, sa, tok_m)
                    edge("dv", v, sv, v.tensor_mul(m2_sb[:], m_sb[:], m_sb[:]))
                    v.scalar_tensor_tensor(ve_sb[:], psSC[:, 1536:2048], 1.0 / D,
                                           m2_sb[:], ALU.mult, ALU.subtract).then_inc(sv.h, 1)
                dve(_)
                sv.v += 1
                tok_ve = sv.v
                tok_ve_prev = tok_ve

                def _(sc, tok_ve=tok_ve):
                    W.w("ac", sc, sv, tok_ve)
                    edge("ac", sc, sa, sc.activation(ve_sb[:], ve_sb[:], AF.Ln, bias=eps_col[:, 0:1]))
                    sc.activation(rstd_sb[:], ve_sb[:], AF.Exp, scale=-0.5).then_inc(sa.h, 1)
                act(_)
                sa.v += 1
                tok_rstd = sa.v

                tok_y_d = {}

                def _(v, tok_rstd=tok_rstd):
                    W.w("dv", v, sa, tok_rstd)
                    for jch in range(NDC):
                        zc = z_sb[:, jch * TS:(jch + 1) * TS]
                        edge("dv", v, sv, v.tensor_sub(zc, zc, m_sb[:]))
                        v.tensor_mul(zc, zc, rstd_sb[:]).then_inc(sv.h, 1)
                        sv.v += 1
                        tok_y_d[jch] = sv.v
                dve(_)
                tok_lnapply_last = sv.v

                if phase == 1:
                    def _(g, u=u):
                        for jch in range(NDC):
                            W.w("gp", g, sv, tok_y_d[jch])
                            r0 = (u * NDC + jch) * 128
                            g.dma_start(out=out[r0:r0 + 128, :],
                                        in_=z_sb[:, jch * TS:(jch + 1) * TS]).then_inc(so.h, 16)
                            so.v += 16
                            tok_store_d[jch] = so.v
                    gps(_)

                if phase == 0:
                    def _(g):
                        W.w("gp", g, sm, tok_stats)
                        for jch in range(NDC):
                            W.w("gp", g, sv, tok_y_d[jch])
                            g.tensor_copy(yz_bf[:, jch * TS:(jch + 1) * TS],
                                          z_sb[:, jch * TS:(jch + 1) * TS]).then_inc(sg.h, 1)
                            sg.v += 1
                            tok_ybf[jch] = sg.v
                    gps(_)
                else:
                    tok_zout = so.v

            if "z" in dbg and u == 0:
                def _(sc):
                    W.w("ac", sc, sv, sv.v)
                    sc.dma_start(out=dbg["z"], in_=z_sb[:]).then_inc(so.h, 16); so.v += 16
                act(_)

        final_so = so.v

        def _(sc):
            W.w("ac", sc, so, final_so)
        act(_)

    return nc



def _stage_shared(X, Wq, Wk, Wv, Wo, Wf):
    sh = {}
    xt = np.empty((NSLAB, 4, 128, 8, TS), np.float32)
    for b in range(B):
        Xb = np.asarray(X[b])
        for ts in range(NTS):
            sl = Xb[ts * TS:(ts + 1) * TS, :]
            blk = sl.T.reshape(NDC, 128, TS)
            for qq in range(4):
                xt[b * NTS + ts, qq] = blk[8 * qq:8 * qq + 8].transpose(1, 0, 2)
    sh["xt"] = np.ascontiguousarray(xt.reshape(NSLAB * 4 * 128, 8 * TS))

    def wblock(Wm):
        out = np.empty((NDC, 128, NDC, 128), np.float32)
        for jch in range(NDC):
            blk = np.asarray(Wm[:, jch * 128:(jch + 1) * 128]).reshape(NDC, 128, 128)
            out[jch] = blk.transpose(1, 0, 2)
        return np.ascontiguousarray(out.reshape(NDC * 128, NDC * 128))

    sh["wo"] = wblock(Wo)
    sh["wf"] = wblock(Wf)

    pos = np.arange(T, dtype=np.float64)
    inv = 10000.0 ** (-np.arange(64, dtype=np.float64) / 64.0)
    ang = np.outer(pos, inv)
    c64 = np.cos(ang).T.astype(np.float32)
    s64 = np.sin(ang).T.astype(np.float32)
    ct = np.concatenate([c64, c64], 0)
    st = np.concatenate([s64, -s64], 0)
    scl = np.float32(1.0 / np.sqrt(DH))
    sh["tabs"] = np.ascontiguousarray(np.concatenate([ct * scl, st * scl, ct, st], 0)).astype(ml_dtypes.bfloat16)

    mk = np.zeros((128, 4 * TS), np.float32)
    pp = np.arange(128)[:, None]
    ff = np.arange(TS)[None, :]
    for i in range(4):
        mk[:, i * TS:(i + 1) * TS] = (pp + 128 * i <= ff)
    sh["masks"] = mk.astype(ml_dtypes.bfloat16)

    R = np.zeros((128, 128), np.float32)
    R[np.arange(128), (np.arange(128) + 64) % 128] = 1.0
    sh["rmat"] = R.astype(ml_dtypes.bfloat16)
    return sh


def _stage_core(c, X, Wq, Wk, Wv):
    m = {}
    wqc = np.empty((HPC, 128, NDC, 128), np.float32)
    for hh in range(HPC):
        col0 = 512 * c + 128 * hh
        wqc[hh] = np.asarray(Wq[:, col0:col0 + 128]).reshape(NDC, 128, 128).transpose(1, 0, 2)
    m["wq"] = np.ascontiguousarray(wqc.reshape(HPC * 128, NDC * 128))
    m["wk"] = np.ascontiguousarray(
        np.asarray(Wk[:, 128 * c:128 * (c + 1)]).reshape(NDC, 128, 128).transpose(1, 0, 2).reshape(128, NDC * 128))
    m["wv"] = np.ascontiguousarray(
        np.asarray(Wv[:, 128 * c:128 * (c + 1)]).reshape(NDC, 128, 128).transpose(1, 0, 2).reshape(128, NDC * 128))
    b_o, q_o = c // 4, c % 4
    xr = np.empty((2, 128, NDC, TS), np.float32)
    Xb = np.asarray(X[b_o])
    for u in range(2):
        t0 = 1024 * q_o + 512 * u
        sl = Xb[t0:t0 + TS, :]
        xr[u] = sl.T.reshape(NDC, 128, TS).transpose(1, 0, 2)
    m["xres"] = np.ascontiguousarray(xr.reshape(2 * 128, NDC * TS))
    return m


def kernel(L0_attn_X, L0_attn_Wq, L0_attn_Wk, L0_attn_Wv, L0_attn_Wo, L0_ffn_weight):
    X = np.asarray(L0_attn_X, np.float32)
    sh = _stage_shared(X, L0_attn_Wq, L0_attn_Wk, L0_attn_Wv, L0_attn_Wo, L0_ffn_weight)
    in_maps = []
    for c in range(8):
        m = dict(sh)
        m.update(_stage_core(c, X, L0_attn_Wq, L0_attn_Wk, L0_attn_Wv))
        in_maps.append(m)

    nc = build_program(debug=_DEBUG)
    res = run_bass_kernel_spmd(nc, in_maps, core_ids=list(range(8)))

    outp = np.empty((B, T, D), np.float32)
    for c in range(8):
        b_o, q_o = c // 4, c % 4
        arr = np.asarray(res.results[c]["out"]).reshape(2, NDC, 128, TS)
        yt = arr.transpose(1, 2, 0, 3).reshape(D, 1024)
        outp[b_o, 1024 * q_o:1024 * (q_o + 1), :] = yt.T
    return outp
